# version 06
# baseline (speedup 1.0000x reference)
from contextlib import ExitStack

import numpy as np
import ml_dtypes

import concourse.bass as bass
import concourse.mybir as mybir
from concourse.bass_utils import run_bass_kernel_spmd

F32 = mybir.dt.float32
BF16 = mybir.dt.bfloat16
FP8 = mybir.dt.float8e4
AF = mybir.ActivationFunctionType
ALU = mybir.AluOpType

H, P, N, G, KK, E = 32, 128, 128, 8, 4, 2048
I = H * P
CD = I + 2 * G * N
S = 4096
BATCH = 4
EPS_GN = 1e-6
EPS_LN = 1e-5
NEG = -1e30

T1 = 512
T2 = 512
T3 = 512
NT1 = S // T1
NT2 = S // T2
NT3 = S // T3
NCH = T2 // 128
NOC = 41

DBG = {"phases": 99, "dump": (), "no_cc": False, "sweep_tiles": NT2 if False else None}

ENGS = ("tensor", "vector", "scalar", "gpsimd", "sync")
SEMS = ("dK", "dG", "dW", "dA0", "dA1", "dX", "dO", "dP", "sT", "sV", "sS", "sG", "cc")


def _heads(lr):
    return [8 * a + j + 4 * lr for a in range(4) for j in range(4)]


def _lh_of_part(p):
    return (p % 4) * 4 + p // 4


class Tok:
    __slots__ = ("sem", "val")

    def __init__(self, sem, val):
        self.sem, self.val = sem, val


class Ctx:
    pass


def build_program():
    nc = bass.Bass("TRN2", target_bir_lowering=False)
    c = Ctx()
    c.nc = nc
    dram = {}

    def din(name, shape, dt):
        dram[name] = nc.dram_tensor(name, shape, dt, kind="ExternalInput").ap()
        return dram[name]

    def dint(name, shape, dt):
        dram[name] = nc.dram_tensor(name, shape, dt, kind="Internal").ap()
        return dram[name]

    din("xh", [128, 16 * S], F32)
    din("winh", [NOC, 16, 128, 128], F32)
    din("wouth", [32, 128, 8, 128], F32)
    din("wf1h", [128, 16 * 1024], FP8)
    din("wf2h", [128, 16 * 1024], FP8)
    din("convw", [128, 24 * 4], F32)
    din("convbh", [128, 24], F32)
    din("dtb_h", [16, 1], F32)
    din("alog_h", [16, 1], F32)
    din("d_h", [1, 16], F32)
    din("nrmw_h", [128, 32], F32)
    din("isc_h", [1, 1], BF16)
    din("wsc_h", [1, 1], F32)
    din("identh", [128, 128], BF16)
    din("identf_h", [128, 128], F32)
    din("tuh", [128, 128], F32)
    din("maskh", [128, 128], F32)
    din("ones_r_f", [1, 128], F32)
    din("ones_r_b", [1, 128], BF16)
    din("ones_c_b", [128, 1], BF16)
    dram["outh"] = nc.dram_tensor("outh", [128, 8 * S], F32, kind="ExternalOutput").ap()

    dint("win_bf", [NOC, 128, 16, 128], BF16)
    dint("wout_bf", [128, 8, 32, 128], BF16)
    dint("xnorm_d", [128, 16 * S], BF16)
    for k in range(2):
        dint(f"cc1i{k}", [128, 8 * (S // 2)], BF16)
        dint(f"cc1o{k}", [2, 128, 8 * (S // 2)], BF16)
        dint(f"cc2i{k}", [128, 16 * (S // 2)], BF16)
        dint(f"cc2o{k}", [2, 128, 16 * (S // 2)], BF16)
        dint(f"cc3i{k}", [128, 8 * (S // 2)], BF16)
        dint(f"cc3o{k}", [2, 128, 8 * (S // 2)], BF16)
    c.dram = dram

    dumps = {}
    for nm in DBG["dump"]:
        shp, dt = {"x1me": ([128, 8 * S], BF16), "xn2": ([128, 16 * S], BF16),
                   "convout": ([128, 24 * (4 + T2)], BF16), "dtsp": ([16, T2], F32),
                   "h32n": ([128, 16 * S], BF16), "sg": ([128, 16 * T2], BF16),
                   "x2me": ([128, 8 * S], BF16), "xq1": ([128, 16 * T1], FP8),
                   "srun": ([128, 16 * 128], F32), "convpre": ([128, 24 * (4 + T2)], BF16),
                   "spc": ([128, 48], F32), "sc4": ([128, 512], BF16), "sd4": ([128, 512], BF16),
                   "c4": ([128, 512], BF16), "b4": ([128, 512], BF16), "xts": ([128, 512], BF16),
                   "bts": ([128, 128], BF16), "l4": ([128, 512], BF16), "d4": ([128, 512], F32),
                   "h32c": ([128, 2048], BF16), "t1b": ([128, 512], F32), "sbf": ([128, 2048], BF16),
                   "tmb": ([128, 64], F32), "cumafm": ([16, T2], F32)}[nm]
        dumps[nm] = nc.dram_tensor("dump_" + nm, shp, dt, kind="ExternalOutput").ap()
    c.dumps = dumps

    counts = {s: 0 for s in SEMS}
    streams = {k: [] for k in ENGS}
    c.counts = counts

    def op(eng, emit, waits=(), inc=None, dma=False):
        tok = None
        if inc is not None:
            counts[inc] += (16 if dma else 1)
            tok = Tok(inc, counts[inc])
        streams[eng].append(([w for w in waits if w is not None], emit, inc, 16 if dma else 1))
        return tok

    c.op = op

    def barrier():
        toks = [Tok(s, counts[s]) for s in SEMS]
        for eng in ENGS:
            op(eng, None, waits=toks)

    c.barrier = barrier

    main_stack = ExitStack()
    sems = {}
    for snm in SEMS:
        sems[snm] = main_stack.enter_context(nc.semaphore(snm))
    c.sems = sems

    sb = {}
    c.sb = sb

    def alloc(stk, name, shape, dt):
        sb[name] = stk.enter_context(nc.sbuf_tensor(name, shape, dt))
        return sb[name]

    c.alloc = alloc

    for nm, shp, dt in [("ident", [128, 128], BF16), ("identf", [128, 128], F32),
                        ("tu", [128, 128], F32), ("maskneg", [128, 128], F32),
                        ("ones_rf", [1, 128], F32), ("ones_rb", [1, 128], BF16),
                        ("ones_cb", [128, 1], BF16), ("convw_b", [128, 96], F32),
                        ("convb_f", [128, 24], F32), ("dtb", [16, 1], F32),
                        ("aneg", [16, 1], F32), ("dcols", [128, 16], F32),
                        ("swscol", [128, 1], F32), ("sc_f", [1, 4], F32),
                        ("nrmw", [128, 32], F32), ("rows", [1, 2048], F32),
                        ("rowbf", [1, 1024], BF16), ("kcol", [16, 4], F32)]:
        alloc(main_stack, nm, shp, dt)

    psum = {}
    psum["A"] = main_stack.enter_context(nc.psum_tensor("psA", [128, 1024], F32))
    psum["Bk"] = main_stack.enter_context(nc.psum_tensor("psB", [128, 1024], F32))
    psum["C"] = main_stack.enter_context(nc.psum_tensor("psC", [128, 512], F32))
    psum["D"] = main_stack.enter_context(nc.psum_tensor("psD", [128, 512], F32))
    psum["Ek"] = main_stack.enter_context(nc.psum_tensor("psE", [128, 512], F32))
    psum["T"] = main_stack.enter_context(nc.psum_tensor("psT", [128, 1024], BF16))
    c.psum = psum

    emit_phases(c)

    OWN = {"tensor": ("sT",), "vector": ("sV",), "scalar": ("sS",), "gpsimd": ("sG",), "sync": ()}
    with nc.Block() as block:
        def mk(eng_name):
            own = OWN[eng_name]

            def run(eng):
                waited = {}
                done_own = {s: 0 for s in own}
                for waits, emit, inc, step in streams[eng_name]:
                    for tok in waits:
                        if waited.get(tok.sem, -1) >= tok.val:
                            continue
                        if DBG.get("prune_own") and tok.sem in done_own and done_own[tok.sem] >= tok.val:
                            continue
                        eng.wait_ge(sems[tok.sem], tok.val)
                        waited[tok.sem] = tok.val
                    if emit is None:
                        continue
                    inst = emit(eng)
                    if inc is not None:
                        inst.then_inc(sems[inc], step)
                        if inc in done_own:
                            done_own[inc] += step
            return run
        block.tensor(mk("tensor"))
        block.vector(mk("vector"))
        block.scalar(mk("scalar"))
        block.gpsimd(mk("gpsimd"))
        block.sync(mk("sync"))

    main_stack.close()
    return nc


def emit_phases(c):
    op, sb, ps, d = c.op, c.sb, c.psum, c.dram

    tcs = []
    for nm_s, nm_d in [("identh", "ident"), ("identf_h", "identf"), ("tuh", "tu"),
                       ("maskh", "maskneg"), ("ones_r_f", "ones_rf"),
                       ("ones_r_b", "ones_rb"), ("ones_c_b", "ones_cb"),
                       ("convbh", "convb_f"), ("dtb_h", "dtb"), ("nrmw_h", "nrmw")]:
        tcs.append(op("sync", lambda e, a=d[nm_s], b=sb[nm_d]: e.dma_start(out=b[:], in_=a[:]), inc="dK", dma=True))
    t_convw = op("sync", lambda e: e.dma_start(out=sb["convw_b"][:], in_=d["convw"][:]), inc="dK", dma=True)
    t_sc1 = op("sync", lambda e: e.dma_start(out=sb["rowbf"][0:1, 0:1], in_=d["isc_h"][:]), inc="dK", dma=True)
    t_sc2 = op("sync", lambda e: e.dma_start(out=sb["sc_f"][0:1, 1:2], in_=d["wsc_h"][:]), inc="dK", dma=True)
    t_dr = op("sync", lambda e: e.dma_start(out=sb["rows"][0:1, 0:16], in_=d["d_h"][:]), inc="dK", dma=True)
    t_al = op("sync", lambda e: e.dma_start(out=sb["aneg"][:], in_=d["alog_h"][:]), inc="dK", dma=True)
    kall0 = Tok("dK", c.counts["dK"])
    t_s1 = op("scalar", lambda e: e.copy(sb["sc_f"][0:1, 0:1], sb["rowbf"][0:1, 0:1]), waits=[kall0], inc="sS")
    t_s2 = op("vector", lambda e: e.tensor_tensor(sb["sc_f"][0:1, 2:3], sb["sc_f"][0:1, 0:1], sb["sc_f"][0:1, 1:2], ALU.mult), waits=[t_s1, t_sc2], inc="sV")
    t_s3 = op("vector", lambda e: e.tensor_scalar(sb["sc_f"][0:1, 2:3], sb["sc_f"][0:1, 2:3], 0.5, None, ALU.mult, ALU.bypass), waits=[t_s2], inc="sV")
    t_s4 = op("vector", lambda e: e.reciprocal(sb["sc_f"][0:1, 3:4], sb["sc_f"][0:1, 0:1]), waits=[t_s2], inc="sV")
    t_a1 = op("scalar", lambda e: e.activation(sb["aneg"][:], sb["aneg"][:], AF.Exp), waits=[kall0], inc="sS")
    t_a2 = op("vector", lambda e: e.tensor_scalar(sb["aneg"][:], sb["aneg"][:], -1.0, None, ALU.mult, ALU.bypass), waits=[t_a1], inc="sV")
    t_db = op("tensor", lambda e: e.matmul(ps["C"][:, 0:16], sb["ones_rf"][:], sb["rows"][0:1, 0:16], start=True, stop=True), waits=[kall0], inc="sT")
    t_db2 = op("vector", lambda e: e.tensor_copy(sb["dcols"][:], ps["C"][:, 0:16]), waits=[t_db], inc="sV")
    t_sw = op("tensor", lambda e: e.matmul(ps["C"][:, 16:17], sb["ones_rf"][:], sb["sc_f"][0:1, 2:3], start=True, stop=True), waits=[t_s3, t_db2], inc="sT")
    t_sw2 = op("vector", lambda e: e.tensor_copy(sb["swscol"][:], ps["C"][:, 16:17]), waits=[t_sw], inc="sV")
    kall = Tok("dK", c.counts["dK"])
    c.kall = kall
    t_k1 = op("vector", lambda e: e.memset(sb["kcol"][:, 0:1], EPS_LN), inc="sV")
    t_k2 = op("vector", lambda e: e.memset(sb["kcol"][:, 1:2], 1.0), inc="sV")
    t_k3 = op("vector", lambda e: e.memset(sb["kcol"][:, 2:3], EPS_GN), inc="sV")
    c.kcol_tok = t_k3
    setup_toks = [kall, t_a2, t_db2, t_sw2, t_k3]

    stg_in = []
    for oc in range(NOC):
        src = d["winh"][oc].rearrange("ic p q -> p ic q")
        dst = d["win_bf"][oc]
        stg_in.append(op("gpsimd", lambda e, s_=src, t_=dst: e.dma_start(out=t_, in_=s_), inc="dG", dma=True))
    c.stgin_last = stg_in[-1]

    st1 = ExitStack()
    c.alloc(st1, "stg32", [128, 2 * 1024], F32)
    c.alloc(st1, "stgbf", [128, 2 * 1024], BF16)
    c.alloc(st1, "xt", [128, 16 * T1], F32)
    c.alloc(st1, "xtB", [128, 16 * T1], F32)
    c.alloc(st1, "xbf", [128, 16 * T1], BF16)
    c.alloc(st1, "xbfB", [128, 16 * T1], BF16)
    c.alloc(st1, "sqb", [128, 16 * T1], BF16)
    c.alloc(st1, "xq", [128, 16 * T1], FP8)
    c.alloc(st1, "xqB", [128, 16 * T1], FP8)
    c.alloc(st1, "x1me", [128, 8 * T1], BF16)
    c.alloc(st1, "x1meB", [128, 8 * T1], BF16)
    c.alloc(st1, "wffn", [128, 16 * 1024], FP8)
    c.alloc(st1, "bcsb", [128, 1024], BF16)

    stg_store = {0: None, 1: None}
    for ic in range(32):
        slp = ic % 2
        s32 = sb["stg32"][:, slp * 1024:slp * 1024 + 1024]
        sbfv = sb["stgbf"][:, slp * 1024:slp * 1024 + 1024]
        tl = op("sync", lambda e, ic=ic, s32=s32: e.dma_start(out=s32, in_=d["wouth"][ic].rearrange("p ec q -> p (ec q)")), waits=[stg_store[slp]], inc="dW", dma=True)
        tv = op("vector", lambda e, ic=ic, s32=s32, sbfv=sbfv: e.tensor_scalar(sbfv, s32, sb["nrmw"][:, ic:ic + 1], None, ALU.mult, ALU.bypass), waits=[tl, kall], inc="sV")
        stg_store[slp] = op("sync", lambda e, ic=ic, sbfv=sbfv: e.dma_start(out=d["wout_bf"][:, :, ic, :], in_=sbfv.rearrange("p (ec q) -> p ec q", ec=8)), waits=[tv], inc="dW", dma=True)
    c.stgout_last = stg_store[1]

    t_ffw = op("sync", lambda e: e.dma_start(out=sb["wffn"][:], in_=d["wf1h"][:]), inc="dK", dma=True)
    t_ffw = Tok("dK", c.counts["dK"])

    if DBG["phases"] < 1:
        fin(c, c.stgout_last)
        st1.close()
        return

    ccw = {0: [], 1: []}
    p1_last_ln = None
    prev_ln1 = None
    ev_hist1 = {}
    tw_hist1 = {}
    psu1 = {}
    for tau in range(NT1):
        sx = ("", "B")[tau % 2]
        xtn, bfn, xqn, x1n = "xt" + sx, "xbf" + sx, "xq" + sx, "x1me" + sx
        tx = op("sync", lambda e, tau=tau, xtn=xtn: e.dma_start(
            out=sb[xtn][:].rearrange("p (c s) -> p c s", c=16),
            in_=d["xh"][:].rearrange("p (ch s) -> p ch s", ch=16)[:, :, tau * T1:(tau + 1) * T1]),
            waits=[ev_hist1.get(tau - 2)], inc="dX", dma=True)
        ln = emit_ln(c, src=(xtn, F32), Tt=T1, quant=True, waits=[tx, t_s4],
                     bufs=dict(bf=bfn, sq="sqb", t1="sqb", out=bfn, xq=xqn), prev_ln=prev_ln1)
        prev_ln1 = ln["last"]
        p1_last_ln = ln["last"]
        ev_toks = []
        for oc in range(8):
            pst = (ps["A"], ps["Bk"])[oc % 2]
            half = (oc // 2) % 2
            pv = pst[:, half * 512:half * 512 + 512]
            key = (oc % 2, half)
            w0 = ([ev_toks[oc - 2]] if oc >= 2 else [psu1.get(key)]) + [ln["xq"], t_ffw]
            w0 = [x for x in w0 if x is not None]
            mm = None
            for ic in range(16):
                mm = op("tensor", lambda e, oc=oc, ic=ic, pv=pv, xqn=xqn: e.matmul(
                    pv, sb["wffn"][:, ic * 1024 + oc * 128: ic * 1024 + oc * 128 + 128],
                    sb[xqn][:, ic * T1:(ic + 1) * T1], start=(ic == 0), stop=(ic == 15)),
                    waits=(w0 if ic == 0 else []), inc=("sT" if ic == 15 else None))
            evw = [mm, t_sw2] + ([tw_hist1.get(tau - 2)] if (oc == 0 and (tau - 2) in tw_hist1) else [])
            ev = op("vector", lambda e, oc=oc, pv=pv, x1n=x1n, xtn=xtn: e.scalar_tensor_tensor(
                sb[x1n][:, oc * T1:(oc + 1) * T1], pv, sb["swscol"][:, 0:1],
                sb[xtn][:, oc * T1:(oc + 1) * T1], ALU.mult, ALU.add),
                waits=evw, inc="sV")
            ev_toks.append(ev)
            psu1[key] = ev
        half = tau // (NT1 // 2)
        base = (tau % (NT1 // 2)) * T1
        tw = op("sync", lambda e, half=half, base=base, x1n=x1n: e.dma_start(
            out=d[f"cc1i{half}"][:].rearrange("p (ch s) -> p ch s", ch=8)[:, :, base:base + T1],
            in_=sb[x1n][:].rearrange("p (c s) -> p c s", c=8)), waits=ev_toks, inc="dO", dma=True)
        ccw[half].append(tw)
        ev_hist1[tau] = ev_toks[-1]
        tw_hist1[tau] = tw

    def mk_ag1(half):
        if DBG["no_cc"]:
            return op("gpsimd", lambda e: e.engine_nop(), waits=[ccw[half][-1]], inc="cc")
        return op("gpsimd", lambda e, half=half: e.collective_compute(
            "AllGather", ALU.bypass, ins=[d[f"cc1i{half}"][:]], outs=[d[f"cc1o{half}"][:]],
            replica_groups=[[0, 1], [2, 3], [4, 5], [6, 7]]), waits=[ccw[half][-1]], inc="cc")
    ag1 = [mk_ag1(0), None]
    if DBG["phases"] < 2:
        ag1[1] = mk_ag1(1)
        fin(c, ag1[1])
        st1.close()
        return

    store_hist = {}
    prev_ln = p1_last_ln
    for tau in range(NT1):
        half = tau // (NT1 // 2)
        if tau == NT1 // 2:
            ag1[1] = mk_ag1(1)
        base = (tau % (NT1 // 2)) * T1
        slp = tau % 2
        bfn = ("xbf", "xbfB")[slp]
        outn = bfn
        w = [ag1[half], store_hist.get(tau - 2) if tau >= 2 else p1_last_ln]
        tl = []
        for sslot in range(2):
            tl.append(op("sync", lambda e, half=half, base=base, sslot=sslot, bfn=bfn: e.dma_start(
                out=sb[bfn][:, sslot * 8 * T1:(sslot + 1) * 8 * T1].rearrange("p (c s) -> p c s", c=8),
                in_=d[f"cc1o{half}"][sslot].rearrange("p (ch t) -> p ch t", ch=8)[:, :, base:base + T1]),
                waits=w, inc="dX", dma=True))
        ln = emit_ln(c, src=(bfn, BF16), Tt=T1, quant=False, waits=tl,
                     bufs=dict(bf=bfn, sq="sqb", t1="sqb", out=outn), prev_ln=prev_ln)
        prev_ln = ln["last"]
        store_hist[tau] = op("sync", lambda e, tau=tau, outn=outn: e.dma_start(
            out=d["xnorm_d"][:].rearrange("p (ch s) -> p ch s", ch=16)[:, :, tau * T1:(tau + 1) * T1],
            in_=sb[outn][:].rearrange("p (c s) -> p c s", c=16)), waits=[ln["xn"]], inc="dO", dma=True)
    t_p1b = store_hist[NT1 - 1]
    if "xn2" in c.dumps:
        op("sync", lambda e: e.dma_start(out=c.dumps["xn2"][:], in_=d["xnorm_d"][:]), waits=[t_p1b], inc="dX", dma=True)
    st1.close()
    if DBG["phases"] < 3:
        fin(c, t_p1b)
        return

    c.barrier()
    st2 = ExitStack()
    c.alloc(st2, "xn", [128, 16 * T2], BF16)
    c.alloc(st2, "wbuf", [128, 3 * 2048], BF16)
    c.alloc(st2, "convio", [128, 2 * 24 * (4 + T2)], BF16)
    c.alloc(st2, "carry", [128, 24 * 4], BF16)
    c.alloc(st2, "ctmp1", [128, 2048], BF16)
    c.alloc(st2, "ctmp2", [128, 2 * T2], BF16)
    c.alloc(st2, "sg", [128, 2 * 16 * T2], BF16)
    c.alloc(st2, "dtraw", [16, 2 * T2], F32)
    c.alloc(st2, "dtsp", [16, T2], F32)
    c.alloc(st2, "afm", [16, T2], F32)
    c.alloc(st2, "cumafm", [16, T2], F32)
    c.alloc(st2, "cuma_p0", [1, 2048], F32)
    c.alloc(st2, "tmb", [128, 64], F32)
    c.alloc(st2, "spc", [128, 48], F32)
    c.alloc(st2, "srun", [128, 16 * 128], F32)
    c.alloc(st2, "srun2", [128, 16 * 128], F32)
    c.alloc(st2, "sbf", [128, 16 * 128], BF16)
    c.alloc(st2, "xts", [128, 4 * 128], BF16)
    c.alloc(st2, "bts", [128, 128], BF16)
    c.alloc(st2, "b4", [128, 4 * 128], BF16)
    c.alloc(st2, "sc4", [128, 4 * 128], BF16)
    c.alloc(st2, "d4", [128, 4 * 128], F32)
    c.alloc(st2, "l4", [128, 4 * 128], BF16)
    c.alloc(st2, "sd4", [128, 4 * 128], BF16)
    c.alloc(st2, "c4", [128, 4 * 128], BF16)
    c.alloc(st2, "t1b", [128, 4 * 128], F32)
    c.alloc(st2, "h32c", [128, 16 * 128], BF16)
    c.alloc(st2, "h32n2", [128, 2 * 16 * 128], BF16)

    def mk_ag2(half):
        return op("gpsimd", lambda e, half=half: e.collective_compute(
            "AllGather", ALU.bypass, ins=[d[f"cc2i{half}"][:]], outs=[d[f"cc2o{half}"][:]],
            replica_groups=[[0, 1], [2, 3], [4, 5], [6, 7]]), waits=[cw2[half][-1]], inc="cc")
    t_done, cw2 = emit_sweep(c, t_p1b, setup_toks)
    ag2 = [mk_ag2(0), None]
    st2.close()
    if DBG["phases"] < 4:
        ag2[1] = mk_ag2(1)
        fin(c, ag2[1])
        return

    c.barrier()
    st3 = ExitStack()
    c.alloc(st3, "rhs3", [128, 2 * 32 * T3], BF16)
    c.alloc(st3, "wo", [128, 2 * 32 * 128], BF16)
    c.alloc(st3, "x2me", [128, 2 * 8 * T3], BF16)
    c.alloc(st3, "x1r", [128, 2 * 8 * T3], BF16)

    last3 = {0: [], 1: []}
    tw_hist3 = {}
    ev_hist3 = {}
    wo_prev = {}
    for sl in range(NT3):
        half = sl // (NT3 // 2)
        if sl == NT3 // 2:
            ag2[1] = mk_ag2(1)
        base = (sl % (NT3 // 2)) * T3
        rbase = (sl % 2) * 32 * T3
        xbase = (sl % 2) * 8 * T3
        w = [ag2[half], ev_hist3.get(sl - 2)]
        trh = []
        for sslot in range(2):
            trh.append(op("sync", lambda e, half=half, base=base, sslot=sslot, rbase=rbase: e.dma_start(
                out=sb["rhs3"][:, rbase + sslot * 16 * T3:rbase + (sslot + 1) * 16 * T3].rearrange("p (c s) -> p c s", c=16),
                in_=d[f"cc2o{half}"][sslot].rearrange("p (ch t) -> p ch t", ch=16)[:, :, base:base + T3]),
                waits=w, inc="dX", dma=True))
        tx1 = op("sync", lambda e, half=half, base=base, xbase=xbase: e.dma_start(
            out=sb["x1r"][:, xbase:xbase + 8 * T3].rearrange("p (c s) -> p c s", c=8),
            in_=d[f"cc1i{half}"][:].rearrange("p (ch t) -> p ch t", ch=8)[:, :, base:base + T3]),
            waits=w, inc="dX", dma=True)
        ev_toks = []
        for ec in range(8):
            wws = [ev_toks[ec - 2]] if ec >= 2 else [wo_prev.get(ec % 2)]
            wws = [x for x in wws if x is not None]
            two = op("scalar", lambda e, ec=ec: e.dma_start(
                out=sb["wo"][:, (ec % 2) * 4096:(ec % 2) * 4096 + 4096],
                in_=d["wout_bf"][:, ec].rearrange("p ic q -> p (ic q)")),
                waits=wws + [c.stgout_last], inc=("dA0", "dA1")[ec % 2], dma=True)
            pv = (ps["A"], ps["Bk"])[ec % 2][:, 0:512]
            mm = None
            for ic in range(32):
                mm = op("tensor", lambda e, ec=ec, ic=ic, pv=pv, rbase=rbase: e.matmul(
                    pv, sb["wo"][:, (ec % 2) * 4096 + ic * 128:(ec % 2) * 4096 + ic * 128 + 128],
                    sb["rhs3"][:, rbase + ic * T3:rbase + (ic + 1) * T3], start=(ic == 0), stop=(ic == 31)),
                    waits=(([two, tx1] + trh + wws) if ic == 0 else []),
                    inc=("sT" if ic == 31 else None))
            evw = [mm, tx1] + ([tw_hist3.get(sl - 2)] if (ec == 0 and (sl - 2) in tw_hist3) else [])
            ev = op("vector", lambda e, ec=ec, pv=pv, xbase=xbase: e.tensor_tensor(
                sb["x2me"][:, xbase + ec * T3:xbase + (ec + 1) * T3], pv,
                sb["x1r"][:, xbase + ec * T3:xbase + (ec + 1) * T3], ALU.add),
                waits=evw, inc="sV")
            ev_toks.append(ev)
            wo_prev[ec % 2] = ev
        tw = op("sync", lambda e, half=half, base=base, xbase=xbase: e.dma_start(
            out=d[f"cc3i{half}"][:].rearrange("p (ch t) -> p ch t", ch=8)[:, :, base:base + T3],
            in_=sb["x2me"][:, xbase:xbase + 8 * T3].rearrange("p (c s) -> p c s", c=8)), waits=ev_toks, inc="dO", dma=True)
        last3[half].append(tw)
        tw_hist3[sl] = tw
        ev_hist3[sl] = ev_toks[-1]
    def mk_ag3(half):
        return op("gpsimd", lambda e, half=half: e.collective_compute(
            "AllGather", ALU.bypass, ins=[d[f"cc3i{half}"][:]], outs=[d[f"cc3o{half}"][:]],
            replica_groups=[[0, 1], [2, 3], [4, 5], [6, 7]]), waits=[last3[half][-1]], inc="cc")
    ag3 = [mk_ag3(0), None]
    st3.close()
    if DBG["phases"] < 5:
        ag3[1] = mk_ag3(1)
        fin(c, ag3[1])
        return

    c.barrier()
    st4 = ExitStack()
    for slp in ("", "B"):
        c.alloc(st4, "xbf4" + slp, [128, 16 * T1], BF16)
        c.alloc(st4, "xq4" + slp, [128, 16 * T1], FP8)
        c.alloc(st4, "x2r" + slp, [128, 8 * T1], BF16)
        c.alloc(st4, "outst" + slp, [128, 8 * T1], F32)
        c.alloc(st4, "xnb4" + slp, [128, 16 * T1], BF16)
    c.alloc(st4, "sqb4", [128, 16 * T1], BF16)
    c.alloc(st4, "wffn2", [128, 16 * 1024], FP8)
    c.alloc(st4, "bcsb4", [128, 1024], BF16)

    t_ffw2 = op("sync", lambda e: e.dma_start(out=sb["wffn2"][:], in_=d["wf2h"][:]), inc="dK", dma=True)
    lastout_hist = {}
    psu = {}
    prev_ln = None
    last_out = None
    for tau in range(NT1):
        half = tau // (NT1 // 2)
        if tau == NT1 // 2:
            ag3[1] = mk_ag3(1)
        base = (tau % (NT1 // 2)) * T1
        sx = ("", "B")[tau % 2]
        bfn, xqn, x2rn, outsn, xnn = "xbf4" + sx, "xq4" + sx, "x2r" + sx, "outst" + sx, "xnb4" + sx
        w = [ag3[half], lastout_hist.get(tau - 2)]
        tl = []
        for sslot in range(2):
            tl.append(op("sync", lambda e, half=half, base=base, sslot=sslot, bfn=bfn: e.dma_start(
                out=sb[bfn][:, sslot * 8 * T1:(sslot + 1) * 8 * T1].rearrange("p (c s) -> p c s", c=8),
                in_=d[f"cc3o{half}"][sslot].rearrange("p (ch t) -> p ch t", ch=8)[:, :, base:base + T1]),
                waits=w, inc="dX", dma=True))
        tx2 = op("sync", lambda e, half=half, base=base, x2rn=x2rn: e.dma_start(
            out=sb[x2rn][:].rearrange("p (c s) -> p c s", c=8),
            in_=d[f"cc3i{half}"][:].rearrange("p (ch t) -> p ch t", ch=8)[:, :, base:base + T1]),
            waits=w, inc="dX", dma=True)
        ln = emit_ln(c, src=(bfn, BF16), Tt=T1, quant=True, waits=tl + [t_s4],
                     bufs=dict(bf=bfn, sq="sqb4", t1="sqb4", out=xnn, xq=xqn, bc="bcsb4"),
                     prev_ln=prev_ln)
        prev_ln = ln["last"]
        ev_toks = []
        for oc in range(8):
            pst = (ps["A"], ps["Bk"])[oc % 2]
            half2 = (oc // 2) % 2
            pv = pst[:, half2 * 512:half2 * 512 + 512]
            key = (oc % 2, half2)
            w0 = ([ev_toks[oc - 2]] if oc >= 2 else [psu.get(key)]) + [ln["xq"], t_ffw2]
            w0 = [x for x in w0 if x is not None]
            mm = None
            for ic in range(16):
                mm = op("tensor", lambda e, oc=oc, ic=ic, pv=pv, xqn=xqn: e.matmul(
                    pv, sb["wffn2"][:, ic * 1024 + oc * 128: ic * 1024 + oc * 128 + 128],
                    sb[xqn][:, ic * T1:(ic + 1) * T1], start=(ic == 0), stop=(ic == 15)),
                    waits=(w0 if ic == 0 else []), inc=("sT" if ic == 15 else None))
            evw = [mm, tx2] + ([lastout_hist.get(tau - 2)] if (oc == 0 and (tau - 2) in lastout_hist) else [])
            ev = op("vector", lambda e, oc=oc, pv=pv, outsn=outsn, x2rn=x2rn: e.scalar_tensor_tensor(
                sb[outsn][:, oc * T1:(oc + 1) * T1], pv, sb["swscol"][:, 0:1],
                sb[x2rn][:, oc * T1:(oc + 1) * T1], ALU.mult, ALU.add),
                waits=evw, inc="sV")
            ev_toks.append(ev)
            psu[key] = ev
        last_out = op("sync", lambda e, tau=tau, outsn=outsn: e.dma_start(
            out=d["outh"][:].rearrange("p (ch s) -> p ch s", ch=8)[:, :, tau * T1:(tau + 1) * T1],
            in_=sb[outsn][:].rearrange("p (c s) -> p c s", c=8)), waits=ev_toks, inc="dO", dma=True)
        lastout_hist[tau] = last_out
    st4.close()
    fin(c, last_out)


def fin(c, tok):
    c.op("gpsimd", lambda e: e.engine_nop(), waits=[tok], inc="sG")


def emit_ln(c, src, Tt, quant, waits, bufs, prev_ln=None):
    op, sb, ps = c.op, c.sb, c.psum
    nm, dt = src
    bf, sq, t1n, outn = bufs["bf"], bufs["sq"], bufs["t1"], bufs["out"]
    bcn = bufs.get("bc", "bcsb")
    w0 = list(waits) + ([prev_ln] if prev_ln is not None else [])
    if dt == F32:
        tb = op("vector", lambda e: e.tensor_copy(sb[bf][:], sb[nm][:]), waits=w0, inc="sV")
        bfn = bf
    else:
        tb = op("vector", lambda e: e.engine_nop(), waits=w0, inc="sV")
        bfn = nm
    tq = op("vector", lambda e: e.tensor_tensor(sb[sq][:], sb[bfn][:], sb[bfn][:], ALU.mult), waits=[tb], inc="sV")
    m1 = m2 = None
    for ic in range(16):
        m1 = op("tensor", lambda e, ic=ic: e.matmul(ps["C"][0:1, 0:Tt], sb["ones_cb"][:], sb[bfn][:, ic * Tt:(ic + 1) * Tt], start=(ic == 0), stop=(ic == 15)), waits=([tb] if ic == 0 else []), inc=("sT" if ic == 15 else None))
    for ic in range(16):
        m2 = op("tensor", lambda e, ic=ic: e.matmul(ps["D"][0:1, 0:Tt], sb["ones_cb"][:], sb[sq][:, ic * Tt:(ic + 1) * Tt], start=(ic == 0), stop=(ic == 15)), waits=([tq] if ic == 0 else []), inc=("sT" if ic == 15 else None))
    r = sb["rows"]
    t1 = op("vector", lambda e: e.tensor_scalar(r[0:1, 0:Tt], ps["C"][0:1, 0:Tt], 1.0 / 2048, None, ALU.mult, ALU.bypass), waits=[m1], inc="sV")
    t2 = op("vector", lambda e: e.tensor_scalar(r[0:1, 512:512 + Tt], ps["D"][0:1, 0:Tt], 1.0 / 2048, None, ALU.mult, ALU.bypass), waits=[m2], inc="sV")
    t3 = op("vector", lambda e: e.tensor_tensor(r[0:1, 1536:1536 + Tt], r[0:1, 0:Tt], r[0:1, 0:Tt], ALU.mult), waits=[t1], inc="sV")
    t4 = op("vector", lambda e: e.tensor_tensor(r[0:1, 1024:1024 + Tt], r[0:1, 512:512 + Tt], r[0:1, 1536:1536 + Tt], ALU.subtract), waits=[t2, t3], inc="sV")
    t5 = op("scalar", lambda e: e.activation(r[0:1, 1536:1536 + Tt], r[0:1, 1024:1024 + Tt], AF.Ln, bias=sb["kcol"][0:1, 0:1]), waits=[t4, c.kcol_tok], inc="sS")
    t6 = op("scalar", lambda e: e.activation(r[0:1, 1024:1024 + Tt], r[0:1, 1536:1536 + Tt], AF.Exp, scale=-0.5), waits=[t5], inc="sS")
    if quant:
        t7 = op("vector", lambda e: e.tensor_scalar(sb["rowbf"][0:1, 0:Tt], r[0:1, 1024:1024 + Tt], sb["sc_f"][0:1, 3:4], None, ALU.mult, ALU.bypass), waits=[t6], inc="sV")
    else:
        t7 = op("vector", lambda e: e.tensor_copy(sb["rowbf"][0:1, 0:Tt], r[0:1, 1024:1024 + Tt]), waits=[t6], inc="sV")
    t8 = op("vector", lambda e: e.tensor_tensor(sb["rowbf"][0:1, 512:512 + Tt], r[0:1, 0:Tt], sb["rowbf"][0:1, 0:Tt], ALU.mult), waits=[t7], inc="sV")
    b1 = op("tensor", lambda e: e.matmul(ps["C"][:, 0:512], sb["ones_rb"][:], sb["rowbf"][0:1, 0:512], start=True, stop=True), waits=[t8], inc="sT")
    b2 = op("tensor", lambda e: e.matmul(ps["D"][:, 0:512], sb["ones_rb"][:], sb["rowbf"][0:1, 512:1024], start=True, stop=True), waits=[t8], inc="sT")
    b3 = op("vector", lambda e: e.tensor_copy(sb[bcn][:, 0:512], ps["C"][:, 0:512]), waits=[b1], inc="sV")
    b3b = op("vector", lambda e: e.tensor_copy(sb[bcn][:, 512:1024], ps["D"][:, 0:512]), waits=[b2], inc="sV")
    rep_rs = sb[bcn][:, 0:Tt].unsqueeze(1).broadcast_to([128, 16, Tt])
    rep_mu = sb[bcn][:, 512:512 + Tt].unsqueeze(1).broadcast_to([128, 16, Tt])
    g1 = op("gpsimd", lambda e: e.tensor_tensor(sb[t1n][:].rearrange("p (c t) -> p c t", c=16), sb[bfn][:].rearrange("p (c t) -> p c t", c=16), rep_rs, ALU.mult), waits=[b3, b3b, tq], inc="sG")
    v1 = op("vector", lambda e: e.tensor_tensor(sb[outn][:].rearrange("p (c t) -> p c t", c=16), sb[t1n][:].rearrange("p (c t) -> p c t", c=16), rep_mu, ALU.subtract), waits=[g1], inc="sV")
    res = {"xn": v1, "last": v1}
    if quant:
        res["xq"] = op("vector", lambda e: e.tensor_scalar(sb[bufs["xq"]][:], sb[outn][:], 448.0, -448.0, ALU.min, ALU.max), waits=[v1], inc="sV")
        res["last"] = res["xq"]
    return res


def emit_sweep(c, t_xn_ready, setup_toks):
    op, sb, ps, d = c.op, c.sb, c.psum, c.dram
    cw = {0: [], 1: []}
    t_sruninit = op("vector", lambda e: e.memset(sb["srun"][:], 0.0), inc="sV")
    t_sbfinit = op("vector", lambda e: e.memset(sb["sbf"][:], 0.0), inc="sV")
    t_carinit = op("vector", lambda e: e.memset(sb["carry"][:], 0.0), inc="sV")
    st = {"srun": "srun", "srun2": "srun2", "last_chain": t_sruninit, "last_sbf": t_sbfinit,
          "last_y": None, "h32toks": [], "chunkpar": 0, "prev_ex2": None, "last_gn": None,
          "last_carry": t_carinit, "h32dma": None, "yprev": {}, "last_vh": None,
          "last_vsq": None, "convio_free": {}}
    psA_user = {}
    prev_w = {}
    info = {}
    GRPS = [list(range(0, 11)), list(range(11, 21)), list(range(21, 31)), list(range(31, 41))]
    NTT = DBG["sweep_tiles"] or NT2

    def emit_inproj_grp(tt, ocs):
        par = tt % 2
        cvb = par * 24 * (4 + T2)
        tb0 = par * T2
        sgb = par * 16 * T2
        inf = info[tt]
        free_tok = st["convio_free"].get(par)
        for oc in ocs:
            bslot = oc % 3
            twl = op("gpsimd", lambda e, oc=oc, bslot=bslot: e.dma_start(
                out=sb["wbuf"][:, bslot * 2048:bslot * 2048 + 2048],
                in_=d["win_bf"][oc].rearrange("p ic q -> p (ic q)")),
                waits=[prev_w.get(bslot), c.stgin_last], inc=("dA0", "dA1")[oc % 2], dma=True)
            sub = oc % 2
            pv = ps["A"][:, sub * 512:sub * 512 + 512]
            mm = None
            for ic in range(16):
                mm = op("tensor", lambda e, oc=oc, ic=ic, pv=pv, bslot=bslot: e.matmul(
                    pv, sb["wbuf"][:, bslot * 2048 + ic * 128:bslot * 2048 + ic * 128 + 128],
                    sb["xn"][:, ic * T2:ic * T2 + T2], start=(ic == 0), stop=(ic == 15)),
                    waits=([twl, inf["txn"], psA_user.get(sub)] if ic == 0 else []),
                    inc=("sT" if ic == 15 else None))
            prev_w[bslot] = mm
            if oc < 24:
                gv = op("vector", lambda e, oc=oc, pv=pv, cvb=cvb: e.tensor_copy(
                    sb["convio"][:, cvb + oc * (4 + T2) + 4: cvb + oc * (4 + T2) + 4 + T2], pv),
                    waits=[mm, free_tok], inc="sV")
            elif oc == 24:
                gv = op("vector", lambda e, pv=pv, tb0=tb0: e.tensor_copy(
                    sb["dtraw"][:, tb0:tb0 + T2], pv[0:16, :]), waits=[mm, free_tok], inc="sV")
            else:
                lh = oc - 25
                gv = op("vector", lambda e, lh=lh, pv=pv, sgb=sgb: e.tensor_copy(
                    sb["sg"][:, sgb + lh * T2:sgb + (lh + 1) * T2], pv), waits=[mm, free_tok], inc="sV")
            psA_user[sub] = gv
            inf["gv"].append(gv)

    def emit_post(tt):
        par = tt % 2
        cvb = par * 24 * (4 + T2)
        tb0 = par * T2
        inf = info[tt]
        gv24 = inf["gv"][24]
        s1 = op("scalar", lambda e, tb0=tb0: e.activation(sb["afm"][:, 0:T2], sb["dtraw"][:, tb0:tb0 + T2], AF.Relu, bias=sb["dtb"][:, 0:1]), waits=[gv24] + setup_toks, inc="sS")
        s2 = op("scalar", lambda e, tb0=tb0: e.activation(sb["dtsp"][:, 0:T2], sb["dtraw"][:, tb0:tb0 + T2], AF.Abs, bias=sb["dtb"][:, 0:1]), waits=[gv24], inc="sS")
        s3 = op("scalar", lambda e: e.activation(sb["dtsp"][:, 0:T2], sb["dtsp"][:, 0:T2], AF.Exp, scale=-1.0), waits=[s2], inc="sS")
        s4 = op("scalar", lambda e: e.activation(sb["dtsp"][:, 0:T2], sb["dtsp"][:, 0:T2], AF.Ln, bias=sb["kcol"][0:16, 1:2]), waits=[s3], inc="sS")
        v5 = op("vector", lambda e: e.tensor_tensor(sb["dtsp"][:, 0:T2], sb["afm"][:, 0:T2], sb["dtsp"][:, 0:T2], ALU.add), waits=[s4, s1], inc="sV")
        v5 = op("vector", lambda e: e.tensor_scalar(sb["dtsp"][:, 0:T2], sb["dtsp"][:, 0:T2], 1e-19, None, ALU.max, ALU.bypass), waits=[v5], inc="sV")
        v6 = op("vector", lambda e: e.tensor_scalar(sb["afm"][:, 0:T2], sb["dtsp"][:, 0:T2], sb["aneg"][:, 0:1], None, ALU.mult, ALU.bypass), waits=[v5], inc="sV")
        inf["tdt"] = v6
        cvv = sb["convio"][:, cvb:cvb + 24 * (4 + T2)].rearrange("p (m t) -> p m t", m=24)
        t_car = op("vector", lambda e, cvv=cvv: e.tensor_copy(cvv[:, :, 0:4], sb["carry"][:].rearrange("p (m t) -> p m t", m=24)),
                   waits=[st["last_carry"]] + [inf["gv"][m] for m in range(24)] + setup_toks, inc="sV")
        silu_toks = []
        vc_toks = []
        for m in range(24):
            cpar = m % 2
            acc1 = sb["ctmp1"][:, cpar * T2:(cpar + 1) * T2]
            acc2 = sb["ctmp2"][:, cpar * T2:(cpar + 1) * T2]

            def um(k, m=m, cvb=cvb):
                base = cvb + m * (4 + T2) + 1 + k
                return sb["convio"][:, base:base + T2]

            def wc(k, m=m):
                return sb["convw_b"][:, m * 4 + k:m * 4 + k + 1]

            wv = [t_car] + ([silu_toks[m - 2]] if m >= 2 else [st["last_vsq"]])
            a1 = op("vector", lambda e, m=m, acc1=acc1: e.tensor_scalar(acc1, um(0, m), wc(0, m), None, ALU.mult, ALU.bypass), waits=wv, inc="sV")
            a2 = op("vector", lambda e, m=m, acc1=acc1, acc2=acc2: e.scalar_tensor_tensor(acc2, um(1, m), wc(1, m), acc1, ALU.mult, ALU.add), waits=[a1], inc="sV")
            a3 = op("vector", lambda e, m=m, acc1=acc1, acc2=acc2: e.scalar_tensor_tensor(acc1, um(2, m), wc(2, m), acc2, ALU.mult, ALU.add), waits=[a2], inc="sV")
            a4 = op("vector", lambda e, m=m, acc1=acc1, acc2=acc2: e.scalar_tensor_tensor(acc2, um(3, m), wc(3, m), acc1, ALU.mult, ALU.add), waits=[a3], inc="sV")
            vc = op("vector", lambda e, m=m, cvb=cvb: e.tensor_copy(
                sb["carry"][:, m * 4:m * 4 + 4],
                sb["convio"][:, cvb + m * (4 + T2) + T2: cvb + m * (4 + T2) + T2 + 4]), waits=[a4], inc="sV")
            vc_toks.append(vc)
            silu_toks.append(op("scalar", lambda e, m=m, acc2=acc2, cvb=cvb: e.activation(
                sb["convio"][:, cvb + m * (4 + T2) + 4: cvb + m * (4 + T2) + 4 + T2],
                acc2, AF.Silu, bias=sb["convb_f"][:, m:m + 1]),
                waits=[vc], inc="sS"))
        st["last_carry"] = vc_toks[-1]
        inf["silu"] = silu_toks

    for tt in range(NTT):
        info[tt] = {"gv": []}
        txn = op("sync", lambda e, tt=tt: e.dma_start(
            out=sb["xn"][:].rearrange("p (c s) -> p c s", c=16),
            in_=d["xnorm_d"][:].rearrange("p (ch s) -> p ch s", ch=16)[:, :, tt * T2:(tt + 1) * T2]),
            waits=[t_xn_ready, (info[tt - 1]["gv"][-1] if tt >= 1 else None)], inc="dX", dma=True)
        info[tt]["txn"] = txn
        for grp in range(4):
            emit_inproj_grp(tt, GRPS[grp])
            if tt >= 1:
                ssd_chunk(c, tt - 1, grp, info[tt - 1]["silu"], info[tt - 1]["tdt"], info[tt - 1]["gv"][-1], st, cw)
        if tt >= 1:
            st["convio_free"][(tt - 1) % 2] = st["last_vh"]
        emit_post(tt)
    for grp in range(4):
        ssd_chunk(c, NTT - 1, grp, info[NTT - 1]["silu"], info[NTT - 1]["tdt"], info[NTT - 1]["gv"][-1], st, cw)
    return st["last_y"], cw


def ssd_chunk(c, u, cc_, silu_toks, tdt, gate_tok, st, cw):
    op, sb, ps, d = c.op, c.sb, c.psum, c.dram
    gchunk = u * NCH + cc_
    t0 = cc_ * 128
    par = u % 2
    cvb = par * 24 * (4 + T2)
    sgb = par * 16 * T2
    cv = sb["convio"]

    def xfm(lh):
        return cv[:, cvb + lh * (4 + T2) + 4 + t0: cvb + lh * (4 + T2) + 4 + t0 + 128]

    def bfm(j):
        return cv[:, cvb + (16 + j) * (4 + T2) + 4 + t0: cvb + (16 + j) * (4 + T2) + 4 + t0 + 128]

    def cfm(j):
        return cv[:, cvb + (20 + j) * (4 + T2) + 4 + t0: cvb + (20 + j) * (4 + T2) + 4 + t0 + 128]

    t_sil = op("scalar", lambda e: e.activation(
        sb["h32c"][:].rearrange("p (lh t) -> p lh t", lh=16),
        sb["sg"][:, sgb:sgb + 16 * T2].rearrange("p (lh t) -> p lh t", lh=16)[:, :, t0:t0 + 128],
        AF.Silu), waits=[gate_tok, st["last_vsq"], st["last_y"]], inc="sS")

    w0 = [tdt, st["last_chain"], st["last_gn"]]
    ta = op("tensor", lambda e, t0=t0: e.transpose(ps["D"][:, 0:16], sb["afm"][:, t0:t0 + 128], sb["identf"][0:16, 0:16]), waits=w0, inc="sT")
    tb = op("tensor", lambda e, t0=t0: e.transpose(ps["D"][:, 16:32], sb["dtsp"][:, t0:t0 + 128], sb["identf"][0:16, 0:16]), waits=[], inc="sT")
    tv = op("vector", lambda e: e.tensor_copy(sb["tmb"][:, 0:32], ps["D"][:, 0:32]), waits=[ta, tb], inc="sV")
    tc = op("tensor", lambda e: e.matmul(ps["Ek"][:, 0:16], sb["tu"][:], sb["tmb"][:, 0:16], start=True, stop=True), waits=[tv, st["last_gn"]], inc="sT")
    tv2 = op("vector", lambda e: e.tensor_copy(sb["tmb"][:, 32:48], ps["Ek"][:, 0:16]), waits=[tc], inc="sV")
    td = op("tensor", lambda e: e.transpose(ps["D"][0:16, 32:160], sb["tmb"][:, 32:48], sb["identf"][:]), waits=[tv2], inc="sT")
    tv3 = op("vector", lambda e, t0=t0: e.tensor_copy(sb["cumafm"][:, t0:t0 + 128], ps["D"][0:16, 32:160]), waits=[td], inc="sV")
    tp0 = op("sync", lambda e, t0=t0: e.dma_start(out=sb["cuma_p0"][0:1, :].rearrange("o (h t) -> o h t", h=16), in_=sb["cumafm"][:, t0:t0 + 128]), waits=[tv3], inc="dP", dma=True)
    s1 = op("scalar", lambda e: e.activation(sb["spc"][:, 0:16], sb["tmb"][:, 16:32], AF.Ln), waits=[tv, st["last_gn"]], inc="sS")
    v4 = op("vector", lambda e: e.tensor_tensor(sb["spc"][:, 0:16], sb["spc"][:, 0:16], sb["tmb"][:, 32:48], ALU.subtract), waits=[s1, tv2], inc="sV")
    lam_rhs = sb["cuma_p0"][0:1, :].rearrange("o (h t) -> o h t", h=16)[:, :, 127:128].rearrange("o h t -> o (h t)")
    tl = op("tensor", lambda e: e.matmul(ps["Ek"][:, 16:32], sb["ones_rf"][:], lam_rhs, start=True, stop=True), waits=[tp0, tv2], inc="sT")
    v5 = op("scalar", lambda e: e.activation(sb["spc"][:, 16:32], ps["Ek"][:, 16:32], AF.Exp), waits=[tl], inc="sS")
    v6 = op("vector", lambda e: e.tensor_tensor(sb["spc"][:, 32:48], ps["Ek"][:, 16:32], sb["tmb"][:, 32:48], ALU.subtract), waits=[tl], inc="sV")
    s3 = op("scalar", lambda e: e.activation(sb["spc"][:, 32:48], sb["spc"][:, 32:48], AF.Exp), waits=[v6], inc="sS")
    v7 = op("vector", lambda e: e.tensor_tensor(sb["spc"][:, 32:48], sb["spc"][:, 32:48], sb["tmb"][:, 16:32], ALU.mult), waits=[s3], inc="sV")

    h32_toks = []
    for j in range(4):
        wsc = (silu_toks if gchunk % NCH == 0 else []) if j == 0 else []
        tsc = op("tensor", lambda e, j=j: e.matmul(ps["C"][:, j * 128:j * 128 + 128], bfm(j), cfm(j), start=True, stop=True),
                 waits=wsc + ([st["last_y"]] if (j == 0 and st["last_y"]) else []), inc="sT")
        pd = (ps["D"], ps["Ek"])[j % 2]
        wdb = [tp0] + ([tv3] if j % 2 == 0 else [v6, v7])
        tdb = op("tensor", lambda e, j=j, pd=pd: e.matmul(pd[:], sb["ones_rf"][:], sb["cuma_p0"][0:1, 4 * j * 128:(4 * j + 4) * 128], start=True, stop=True), waits=wdb, inc="sT")
        ttx = []
        wtx = ([silu_toks[-1]] if (j == 0 and gchunk % NCH == 0) else []) + ([st["prev_ex2"]] if st["prev_ex2"] else [])
        for q in range(4):
            lh = 4 * q + j
            ttx.append(op("tensor", lambda e, q=q, lh=lh: e.transpose(ps["T"][:, q * 128:q * 128 + 128], xfm(lh), sb["ident"][:]), waits=(wtx if q == 0 else []), inc="sT"))
        ttb = op("tensor", lambda e, j=j: e.transpose(ps["T"][:, 512:640], bfm(j), sb["ident"][:]), waits=[], inc="sT")
        ex1 = op("vector", lambda e: e.tensor_copy(sb["xts"][:], ps["T"][:, 0:512]), waits=ttx + [ttb], inc="sV")
        ex2 = op("vector", lambda e: e.tensor_copy(sb["bts"][:], ps["T"][:, 512:640]), waits=[ex1], inc="sV")
        st["prev_ex2"] = ex2
        vb = []
        for q in range(4):
            p_ = 4 * j + q
            vb.append(op("vector", lambda e, q=q, p_=p_: e.tensor_scalar(sb["b4"][:, q * 128:q * 128 + 128], sb["bts"][:], sb["spc"][:, 32 + p_:32 + p_ + 1], None, ALU.mult, ALU.bypass), waits=([ex2, v7] if q == 0 else []), inc="sV"))
        vds = []
        for q in range(4):
            p_ = 4 * j + q
            vds.append(op("vector", lambda e, q=q, p_=p_, pd=pd: e.scalar_tensor_tensor(
                sb["d4"][:, q * 128:q * 128 + 128], pd[:, q * 128:q * 128 + 128], sb["spc"][:, p_:p_ + 1],
                sb["maskneg"][:], ALU.add, ALU.add), waits=([tdb, v4] if q == 0 else []), inc="sV"))
        vd = vds[-1]
        sl4 = op("scalar", lambda e: e.activation(sb["l4"][:], sb["d4"][:], AF.Exp), waits=[vd], inc="sS")
        ssd4 = op("scalar", lambda e, pd=pd: e.activation(sb["sd4"][:], pd[:], AF.Exp), waits=[vd], inc="sS")
        screp = ps["C"][:, j * 128:j * 128 + 128].unsqueeze(1).broadcast_to([128, 4, 128])
        vm4 = op("vector", lambda e, screp=screp: e.tensor_tensor(sb["sc4"][:].rearrange("p (r t) -> p r t", r=4), screp, sb["l4"][:].rearrange("p (r t) -> p r t", r=4), ALU.mult), waits=[sl4, tsc], inc="sV")
        crep = cfm(j).unsqueeze(1).broadcast_to([128, 4, 128])
        vc4 = op("vector", lambda e, j=j, crep=crep: e.tensor_tensor(sb["c4"][:].rearrange("p (r t) -> p r t", r=4), crep, sb["sd4"][:].rearrange("p (r t) -> p r t", r=4), ALU.mult), waits=[ssd4], inc="sV")
        yv = ps["Bk"][:, (j % 2) * 512:(j % 2) * 512 + 512]
        t_y = None
        for q in range(4):
            p_ = 4 * j + q
            w_ = ([vm4, vc4, st["last_sbf"], st["yprev"].get(j % 2)] if q == 0 else [])
            w_ = [x for x in w_ if x is not None]
            op("tensor", lambda e, q=q, yv=yv: e.matmul(yv[:, q * 128:q * 128 + 128], sb["xts"][:, q * 128:q * 128 + 128], sb["sc4"][:, q * 128:q * 128 + 128], start=True, stop=False), waits=w_, inc=None)
            t_y = op("tensor", lambda e, q=q, p_=p_, yv=yv: e.matmul(yv[:, q * 128:q * 128 + 128], sb["sbf"][:, p_ * 128:p_ * 128 + 128], sb["c4"][:, q * 128:q * 128 + 128], start=False, stop=True), waits=[], inc=("sT" if q == 3 else None))
        stp = (ps["D"], ps["Ek"])[j % 2]
        smm = None
        for q in range(4):
            smm = op("tensor", lambda e, q=q, stp=stp: e.matmul(stp[:, q * 128:q * 128 + 128], sb["b4"][:, q * 128:q * 128 + 128], sb["xts"][:, q * 128:q * 128 + 128], start=True, stop=True), waits=(([vd, ssd4, vm4] + vb) if q == 0 else []), inc=("sT" if q == 3 else None))
        vt1 = []
        for q in range(4):
            p_ = 4 * j + q
            lh = 4 * q + j
            vt1.append(op("vector", lambda e, q=q, p_=p_, lh=lh, yv=yv: e.scalar_tensor_tensor(
                sb["t1b"][:, q * 128:q * 128 + 128], xfm(lh), sb["dcols"][:, p_:p_ + 1],
                yv[:, q * 128:q * 128 + 128], ALU.mult, ALU.add), waits=([t_y] if q == 0 else []), inc="sV"))
        st["yprev"][j % 2] = vt1[-1]
        h32_ap = sb["h32c"][:].rearrange("p (lh t) -> p lh t", lh=16)[:, j::4, :]
        vh = op("vector", lambda e, h32_ap=h32_ap: e.tensor_tensor(h32_ap, sb["t1b"][:].rearrange("p (r t) -> p r t", r=4), h32_ap, ALU.mult),
                waits=vt1 + [t_sil], inc="sV")
        h32_toks.append(vh)
        vch = []
        srn, srn2 = st["srun"], st["srun2"]
        for q in range(4):
            p_ = 4 * j + q
            vch.append(op("vector", lambda e, q=q, p_=p_, stp=stp, srn=srn, srn2=srn2: e.scalar_tensor_tensor(
                sb[srn2][:, p_ * 128:p_ * 128 + 128], sb[srn][:, p_ * 128:p_ * 128 + 128],
                sb["spc"][:, 16 + p_:16 + p_ + 1], stp[:, q * 128:q * 128 + 128], ALU.mult, ALU.add),
                waits=([smm, v5, st["last_chain"]] if q == 0 else []), inc="sV"))
        vsb = op("vector", lambda e, j=j, srn2=srn2: e.tensor_copy(sb["sbf"][:, 4 * j * 128:(4 * j + 4) * 128], sb[srn2][:, 4 * j * 128:(4 * j + 4) * 128]), waits=[vch[-1], t_y], inc="sV")
        if j == 3:
            st["last_chain"] = vch[-1]
            st["last_sbf"] = vsb
            st["srun"], st["srun2"] = st["srun2"], st["srun"]
    vsq = op("vector", lambda e: e.tensor_tensor(sb["ctmp1"][:], sb["h32c"][:], sb["h32c"][:], ALU.mult), waits=h32_toks + [silu_toks[-1]], inc="sV")
    st["last_vsq"] = vsq
    st["last_vh"] = h32_toks[-1]
    gmm = None
    for lh in range(16):
        gg = lh // 4
        gmm = op("tensor", lambda e, lh=lh, gg=gg: e.matmul(ps["Ek"][0:1, gg * 128:gg * 128 + 128], sb["ones_cb"][:], sb["ctmp1"][:, lh * 128:lh * 128 + 128], start=(lh % 4 == 0), stop=(lh % 4 == 3)), waits=([vsq, st["last_chain"]] if lh == 0 else []), inc=("sT" if lh == 15 else None))
    sgn1 = op("scalar", lambda e: e.activation(sb["rows"][0:1, 0:512], ps["Ek"][0:1, 0:512], AF.Ln, bias=sb["kcol"][0:1, 2:3], scale=1.0 / 512), waits=[gmm], inc="sS")
    sgn2 = op("scalar", lambda e: e.activation(sb["rowbf"][0:1, 0:512], sb["rows"][0:1, 0:512], AF.Exp, scale=-0.5), waits=[sgn1], inc="sS")
    st["last_gn"] = sgn2
    bcg = op("tensor", lambda e: e.matmul(ps["C"][:, 0:512], sb["ones_rb"][:], sb["rowbf"][0:1, 0:512], start=True, stop=True), waits=[sgn2], inc="sT")
    slot = st["chunkpar"]
    gap = []
    for gg in range(4):
        rep = ps["C"][:, gg * 128:gg * 128 + 128].unsqueeze(1).broadcast_to([128, 4, 128])
        gap.append(op("vector", lambda e, gg=gg, slot=slot, rep=rep: e.tensor_tensor(
            sb["h32n2"][:].rearrange("p (lh u) -> p lh u", lh=16)[:, gg * 4:(gg + 1) * 4, slot * 128:slot * 128 + 128],
            sb["h32c"][:, gg * 512:(gg + 1) * 512].rearrange("p (r t) -> p r t", r=4), rep, ALU.mult),
            waits=([bcg, st["h32dma"]] if gg == 0 else []), inc="sV"))
    st["h32toks"].append(gap[-1])
    st["last_y"] = gap[-1]
    if gchunk % 2 == 1:
        half = gchunk // 16
        basech = (gchunk % 16) - 1
        wv = st["h32toks"][-2:]
        cw[half].append(op("sync", lambda e, half=half, basech=basech: e.dma_start(
            out=d[f"cc2i{half}"][:].rearrange("p (lh t) -> p lh t", lh=16)[:, :, basech * 128:(basech + 2) * 128],
            in_=sb["h32n2"][:].rearrange("p (lh u) -> p lh u", lh=16)),
            waits=wv, inc="dO", dma=True))
        st["h32dma"] = cw[half][-1]
    st["chunkpar"] ^= 1



def _jax_cache():
    try:
        import jax
        jax.config.update("jax_compilation_cache_dir", "/tmp/jaxcache_kernel")
        jax.config.update("jax_persistent_cache_min_compile_time_secs", 1.0)
        jax.config.update("jax_persistent_cache_min_entry_size_bytes", 0)
    except Exception:
        pass


def make_inputs(core, inp):
    b, lr = core // 2, core % 2
    f32 = np.float32
    x = np.asarray(inp["L0_attn_x"][b], f32)
    wip = np.asarray(inp["L0_attn_in_proj_w"], f32)
    wop = np.asarray(inp["L0_attn_out_proj_w"], f32)
    w8 = np.asarray(inp["L0_ffn_weight"]).view(ml_dtypes.float8_e4m3fn)
    GH = _heads(lr)
    GB = [4 * lr + j for j in range(4)]
    my_e = np.arange(lr * 1024, lr * 1024 + 1024)
    loc_e = np.concatenate([my_e, np.arange((1 - lr) * 1024, (1 - lr) * 1024 + 1024)])

    m = {}
    m["xh"] = np.ascontiguousarray(x[:, loc_e].T.reshape(16, 128, S).transpose(1, 0, 2).reshape(128, 16 * S))
    hs_rows = np.concatenate([np.arange(I + g * 128, I + g * 128 + 128) for g in GH])
    b_rows = np.concatenate([np.arange(2 * I + g * 128, 2 * I + g * 128 + 128) for g in GB])
    c_rows = np.concatenate([np.arange(2 * I + G * N + g * 128, 2 * I + G * N + g * 128 + 128) for g in GB])
    gate_rows = np.concatenate([np.arange(g * 128, g * 128 + 128) for g in GH])
    Wsel = np.zeros((NOC * 128, E), f32)
    Wsel[0:2048] = wip[hs_rows]
    Wsel[2048:2560] = wip[b_rows]
    Wsel[2560:3072] = wip[c_rows]
    for p in range(16):
        Wsel[3072 + p] = wip[I + CD + GH[_lh_of_part(p)]]
    Wsel[3200:5248] = wip[gate_rows]
    m["winh"] = np.ascontiguousarray(Wsel.reshape(NOC, 128, 16, 128).transpose(0, 2, 3, 1))
    perm_i = np.concatenate([np.arange(g * 128, g * 128 + 128) for g in (_heads(0) + _heads(1))])
    wo = wop[my_e][:, perm_i]
    m["wouth"] = np.ascontiguousarray(wo.T.reshape(32, 128, 8, 128))
    wf1 = w8[my_e][:, loc_e]
    wf2 = w8[my_e, :]

    def ffn_tiles(wf):
        t = np.ascontiguousarray(wf.T).reshape(16, 128, 8, 128)
        return np.ascontiguousarray(t.transpose(1, 0, 2, 3).reshape(128, 16 * 1024))
    m["wf1h"] = ffn_tiles(wf1)
    m["wf2h"] = ffn_tiles(wf2)
    cwt = np.asarray(inp["L0_attn_conv_w"], f32)
    cbt = np.asarray(inp["L0_attn_conv_b"], f32)
    ch_sel = np.concatenate([np.concatenate([np.arange(g * 128, g * 128 + 128) for g in GH]),
                             np.concatenate([np.arange(I + g * 128, I + g * 128 + 128) for g in GB]),
                             np.concatenate([np.arange(I + G * N + g * 128, I + G * N + g * 128 + 128) for g in GB])])
    m["convw"] = np.ascontiguousarray(cwt[ch_sel, 0, :].reshape(24, 128, 4).transpose(1, 0, 2).reshape(128, 96))
    m["convbh"] = np.ascontiguousarray(cbt[ch_sel].reshape(24, 128).T)
    perm16 = [GH[_lh_of_part(p)] for p in range(16)]
    m["dtb_h"] = np.asarray(inp["L0_attn_dt_bias"], f32)[perm16].reshape(16, 1)
    m["alog_h"] = np.asarray(inp["L0_attn_A_log"], f32)[perm16].reshape(16, 1)
    m["d_h"] = np.asarray(inp["L0_attn_D"], f32)[perm16].reshape(1, 16)
    m["nrmw_h"] = np.ascontiguousarray(np.asarray(inp["L0_attn_norm_w"], f32)[perm_i].reshape(32, 128).T)
    m["isc_h"] = np.asarray(inp["L0_ffn_input_scale"]).reshape(1, 1)
    m["wsc_h"] = np.asarray(inp["L0_ffn_weight_scale"], f32).reshape(1, 1)
    m["identh"] = np.eye(128, dtype=ml_dtypes.bfloat16)
    m["identf_h"] = np.eye(128, dtype=f32)
    tuv = np.triu(np.ones((128, 128), f32))
    m["tuh"] = tuv
    m["maskh"] = np.where(tuv > 0, 0.0, NEG).astype(f32)
    m["ones_r_f"] = np.ones((1, 128), f32)
    m["ones_r_b"] = np.ones((1, 128), ml_dtypes.bfloat16)
    m["ones_c_b"] = np.ones((128, 1), ml_dtypes.bfloat16)
    return m


def kernel(**inputs):
    _jax_cache()
    nc = build_program()
    in_maps = [make_inputs(core, inputs) for core in range(8)]
    res = run_bass_kernel_spmd(nc, in_maps, core_ids=list(range(8)))
    kernel._last = res
    out = np.zeros((BATCH, S, E), np.float32)
    for core in range(8):
        b, lr = core // 2, core % 2
        o = res.results[core]["outh"].reshape(128, 8, S)
        out[b, :, lr * 1024:(lr + 1) * 1024] = o.transpose(2, 1, 0).reshape(S, 1024)
    return out


# version 07 (final)
# speedup vs baseline: 1.0485x; 1.0485x over previous
from contextlib import ExitStack

import numpy as np
import ml_dtypes

import concourse.bass as bass
import concourse.mybir as mybir
from concourse.bass_utils import run_bass_kernel_spmd

F32 = mybir.dt.float32
BF16 = mybir.dt.bfloat16
FP8 = mybir.dt.float8e4
AF = mybir.ActivationFunctionType
ALU = mybir.AluOpType

H, P, N, G, KK, E = 32, 128, 128, 8, 4, 2048
I = H * P
CD = I + 2 * G * N
S = 4096
BATCH = 4
EPS_GN = 1e-6
EPS_LN = 1e-5
NEG = -1e30

T1 = 512
T2 = 512
T3 = 512
NT1 = S // T1
NT2 = S // T2
NT3 = S // T3
NCH = T2 // 128
NOC = 41

DBG = {"phases": 99, "dump": (), "no_cc": False, "sweep_tiles": NT2 if False else None}

ENGS = ("tensor", "vector", "scalar", "gpsimd", "sync")
SEMS = ("dK", "dG", "dW", "dA0", "dA1", "dX", "dO", "dP", "sT", "sV", "sS", "sG", "cc")


def _heads(lr):
    return [8 * a + j + 4 * lr for a in range(4) for j in range(4)]


def _lh_of_part(p):
    return (p % 4) * 4 + p // 4


class Tok:
    __slots__ = ("sem", "val")

    def __init__(self, sem, val):
        self.sem, self.val = sem, val


class Ctx:
    pass


def build_program():
    nc = bass.Bass("TRN2", target_bir_lowering=False)
    c = Ctx()
    c.nc = nc
    dram = {}

    def din(name, shape, dt):
        dram[name] = nc.dram_tensor(name, shape, dt, kind="ExternalInput").ap()
        return dram[name]

    def dint(name, shape, dt):
        dram[name] = nc.dram_tensor(name, shape, dt, kind="Internal").ap()
        return dram[name]

    din("xh", [128, 16 * S], F32)
    din("winh", [NOC, 16, 128, 128], F32)
    din("wouth", [32, 128, 8, 128], F32)
    din("wf1h", [128, 16 * 1024], FP8)
    din("wf2h", [128, 16 * 1024], FP8)
    din("convw", [128, 24 * 4], F32)
    din("convbh", [128, 24], F32)
    din("dtb_h", [16, 1], F32)
    din("alog_h", [16, 1], F32)
    din("d_h", [1, 16], F32)
    din("nrmw_h", [128, 32], F32)
    din("isc_h", [1, 1], BF16)
    din("wsc_h", [1, 1], F32)
    din("identh", [128, 128], BF16)
    din("identf_h", [128, 128], F32)
    din("tuh", [128, 128], F32)
    din("maskh", [128, 128], F32)
    din("ones_r_f", [1, 128], F32)
    din("ones_r_b", [1, 128], BF16)
    din("ones_c_b", [128, 1], BF16)
    dram["outh"] = nc.dram_tensor("outh", [128, 8 * S], F32, kind="ExternalOutput").ap()

    dint("win_bf", [NOC, 128, 16, 128], BF16)
    dint("wout_bf", [128, 8, 32, 128], BF16)
    dint("xnorm_d", [128, 16 * S], BF16)
    for k in range(4):
        dint(f"cc1i{k}", [128, 8 * (S // 4)], BF16)
        dint(f"cc1o{k}", [2, 128, 8 * (S // 4)], BF16)
    for k in range(2):
        dint(f"cc2i{k}", [128, 16 * (S // 2)], BF16)
        dint(f"cc2o{k}", [2, 128, 16 * (S // 2)], BF16)
        dint(f"cc3i{k}", [128, 8 * (S // 2)], BF16)
        dint(f"cc3o{k}", [2, 128, 8 * (S // 2)], BF16)
    c.dram = dram

    dumps = {}
    for nm in DBG["dump"]:
        shp, dt = {"x1me": ([128, 8 * S], BF16), "xn2": ([128, 16 * S], BF16),
                   "convout": ([128, 24 * (4 + T2)], BF16), "dtsp": ([16, T2], F32),
                   "h32n": ([128, 16 * S], BF16), "sg": ([128, 16 * T2], BF16),
                   "x2me": ([128, 8 * S], BF16), "xq1": ([128, 16 * T1], FP8),
                   "srun": ([128, 16 * 128], F32), "convpre": ([128, 24 * (4 + T2)], BF16),
                   "spc": ([128, 48], F32), "sc4": ([128, 512], BF16), "sd4": ([128, 512], BF16),
                   "c4": ([128, 512], BF16), "b4": ([128, 512], BF16), "xts": ([128, 512], BF16),
                   "bts": ([128, 128], BF16), "l4": ([128, 512], BF16), "d4": ([128, 512], F32),
                   "h32c": ([128, 2048], BF16), "t1b": ([128, 512], F32), "sbf": ([128, 2048], BF16),
                   "tmb": ([128, 64], F32), "cumafm": ([16, T2], F32)}[nm]
        dumps[nm] = nc.dram_tensor("dump_" + nm, shp, dt, kind="ExternalOutput").ap()
    c.dumps = dumps

    counts = {s: 0 for s in SEMS}
    streams = {k: [] for k in ENGS}
    c.counts = counts

    def op(eng, emit, waits=(), inc=None, dma=False):
        tok = None
        if inc is not None:
            counts[inc] += (16 if dma else 1)
            tok = Tok(inc, counts[inc])
        streams[eng].append(([w for w in waits if w is not None], emit, inc, 16 if dma else 1))
        return tok

    c.op = op

    def barrier():
        toks = [Tok(s, counts[s]) for s in SEMS]
        for eng in ENGS:
            op(eng, None, waits=toks)

    c.barrier = barrier

    main_stack = ExitStack()
    sems = {}
    for snm in SEMS:
        sems[snm] = main_stack.enter_context(nc.semaphore(snm))
    c.sems = sems

    sb = {}
    c.sb = sb

    def alloc(stk, name, shape, dt):
        sb[name] = stk.enter_context(nc.sbuf_tensor(name, shape, dt))
        return sb[name]

    c.alloc = alloc

    for nm, shp, dt in [("ident", [128, 128], BF16), ("identf", [128, 128], F32),
                        ("tu", [128, 128], F32), ("maskneg", [128, 128], F32),
                        ("ones_rf", [1, 128], F32), ("ones_rb", [1, 128], BF16),
                        ("ones_cb", [128, 1], BF16), ("convw_b", [128, 96], F32),
                        ("convb_f", [128, 24], F32), ("dtb", [16, 1], F32),
                        ("aneg", [16, 1], F32), ("dcols", [128, 16], F32),
                        ("swscol", [128, 1], F32), ("sc_f", [1, 4], F32),
                        ("nrmw", [128, 32], F32), ("rows", [1, 2048], F32),
                        ("rowbf", [1, 1024], BF16), ("kcol", [16, 4], F32)]:
        alloc(main_stack, nm, shp, dt)

    psum = {}
    psum["A"] = main_stack.enter_context(nc.psum_tensor("psA", [128, 1024], F32))
    psum["Bk"] = main_stack.enter_context(nc.psum_tensor("psB", [128, 1024], F32))
    psum["C"] = main_stack.enter_context(nc.psum_tensor("psC", [128, 512], F32))
    psum["D"] = main_stack.enter_context(nc.psum_tensor("psD", [128, 512], F32))
    psum["Ek"] = main_stack.enter_context(nc.psum_tensor("psE", [128, 512], F32))
    psum["T"] = main_stack.enter_context(nc.psum_tensor("psT", [128, 1024], BF16))
    c.psum = psum

    emit_phases(c)

    OWN = {"tensor": ("sT",), "vector": ("sV",), "scalar": ("sS",), "gpsimd": ("sG",), "sync": ()}
    with nc.Block() as block:
        def mk(eng_name):
            own = OWN[eng_name]

            def run(eng):
                waited = {}
                done_own = {s: 0 for s in own}
                for waits, emit, inc, step in streams[eng_name]:
                    for tok in waits:
                        if waited.get(tok.sem, -1) >= tok.val:
                            continue
                        if DBG.get("prune_own") and tok.sem in done_own and done_own[tok.sem] >= tok.val:
                            continue
                        eng.wait_ge(sems[tok.sem], tok.val)
                        waited[tok.sem] = tok.val
                    if emit is None:
                        continue
                    inst = emit(eng)
                    if inc is not None:
                        inst.then_inc(sems[inc], step)
                        if inc in done_own:
                            done_own[inc] += step
            return run
        block.tensor(mk("tensor"))
        block.vector(mk("vector"))
        block.scalar(mk("scalar"))
        block.gpsimd(mk("gpsimd"))
        block.sync(mk("sync"))

    main_stack.close()
    return nc


def emit_phases(c):
    op, sb, ps, d = c.op, c.sb, c.psum, c.dram

    tcs = []
    for nm_s, nm_d in [("identh", "ident"), ("identf_h", "identf"), ("tuh", "tu"),
                       ("maskh", "maskneg"), ("ones_r_f", "ones_rf"),
                       ("ones_r_b", "ones_rb"), ("ones_c_b", "ones_cb"),
                       ("convbh", "convb_f"), ("dtb_h", "dtb"), ("nrmw_h", "nrmw")]:
        tcs.append(op("sync", lambda e, a=d[nm_s], b=sb[nm_d]: e.dma_start(out=b[:], in_=a[:]), inc="dK", dma=True))
    t_convw = op("sync", lambda e: e.dma_start(out=sb["convw_b"][:], in_=d["convw"][:]), inc="dK", dma=True)
    t_sc1 = op("sync", lambda e: e.dma_start(out=sb["rowbf"][0:1, 0:1], in_=d["isc_h"][:]), inc="dK", dma=True)
    t_sc2 = op("sync", lambda e: e.dma_start(out=sb["sc_f"][0:1, 1:2], in_=d["wsc_h"][:]), inc="dK", dma=True)
    t_dr = op("sync", lambda e: e.dma_start(out=sb["rows"][0:1, 0:16], in_=d["d_h"][:]), inc="dK", dma=True)
    t_al = op("sync", lambda e: e.dma_start(out=sb["aneg"][:], in_=d["alog_h"][:]), inc="dK", dma=True)
    kall0 = Tok("dK", c.counts["dK"])
    t_s1 = op("scalar", lambda e: e.copy(sb["sc_f"][0:1, 0:1], sb["rowbf"][0:1, 0:1]), waits=[kall0], inc="sS")
    t_s2 = op("vector", lambda e: e.tensor_tensor(sb["sc_f"][0:1, 2:3], sb["sc_f"][0:1, 0:1], sb["sc_f"][0:1, 1:2], ALU.mult), waits=[t_s1, t_sc2], inc="sV")
    t_s3 = op("vector", lambda e: e.tensor_scalar(sb["sc_f"][0:1, 2:3], sb["sc_f"][0:1, 2:3], 0.5, None, ALU.mult, ALU.bypass), waits=[t_s2], inc="sV")
    t_s4 = op("vector", lambda e: e.reciprocal(sb["sc_f"][0:1, 3:4], sb["sc_f"][0:1, 0:1]), waits=[t_s2], inc="sV")
    t_a1 = op("scalar", lambda e: e.activation(sb["aneg"][:], sb["aneg"][:], AF.Exp), waits=[kall0], inc="sS")
    t_a2 = op("vector", lambda e: e.tensor_scalar(sb["aneg"][:], sb["aneg"][:], -1.0, None, ALU.mult, ALU.bypass), waits=[t_a1], inc="sV")
    t_db = op("tensor", lambda e: e.matmul(ps["C"][:, 0:16], sb["ones_rf"][:], sb["rows"][0:1, 0:16], start=True, stop=True), waits=[kall0], inc="sT")
    t_db2 = op("vector", lambda e: e.tensor_copy(sb["dcols"][:], ps["C"][:, 0:16]), waits=[t_db], inc="sV")
    t_sw = op("tensor", lambda e: e.matmul(ps["C"][:, 16:17], sb["ones_rf"][:], sb["sc_f"][0:1, 2:3], start=True, stop=True), waits=[t_s3, t_db2], inc="sT")
    t_sw2 = op("vector", lambda e: e.tensor_copy(sb["swscol"][:], ps["C"][:, 16:17]), waits=[t_sw], inc="sV")
    kall = Tok("dK", c.counts["dK"])
    c.kall = kall
    t_k1 = op("vector", lambda e: e.memset(sb["kcol"][:, 0:1], EPS_LN), inc="sV")
    t_k2 = op("vector", lambda e: e.memset(sb["kcol"][:, 1:2], 1.0), inc="sV")
    t_k3 = op("vector", lambda e: e.memset(sb["kcol"][:, 2:3], EPS_GN), inc="sV")
    c.kcol_tok = t_k3
    setup_toks = [kall, t_a2, t_db2, t_sw2, t_k3]

    st1 = ExitStack()
    c.alloc(st1, "stg32", [128, 2 * 1024], F32)
    c.alloc(st1, "stgbf", [128, 2 * 1024], BF16)
    c.alloc(st1, "xt", [128, 16 * T1], F32)
    c.alloc(st1, "xtB", [128, 16 * T1], F32)
    c.alloc(st1, "xbf", [128, 16 * T1], BF16)
    c.alloc(st1, "xbfB", [128, 16 * T1], BF16)
    c.alloc(st1, "sqb", [128, 16 * T1], BF16)
    c.alloc(st1, "xq", [128, 16 * T1], FP8)
    c.alloc(st1, "xqB", [128, 16 * T1], FP8)
    c.alloc(st1, "x1me", [128, 8 * T1], BF16)
    c.alloc(st1, "x1meB", [128, 8 * T1], BF16)
    c.alloc(st1, "wffn", [128, 16 * 1024], FP8)
    c.alloc(st1, "bcsb", [128, 1024], BF16)

    t_ffw = op("sync", lambda e: e.dma_start(out=sb["wffn"][:], in_=d["wf1h"][:]), inc="dK", dma=True)
    t_ffw = Tok("dK", c.counts["dK"])

    ccw = {0: [], 1: [], 2: [], 3: []}
    p1_last_ln = None
    prev_ln1 = None
    ev_hist1 = {}
    tw_hist1 = {}
    psu1 = {}
    for tau in range(NT1):
        sx = ("", "B")[tau % 2]
        xtn, bfn, xqn, x1n = "xt" + sx, "xbf" + sx, "xq" + sx, "x1me" + sx
        tx = op("sync", lambda e, tau=tau, xtn=xtn: e.dma_start(
            out=sb[xtn][:].rearrange("p (c s) -> p c s", c=16),
            in_=d["xh"][:].rearrange("p (ch s) -> p ch s", ch=16)[:, :, tau * T1:(tau + 1) * T1]),
            waits=[ev_hist1.get(tau - 2)], inc="dX", dma=True)
        ln = emit_ln(c, src=(xtn, F32), Tt=T1, quant=True, waits=[tx, t_s4],
                     bufs=dict(bf=bfn, sq="sqb", t1="sqb", out=bfn, xq=xqn), prev_ln=prev_ln1,
                     g1_eng="vector")
        prev_ln1 = ln["last"]
        p1_last_ln = ln["last"]
        ev_toks = []
        for oc in range(8):
            pst = (ps["A"], ps["Bk"])[oc % 2]
            half = (oc // 2) % 2
            pv = pst[:, half * 512:half * 512 + 512]
            key = (oc % 2, half)
            w0 = ([ev_toks[oc - 2]] if oc >= 2 else [psu1.get(key)]) + [ln["xq"], t_ffw]
            w0 = [x for x in w0 if x is not None]
            mm = None
            for ic in range(16):
                mm = op("tensor", lambda e, oc=oc, ic=ic, pv=pv, xqn=xqn: e.matmul(
                    pv, sb["wffn"][:, ic * 1024 + oc * 128: ic * 1024 + oc * 128 + 128],
                    sb[xqn][:, ic * T1:(ic + 1) * T1], start=(ic == 0), stop=(ic == 15)),
                    waits=(w0 if ic == 0 else []), inc=("sT" if ic == 15 else None))
            evw = [mm, t_sw2] + ([tw_hist1.get(tau - 2)] if (oc == 0 and (tau - 2) in tw_hist1) else [])
            ev = op("vector", lambda e, oc=oc, pv=pv, x1n=x1n, xtn=xtn: e.scalar_tensor_tensor(
                sb[x1n][:, oc * T1:(oc + 1) * T1], pv, sb["swscol"][:, 0:1],
                sb[xtn][:, oc * T1:(oc + 1) * T1], ALU.mult, ALU.add),
                waits=evw, inc="sV")
            ev_toks.append(ev)
            psu1[key] = ev
        qq = tau // 2
        base = (tau % 2) * T1
        tw = op("sync", lambda e, qq=qq, base=base, x1n=x1n: e.dma_start(
            out=d[f"cc1i{qq}"][:].rearrange("p (ch s) -> p ch s", ch=8)[:, :, base:base + T1],
            in_=sb[x1n][:].rearrange("p (c s) -> p c s", c=8)), waits=ev_toks, inc="dO", dma=True)
        ccw[qq].append(tw)
        ev_hist1[tau] = ev_toks[-1]
        tw_hist1[tau] = tw

    gate_tok = ccw[0][0]
    stg_in = []
    for oc in range(NOC):
        src = d["winh"][oc].rearrange("ic p q -> p ic q")
        dst = d["win_bf"][oc]
        stg_in.append(op("gpsimd", lambda e, s_=src, t_=dst: e.dma_start(out=t_, in_=s_), waits=[gate_tok], inc="dG", dma=True))
    c.stgin_last = stg_in[-1]

    def mk_ag1(qq):
        if DBG["no_cc"]:
            return op("gpsimd", lambda e: e.engine_nop(), waits=[ccw[qq][-1]], inc="cc")
        return op("gpsimd", lambda e, qq=qq: e.collective_compute(
            "AllGather", ALU.bypass, ins=[d[f"cc1i{qq}"][:]], outs=[d[f"cc1o{qq}"][:]],
            replica_groups=[[0, 1], [2, 3], [4, 5], [6, 7]]), waits=[ccw[qq][-1]], inc="cc")
    ag1 = [mk_ag1(0), None, None, None]

    stg_store = {0: None, 1: None}
    for ic in range(32):
        slp = ic % 2
        s32 = sb["stg32"][:, slp * 1024:slp * 1024 + 1024]
        sbfv = sb["stgbf"][:, slp * 1024:slp * 1024 + 1024]
        tl = op("sync", lambda e, ic=ic, s32=s32: e.dma_start(out=s32, in_=d["wouth"][ic].rearrange("p ec q -> p (ec q)")), waits=[stg_store[slp], ag1[0]], inc="dW", dma=True)
        tv = op("vector", lambda e, ic=ic, s32=s32, sbfv=sbfv: e.tensor_scalar(sbfv, s32, sb["nrmw"][:, ic:ic + 1], None, ALU.mult, ALU.bypass), waits=[tl, kall], inc="sV")
        stg_store[slp] = op("sync", lambda e, ic=ic, sbfv=sbfv: e.dma_start(out=d["wout_bf"][:, :, ic, :], in_=sbfv.rearrange("p (ec q) -> p ec q", ec=8)), waits=[tv], inc="dW", dma=True)
    c.stgout_last = stg_store[1]

    if DBG["phases"] < 2:
        for qq in range(1, 4):
            ag1[qq] = mk_ag1(qq)
        fin(c, ag1[3])
        st1.close()
        return

    store_hist = {}
    prev_ln = p1_last_ln
    for tau in range(NT1):
        qq = tau // 2
        if tau >= 2 and tau % 2 == 0:
            ag1[qq] = mk_ag1(qq)
        base = (tau % 2) * T1
        slp = tau % 2
        bfn = ("xbf", "xbfB")[slp]
        outn = bfn
        w = [ag1[qq], store_hist.get(tau - 2) if tau >= 2 else p1_last_ln]
        tl = []
        for sslot in range(2):
            tl.append(op("sync", lambda e, qq=qq, base=base, sslot=sslot, bfn=bfn: e.dma_start(
                out=sb[bfn][:, sslot * 8 * T1:(sslot + 1) * 8 * T1].rearrange("p (c s) -> p c s", c=8),
                in_=d[f"cc1o{qq}"][sslot].rearrange("p (ch t) -> p ch t", ch=8)[:, :, base:base + T1]),
                waits=w, inc="dX", dma=True))
        ln = emit_ln(c, src=(bfn, BF16), Tt=T1, quant=False, waits=tl,
                     bufs=dict(bf=bfn, sq="sqb", t1="sqb", out=outn), prev_ln=prev_ln)
        prev_ln = ln["last"]
        store_hist[tau] = op("sync", lambda e, tau=tau, outn=outn: e.dma_start(
            out=d["xnorm_d"][:].rearrange("p (ch s) -> p ch s", ch=16)[:, :, tau * T1:(tau + 1) * T1],
            in_=sb[outn][:].rearrange("p (c s) -> p c s", c=16)), waits=[ln["xn"]], inc="dO", dma=True)
    t_p1b = store_hist[NT1 - 1]
    if "xn2" in c.dumps:
        op("sync", lambda e: e.dma_start(out=c.dumps["xn2"][:], in_=d["xnorm_d"][:]), waits=[t_p1b], inc="dX", dma=True)
    st1.close()
    if DBG["phases"] < 3:
        fin(c, t_p1b)
        return

    c.barrier()
    st2 = ExitStack()
    c.alloc(st2, "xn", [128, 16 * T2], BF16)
    c.alloc(st2, "wbuf", [128, 3 * 2048], BF16)
    c.alloc(st2, "convio", [128, 2 * 24 * (4 + T2)], BF16)
    c.alloc(st2, "carry", [128, 24 * 4], BF16)
    c.alloc(st2, "ctmp1", [128, 2048], BF16)
    c.alloc(st2, "ctmp2", [128, 2 * T2], BF16)
    c.alloc(st2, "sg", [128, 2 * 16 * T2], BF16)
    c.alloc(st2, "dtraw", [16, 2 * T2], F32)
    c.alloc(st2, "dtsp", [16, T2], F32)
    c.alloc(st2, "afm", [16, T2], F32)
    c.alloc(st2, "cumafm", [16, T2], F32)
    c.alloc(st2, "cuma_p0", [1, 2048], F32)
    c.alloc(st2, "tmb", [128, 64], F32)
    c.alloc(st2, "spc", [128, 48], F32)
    c.alloc(st2, "srun", [128, 16 * 128], F32)
    c.alloc(st2, "srun2", [128, 16 * 128], F32)
    c.alloc(st2, "sbf", [128, 16 * 128], BF16)
    c.alloc(st2, "xts", [128, 4 * 128], BF16)
    c.alloc(st2, "bts", [128, 128], BF16)
    c.alloc(st2, "b4", [128, 4 * 128], BF16)
    c.alloc(st2, "sc4", [128, 4 * 128], BF16)
    c.alloc(st2, "d4", [128, 4 * 128], F32)
    c.alloc(st2, "l4", [128, 4 * 128], BF16)
    c.alloc(st2, "sd4", [128, 4 * 128], BF16)
    c.alloc(st2, "c4", [128, 4 * 128], BF16)
    c.alloc(st2, "t1b", [128, 4 * 128], F32)
    c.alloc(st2, "h32c", [128, 16 * 128], BF16)
    c.alloc(st2, "h32n2", [128, 2 * 16 * 128], BF16)

    def mk_ag2(half):
        return op("gpsimd", lambda e, half=half: e.collective_compute(
            "AllGather", ALU.bypass, ins=[d[f"cc2i{half}"][:]], outs=[d[f"cc2o{half}"][:]],
            replica_groups=[[0, 1], [2, 3], [4, 5], [6, 7]]), waits=[cw2[half][-1]], inc="cc")
    t_done, cw2 = emit_sweep(c, t_p1b, setup_toks)
    ag2 = [mk_ag2(0), None]
    st2.close()
    if DBG["phases"] < 4:
        ag2[1] = mk_ag2(1)
        fin(c, ag2[1])
        return

    c.barrier()
    st3 = ExitStack()
    c.alloc(st3, "rhs3", [128, 2 * 32 * T3], BF16)
    c.alloc(st3, "wo", [128, 2 * 32 * 128], BF16)
    c.alloc(st3, "x2me", [128, 2 * 8 * T3], BF16)
    c.alloc(st3, "x1r", [128, 2 * 8 * T3], BF16)

    last3 = {0: [], 1: []}
    tw_hist3 = {}
    ev_hist3 = {}
    wo_prev = {}
    for sl in range(NT3):
        half = sl // (NT3 // 2)
        if sl == NT3 // 2:
            ag2[1] = mk_ag2(1)
        base = (sl % (NT3 // 2)) * T3
        rbase = (sl % 2) * 32 * T3
        xbase = (sl % 2) * 8 * T3
        w = [ag2[half], ev_hist3.get(sl - 2)]
        trh = []
        for sslot in range(2):
            trh.append(op("sync", lambda e, half=half, base=base, sslot=sslot, rbase=rbase: e.dma_start(
                out=sb["rhs3"][:, rbase + sslot * 16 * T3:rbase + (sslot + 1) * 16 * T3].rearrange("p (c s) -> p c s", c=16),
                in_=d[f"cc2o{half}"][sslot].rearrange("p (ch t) -> p ch t", ch=16)[:, :, base:base + T3]),
                waits=w, inc="dX", dma=True))
        q13 = sl // 2
        b13 = (sl % 2) * T3
        tx1 = op("sync", lambda e, q13=q13, b13=b13, xbase=xbase: e.dma_start(
            out=sb["x1r"][:, xbase:xbase + 8 * T3].rearrange("p (c s) -> p c s", c=8),
            in_=d[f"cc1i{q13}"][:].rearrange("p (ch t) -> p ch t", ch=8)[:, :, b13:b13 + T3]),
            waits=w, inc="dX", dma=True)
        ev_toks = []
        for ec in range(8):
            wws = [ev_toks[ec - 2]] if ec >= 2 else [wo_prev.get(ec % 2)]
            wws = [x for x in wws if x is not None]
            two = op("scalar", lambda e, ec=ec: e.dma_start(
                out=sb["wo"][:, (ec % 2) * 4096:(ec % 2) * 4096 + 4096],
                in_=d["wout_bf"][:, ec].rearrange("p ic q -> p (ic q)")),
                waits=wws + [c.stgout_last], inc=("dA0", "dA1")[ec % 2], dma=True)
            pv = (ps["A"], ps["Bk"])[ec % 2][:, 0:512]
            mm = None
            for ic in range(32):
                mm = op("tensor", lambda e, ec=ec, ic=ic, pv=pv, rbase=rbase: e.matmul(
                    pv, sb["wo"][:, (ec % 2) * 4096 + ic * 128:(ec % 2) * 4096 + ic * 128 + 128],
                    sb["rhs3"][:, rbase + ic * T3:rbase + (ic + 1) * T3], start=(ic == 0), stop=(ic == 31)),
                    waits=(([two, tx1] + trh + wws) if ic == 0 else []),
                    inc=("sT" if ic == 31 else None))
            evw = [mm, tx1] + ([tw_hist3.get(sl - 2)] if (ec == 0 and (sl - 2) in tw_hist3) else [])
            ev = op("vector", lambda e, ec=ec, pv=pv, xbase=xbase: e.tensor_tensor(
                sb["x2me"][:, xbase + ec * T3:xbase + (ec + 1) * T3], pv,
                sb["x1r"][:, xbase + ec * T3:xbase + (ec + 1) * T3], ALU.add),
                waits=evw, inc="sV")
            ev_toks.append(ev)
            wo_prev[ec % 2] = ev
        tw = op("sync", lambda e, half=half, base=base, xbase=xbase: e.dma_start(
            out=d[f"cc3i{half}"][:].rearrange("p (ch t) -> p ch t", ch=8)[:, :, base:base + T3],
            in_=sb["x2me"][:, xbase:xbase + 8 * T3].rearrange("p (c s) -> p c s", c=8)), waits=ev_toks, inc="dO", dma=True)
        last3[half].append(tw)
        tw_hist3[sl] = tw
        ev_hist3[sl] = ev_toks[-1]
    def mk_ag3(half):
        return op("gpsimd", lambda e, half=half: e.collective_compute(
            "AllGather", ALU.bypass, ins=[d[f"cc3i{half}"][:]], outs=[d[f"cc3o{half}"][:]],
            replica_groups=[[0, 1], [2, 3], [4, 5], [6, 7]]), waits=[last3[half][-1]], inc="cc")
    ag3 = [mk_ag3(0), None]
    st3.close()
    if DBG["phases"] < 5:
        ag3[1] = mk_ag3(1)
        fin(c, ag3[1])
        return

    c.barrier()
    st4 = ExitStack()
    for slp in ("", "B"):
        c.alloc(st4, "xbf4" + slp, [128, 16 * T1], BF16)
        c.alloc(st4, "xq4" + slp, [128, 16 * T1], FP8)
        c.alloc(st4, "x2r" + slp, [128, 8 * T1], BF16)
        c.alloc(st4, "outst" + slp, [128, 8 * T1], F32)
        c.alloc(st4, "xnb4" + slp, [128, 16 * T1], BF16)
    c.alloc(st4, "sqb4", [128, 16 * T1], BF16)
    c.alloc(st4, "wffn2", [128, 16 * 1024], FP8)
    c.alloc(st4, "bcsb4", [128, 1024], BF16)

    t_ffw2 = op("sync", lambda e: e.dma_start(out=sb["wffn2"][:], in_=d["wf2h"][:]), inc="dK", dma=True)
    lastout_hist = {}
    psu = {}
    prev_ln = None
    last_out = None
    for tau in range(NT1):
        half = tau // (NT1 // 2)
        if tau == NT1 // 2:
            ag3[1] = mk_ag3(1)
        base = (tau % (NT1 // 2)) * T1
        sx = ("", "B")[tau % 2]
        bfn, xqn, x2rn, outsn, xnn = "xbf4" + sx, "xq4" + sx, "x2r" + sx, "outst" + sx, "xnb4" + sx
        w = [ag3[half], lastout_hist.get(tau - 2)]
        tl = []
        for sslot in range(2):
            tl.append(op("sync", lambda e, half=half, base=base, sslot=sslot, bfn=bfn: e.dma_start(
                out=sb[bfn][:, sslot * 8 * T1:(sslot + 1) * 8 * T1].rearrange("p (c s) -> p c s", c=8),
                in_=d[f"cc3o{half}"][sslot].rearrange("p (ch t) -> p ch t", ch=8)[:, :, base:base + T1]),
                waits=w, inc="dX", dma=True))
        tx2 = op("sync", lambda e, half=half, base=base, x2rn=x2rn: e.dma_start(
            out=sb[x2rn][:].rearrange("p (c s) -> p c s", c=8),
            in_=d[f"cc3i{half}"][:].rearrange("p (ch t) -> p ch t", ch=8)[:, :, base:base + T1]),
            waits=w, inc="dX", dma=True)
        ln = emit_ln(c, src=(bfn, BF16), Tt=T1, quant=True, waits=tl + [t_s4],
                     bufs=dict(bf=bfn, sq="sqb4", t1="sqb4", out=xnn, xq=xqn, bc="bcsb4"),
                     prev_ln=prev_ln)
        prev_ln = ln["last"]
        ev_toks = []
        for oc in range(8):
            pst = (ps["A"], ps["Bk"])[oc % 2]
            half2 = (oc // 2) % 2
            pv = pst[:, half2 * 512:half2 * 512 + 512]
            key = (oc % 2, half2)
            w0 = ([ev_toks[oc - 2]] if oc >= 2 else [psu.get(key)]) + [ln["xq"], t_ffw2]
            w0 = [x for x in w0 if x is not None]
            mm = None
            for ic in range(16):
                mm = op("tensor", lambda e, oc=oc, ic=ic, pv=pv, xqn=xqn: e.matmul(
                    pv, sb["wffn2"][:, ic * 1024 + oc * 128: ic * 1024 + oc * 128 + 128],
                    sb[xqn][:, ic * T1:(ic + 1) * T1], start=(ic == 0), stop=(ic == 15)),
                    waits=(w0 if ic == 0 else []), inc=("sT" if ic == 15 else None))
            evw = [mm, tx2] + ([lastout_hist.get(tau - 2)] if (oc == 0 and (tau - 2) in lastout_hist) else [])
            ev = op("vector", lambda e, oc=oc, pv=pv, outsn=outsn, x2rn=x2rn: e.scalar_tensor_tensor(
                sb[outsn][:, oc * T1:(oc + 1) * T1], pv, sb["swscol"][:, 0:1],
                sb[x2rn][:, oc * T1:(oc + 1) * T1], ALU.mult, ALU.add),
                waits=evw, inc="sV")
            ev_toks.append(ev)
            psu[key] = ev
        last_out = op("sync", lambda e, tau=tau, outsn=outsn: e.dma_start(
            out=d["outh"][:].rearrange("p (ch s) -> p ch s", ch=8)[:, :, tau * T1:(tau + 1) * T1],
            in_=sb[outsn][:].rearrange("p (c s) -> p c s", c=8)), waits=ev_toks, inc="dO", dma=True)
        lastout_hist[tau] = last_out
    st4.close()
    fin(c, last_out)


def fin(c, tok):
    c.op("gpsimd", lambda e: e.engine_nop(), waits=[tok], inc="sG")


def emit_ln(c, src, Tt, quant, waits, bufs, prev_ln=None, g1_eng="gpsimd"):
    op, sb, ps = c.op, c.sb, c.psum
    nm, dt = src
    bf, sq, t1n, outn = bufs["bf"], bufs["sq"], bufs["t1"], bufs["out"]
    bcn = bufs.get("bc", "bcsb")
    w0 = list(waits) + ([prev_ln] if prev_ln is not None else [])
    if dt == F32:
        tb = op("vector", lambda e: e.tensor_copy(sb[bf][:], sb[nm][:]), waits=w0, inc="sV")
        bfn = bf
    else:
        tb = op("vector", lambda e: e.engine_nop(), waits=w0, inc="sV")
        bfn = nm
    tq = op("vector", lambda e: e.tensor_tensor(sb[sq][:], sb[bfn][:], sb[bfn][:], ALU.mult), waits=[tb], inc="sV")
    m1 = m2 = None
    for ic in range(16):
        m1 = op("tensor", lambda e, ic=ic: e.matmul(ps["C"][0:1, 0:Tt], sb["ones_cb"][:], sb[bfn][:, ic * Tt:(ic + 1) * Tt], start=(ic == 0), stop=(ic == 15)), waits=([tb] if ic == 0 else []), inc=("sT" if ic == 15 else None))
    for ic in range(16):
        m2 = op("tensor", lambda e, ic=ic: e.matmul(ps["D"][0:1, 0:Tt], sb["ones_cb"][:], sb[sq][:, ic * Tt:(ic + 1) * Tt], start=(ic == 0), stop=(ic == 15)), waits=([tq] if ic == 0 else []), inc=("sT" if ic == 15 else None))
    r = sb["rows"]
    t1 = op("vector", lambda e: e.tensor_scalar(r[0:1, 0:Tt], ps["C"][0:1, 0:Tt], 1.0 / 2048, None, ALU.mult, ALU.bypass), waits=[m1], inc="sV")
    t2 = op("vector", lambda e: e.tensor_scalar(r[0:1, 512:512 + Tt], ps["D"][0:1, 0:Tt], 1.0 / 2048, None, ALU.mult, ALU.bypass), waits=[m2], inc="sV")
    t3 = op("vector", lambda e: e.tensor_tensor(r[0:1, 1536:1536 + Tt], r[0:1, 0:Tt], r[0:1, 0:Tt], ALU.mult), waits=[t1], inc="sV")
    t4 = op("vector", lambda e: e.tensor_tensor(r[0:1, 1024:1024 + Tt], r[0:1, 512:512 + Tt], r[0:1, 1536:1536 + Tt], ALU.subtract), waits=[t2, t3], inc="sV")
    t5 = op("scalar", lambda e: e.activation(r[0:1, 1536:1536 + Tt], r[0:1, 1024:1024 + Tt], AF.Ln, bias=sb["kcol"][0:1, 0:1]), waits=[t4, c.kcol_tok], inc="sS")
    t6 = op("scalar", lambda e: e.activation(r[0:1, 1024:1024 + Tt], r[0:1, 1536:1536 + Tt], AF.Exp, scale=-0.5), waits=[t5], inc="sS")
    if quant:
        t7 = op("vector", lambda e: e.tensor_scalar(sb["rowbf"][0:1, 0:Tt], r[0:1, 1024:1024 + Tt], sb["sc_f"][0:1, 3:4], None, ALU.mult, ALU.bypass), waits=[t6], inc="sV")
    else:
        t7 = op("vector", lambda e: e.tensor_copy(sb["rowbf"][0:1, 0:Tt], r[0:1, 1024:1024 + Tt]), waits=[t6], inc="sV")
    t8 = op("vector", lambda e: e.tensor_tensor(sb["rowbf"][0:1, 512:512 + Tt], r[0:1, 0:Tt], sb["rowbf"][0:1, 0:Tt], ALU.mult), waits=[t7], inc="sV")
    b1 = op("tensor", lambda e: e.matmul(ps["C"][:, 0:512], sb["ones_rb"][:], sb["rowbf"][0:1, 0:512], start=True, stop=True), waits=[t8], inc="sT")
    b2 = op("tensor", lambda e: e.matmul(ps["D"][:, 0:512], sb["ones_rb"][:], sb["rowbf"][0:1, 512:1024], start=True, stop=True), waits=[t8], inc="sT")
    b3 = op("vector", lambda e: e.tensor_copy(sb[bcn][:, 0:512], ps["C"][:, 0:512]), waits=[b1], inc="sV")
    b3b = op("vector", lambda e: e.tensor_copy(sb[bcn][:, 512:1024], ps["D"][:, 0:512]), waits=[b2], inc="sV")
    rep_rs = sb[bcn][:, 0:Tt].unsqueeze(1).broadcast_to([128, 16, Tt])
    rep_mu = sb[bcn][:, 512:512 + Tt].unsqueeze(1).broadcast_to([128, 16, Tt])
    g1 = op(g1_eng, lambda e: e.tensor_tensor(sb[t1n][:].rearrange("p (c t) -> p c t", c=16), sb[bfn][:].rearrange("p (c t) -> p c t", c=16), rep_rs, ALU.mult), waits=[b3, b3b, tq], inc=("sG" if g1_eng == "gpsimd" else "sV"))
    v1 = op("vector", lambda e: e.tensor_tensor(sb[outn][:].rearrange("p (c t) -> p c t", c=16), sb[t1n][:].rearrange("p (c t) -> p c t", c=16), rep_mu, ALU.subtract), waits=[g1], inc="sV")
    res = {"xn": v1, "last": v1}
    if quant:
        res["xq"] = op("vector", lambda e: e.tensor_scalar(sb[bufs["xq"]][:], sb[outn][:], 448.0, -448.0, ALU.min, ALU.max), waits=[v1], inc="sV")
        res["last"] = res["xq"]
    return res


def emit_sweep(c, t_xn_ready, setup_toks):
    op, sb, ps, d = c.op, c.sb, c.psum, c.dram
    cw = {0: [], 1: []}
    t_sruninit = op("vector", lambda e: e.memset(sb["srun"][:], 0.0), inc="sV")
    t_sbfinit = op("vector", lambda e: e.memset(sb["sbf"][:], 0.0), inc="sV")
    t_carinit = op("vector", lambda e: e.memset(sb["carry"][:], 0.0), inc="sV")
    st = {"srun": "srun", "srun2": "srun2", "last_chain": t_sruninit, "last_sbf": t_sbfinit,
          "last_y": None, "h32toks": [], "chunkpar": 0, "prev_ex2": None, "last_gn": None,
          "last_carry": t_carinit, "h32dma": None, "yprev": {}, "last_vh": None,
          "last_vsq": None, "convio_free": {}}
    psA_user = {}
    prev_w = {}
    info = {}
    GRPS = [list(range(0, 11)), list(range(11, 21)), list(range(21, 31)), list(range(31, 41))]
    NTT = DBG["sweep_tiles"] or NT2

    def emit_inproj_grp(tt, ocs):
        par = tt % 2
        cvb = par * 24 * (4 + T2)
        tb0 = par * T2
        sgb = par * 16 * T2
        inf = info[tt]
        free_tok = st["convio_free"].get(par)
        for oc in ocs:
            bslot = oc % 3
            twl = op("gpsimd", lambda e, oc=oc, bslot=bslot: e.dma_start(
                out=sb["wbuf"][:, bslot * 2048:bslot * 2048 + 2048],
                in_=d["win_bf"][oc].rearrange("p ic q -> p (ic q)")),
                waits=[prev_w.get(bslot), c.stgin_last], inc=("dA0", "dA1")[oc % 2], dma=True)
            sub = oc % 2
            pv = ps["A"][:, sub * 512:sub * 512 + 512]
            mm = None
            for ic in range(16):
                mm = op("tensor", lambda e, oc=oc, ic=ic, pv=pv, bslot=bslot: e.matmul(
                    pv, sb["wbuf"][:, bslot * 2048 + ic * 128:bslot * 2048 + ic * 128 + 128],
                    sb["xn"][:, ic * T2:ic * T2 + T2], start=(ic == 0), stop=(ic == 15)),
                    waits=([twl, inf["txn"], psA_user.get(sub)] if ic == 0 else []),
                    inc=("sT" if ic == 15 else None))
            prev_w[bslot] = mm
            if oc < 24:
                gv = op("vector", lambda e, oc=oc, pv=pv, cvb=cvb: e.tensor_copy(
                    sb["convio"][:, cvb + oc * (4 + T2) + 4: cvb + oc * (4 + T2) + 4 + T2], pv),
                    waits=[mm, free_tok], inc="sV")
            elif oc == 24:
                gv = op("vector", lambda e, pv=pv, tb0=tb0: e.tensor_copy(
                    sb["dtraw"][:, tb0:tb0 + T2], pv[0:16, :]), waits=[mm, free_tok], inc="sV")
            else:
                lh = oc - 25
                gv = op("vector", lambda e, lh=lh, pv=pv, sgb=sgb: e.tensor_copy(
                    sb["sg"][:, sgb + lh * T2:sgb + (lh + 1) * T2], pv), waits=[mm, free_tok], inc="sV")
            psA_user[sub] = gv
            inf["gv"].append(gv)

    def emit_post(tt):
        par = tt % 2
        cvb = par * 24 * (4 + T2)
        tb0 = par * T2
        inf = info[tt]
        gv24 = inf["gv"][24]
        s1 = op("scalar", lambda e, tb0=tb0: e.activation(sb["afm"][:, 0:T2], sb["dtraw"][:, tb0:tb0 + T2], AF.Relu, bias=sb["dtb"][:, 0:1]), waits=[gv24] + setup_toks, inc="sS")
        s2 = op("scalar", lambda e, tb0=tb0: e.activation(sb["dtsp"][:, 0:T2], sb["dtraw"][:, tb0:tb0 + T2], AF.Abs, bias=sb["dtb"][:, 0:1]), waits=[gv24], inc="sS")
        s3 = op("scalar", lambda e: e.activation(sb["dtsp"][:, 0:T2], sb["dtsp"][:, 0:T2], AF.Exp, scale=-1.0), waits=[s2], inc="sS")
        s4 = op("scalar", lambda e: e.activation(sb["dtsp"][:, 0:T2], sb["dtsp"][:, 0:T2], AF.Ln, bias=sb["kcol"][0:16, 1:2]), waits=[s3], inc="sS")
        v5 = op("vector", lambda e: e.tensor_tensor(sb["dtsp"][:, 0:T2], sb["afm"][:, 0:T2], sb["dtsp"][:, 0:T2], ALU.add), waits=[s4, s1], inc="sV")
        v5 = op("vector", lambda e: e.tensor_scalar(sb["dtsp"][:, 0:T2], sb["dtsp"][:, 0:T2], 1e-19, None, ALU.max, ALU.bypass), waits=[v5], inc="sV")
        v6 = op("vector", lambda e: e.tensor_scalar(sb["afm"][:, 0:T2], sb["dtsp"][:, 0:T2], sb["aneg"][:, 0:1], None, ALU.mult, ALU.bypass), waits=[v5], inc="sV")
        inf["tdt"] = v6
        cvv = sb["convio"][:, cvb:cvb + 24 * (4 + T2)].rearrange("p (m t) -> p m t", m=24)
        t_car = op("vector", lambda e, cvv=cvv: e.tensor_copy(cvv[:, :, 0:4], sb["carry"][:].rearrange("p (m t) -> p m t", m=24)),
                   waits=[st["last_carry"]] + [inf["gv"][m] for m in range(24)] + setup_toks, inc="sV")
        silu_toks = []
        vc_toks = []
        for m in range(24):
            cpar = m % 2
            acc1 = sb["ctmp1"][:, cpar * T2:(cpar + 1) * T2]
            acc2 = sb["ctmp2"][:, cpar * T2:(cpar + 1) * T2]

            def um(k, m=m, cvb=cvb):
                base = cvb + m * (4 + T2) + 1 + k
                return sb["convio"][:, base:base + T2]

            def wc(k, m=m):
                return sb["convw_b"][:, m * 4 + k:m * 4 + k + 1]

            wv = [t_car] + ([silu_toks[m - 2]] if m >= 2 else [st["last_vsq"]])
            a1 = op("vector", lambda e, m=m, acc1=acc1: e.tensor_scalar(acc1, um(0, m), wc(0, m), None, ALU.mult, ALU.bypass), waits=wv, inc="sV")
            a2 = op("vector", lambda e, m=m, acc1=acc1, acc2=acc2: e.scalar_tensor_tensor(acc2, um(1, m), wc(1, m), acc1, ALU.mult, ALU.add), waits=[a1], inc="sV")
            a3 = op("vector", lambda e, m=m, acc1=acc1, acc2=acc2: e.scalar_tensor_tensor(acc1, um(2, m), wc(2, m), acc2, ALU.mult, ALU.add), waits=[a2], inc="sV")
            a4 = op("vector", lambda e, m=m, acc1=acc1, acc2=acc2: e.scalar_tensor_tensor(acc2, um(3, m), wc(3, m), acc1, ALU.mult, ALU.add), waits=[a3], inc="sV")
            vc = op("vector", lambda e, m=m, cvb=cvb: e.tensor_copy(
                sb["carry"][:, m * 4:m * 4 + 4],
                sb["convio"][:, cvb + m * (4 + T2) + T2: cvb + m * (4 + T2) + T2 + 4]), waits=[a4], inc="sV")
            vc_toks.append(vc)
            silu_toks.append(op("scalar", lambda e, m=m, acc2=acc2, cvb=cvb: e.activation(
                sb["convio"][:, cvb + m * (4 + T2) + 4: cvb + m * (4 + T2) + 4 + T2],
                acc2, AF.Silu, bias=sb["convb_f"][:, m:m + 1]),
                waits=[vc], inc="sS"))
        st["last_carry"] = vc_toks[-1]
        inf["silu"] = silu_toks

    for tt in range(NTT):
        info[tt] = {"gv": []}
        txn = op("sync", lambda e, tt=tt: e.dma_start(
            out=sb["xn"][:].rearrange("p (c s) -> p c s", c=16),
            in_=d["xnorm_d"][:].rearrange("p (ch s) -> p ch s", ch=16)[:, :, tt * T2:(tt + 1) * T2]),
            waits=[t_xn_ready, (info[tt - 1]["gv"][-1] if tt >= 1 else None)], inc="dX", dma=True)
        info[tt]["txn"] = txn
        for grp in range(4):
            emit_inproj_grp(tt, GRPS[grp])
            if tt >= 1:
                ssd_chunk(c, tt - 1, grp, info[tt - 1]["silu"], info[tt - 1]["tdt"], info[tt - 1]["gv"][-1], st, cw)
        if tt >= 1:
            st["convio_free"][(tt - 1) % 2] = st["last_vh"]
        emit_post(tt)
    for grp in range(4):
        ssd_chunk(c, NTT - 1, grp, info[NTT - 1]["silu"], info[NTT - 1]["tdt"], info[NTT - 1]["gv"][-1], st, cw)
    return st["last_y"], cw


def ssd_chunk(c, u, cc_, silu_toks, tdt, gate_tok, st, cw):
    op, sb, ps, d = c.op, c.sb, c.psum, c.dram
    gchunk = u * NCH + cc_
    t0 = cc_ * 128
    par = u % 2
    cvb = par * 24 * (4 + T2)
    sgb = par * 16 * T2
    cv = sb["convio"]

    def xfm(lh):
        return cv[:, cvb + lh * (4 + T2) + 4 + t0: cvb + lh * (4 + T2) + 4 + t0 + 128]

    def bfm(j):
        return cv[:, cvb + (16 + j) * (4 + T2) + 4 + t0: cvb + (16 + j) * (4 + T2) + 4 + t0 + 128]

    def cfm(j):
        return cv[:, cvb + (20 + j) * (4 + T2) + 4 + t0: cvb + (20 + j) * (4 + T2) + 4 + t0 + 128]

    t_sil = op("scalar", lambda e: e.activation(
        sb["h32c"][:].rearrange("p (lh t) -> p lh t", lh=16),
        sb["sg"][:, sgb:sgb + 16 * T2].rearrange("p (lh t) -> p lh t", lh=16)[:, :, t0:t0 + 128],
        AF.Silu), waits=[gate_tok, st["last_vsq"], st["last_y"]], inc="sS")

    w0 = [tdt, st["last_chain"], st["last_gn"]]
    ta = op("tensor", lambda e, t0=t0: e.transpose(ps["D"][:, 0:16], sb["afm"][:, t0:t0 + 128], sb["identf"][0:16, 0:16]), waits=w0, inc="sT")
    tb = op("tensor", lambda e, t0=t0: e.transpose(ps["D"][:, 16:32], sb["dtsp"][:, t0:t0 + 128], sb["identf"][0:16, 0:16]), waits=[], inc="sT")
    tv = op("vector", lambda e: e.tensor_copy(sb["tmb"][:, 0:32], ps["D"][:, 0:32]), waits=[ta, tb], inc="sV")
    tc = op("tensor", lambda e: e.matmul(ps["Ek"][:, 0:16], sb["tu"][:], sb["tmb"][:, 0:16], start=True, stop=True), waits=[tv, st["last_gn"]], inc="sT")
    tv2 = op("vector", lambda e: e.tensor_copy(sb["tmb"][:, 32:48], ps["Ek"][:, 0:16]), waits=[tc], inc="sV")
    td = op("tensor", lambda e: e.transpose(ps["D"][0:16, 32:160], sb["tmb"][:, 32:48], sb["identf"][:]), waits=[tv2], inc="sT")
    tv3 = op("vector", lambda e, t0=t0: e.tensor_copy(sb["cumafm"][:, t0:t0 + 128], ps["D"][0:16, 32:160]), waits=[td], inc="sV")
    tp0 = op("sync", lambda e, t0=t0: e.dma_start(out=sb["cuma_p0"][0:1, :].rearrange("o (h t) -> o h t", h=16), in_=sb["cumafm"][:, t0:t0 + 128]), waits=[tv3], inc="dP", dma=True)
    s1 = op("scalar", lambda e: e.activation(sb["spc"][:, 0:16], sb["tmb"][:, 16:32], AF.Ln), waits=[tv, st["last_gn"]], inc="sS")
    v4 = op("vector", lambda e: e.tensor_tensor(sb["spc"][:, 0:16], sb["spc"][:, 0:16], sb["tmb"][:, 32:48], ALU.subtract), waits=[s1, tv2], inc="sV")
    lam_rhs = sb["cuma_p0"][0:1, :].rearrange("o (h t) -> o h t", h=16)[:, :, 127:128].rearrange("o h t -> o (h t)")
    tl = op("tensor", lambda e: e.matmul(ps["Ek"][:, 16:32], sb["ones_rf"][:], lam_rhs, start=True, stop=True), waits=[tp0, tv2], inc="sT")
    v5 = op("scalar", lambda e: e.activation(sb["spc"][:, 16:32], ps["Ek"][:, 16:32], AF.Exp), waits=[tl], inc="sS")
    v6 = op("vector", lambda e: e.tensor_tensor(sb["spc"][:, 32:48], ps["Ek"][:, 16:32], sb["tmb"][:, 32:48], ALU.subtract), waits=[tl], inc="sV")
    s3 = op("scalar", lambda e: e.activation(sb["spc"][:, 32:48], sb["spc"][:, 32:48], AF.Exp), waits=[v6], inc="sS")
    v7 = op("vector", lambda e: e.tensor_tensor(sb["spc"][:, 32:48], sb["spc"][:, 32:48], sb["tmb"][:, 16:32], ALU.mult), waits=[s3], inc="sV")

    h32_toks = []
    for j in range(4):
        wsc = (silu_toks if gchunk % NCH == 0 else []) if j == 0 else []
        tsc = op("tensor", lambda e, j=j: e.matmul(ps["C"][:, j * 128:j * 128 + 128], bfm(j), cfm(j), start=True, stop=True),
                 waits=wsc + ([st["last_y"]] if (j == 0 and st["last_y"]) else []), inc="sT")
        pd = (ps["D"], ps["Ek"])[j % 2]
        wdb = [tp0] + ([tv3] if j % 2 == 0 else [v6, v7])
        tdb = op("tensor", lambda e, j=j, pd=pd: e.matmul(pd[:], sb["ones_rf"][:], sb["cuma_p0"][0:1, 4 * j * 128:(4 * j + 4) * 128], start=True, stop=True), waits=wdb, inc="sT")
        ttx = []
        wtx = ([silu_toks[-1]] if (j == 0 and gchunk % NCH == 0) else []) + ([st["prev_ex2"]] if st["prev_ex2"] else [])
        for q in range(4):
            lh = 4 * q + j
            ttx.append(op("tensor", lambda e, q=q, lh=lh: e.transpose(ps["T"][:, q * 128:q * 128 + 128], xfm(lh), sb["ident"][:]), waits=(wtx if q == 0 else []), inc="sT"))
        ttb = op("tensor", lambda e, j=j: e.transpose(ps["T"][:, 512:640], bfm(j), sb["ident"][:]), waits=[], inc="sT")
        ex1 = op("vector", lambda e: e.tensor_copy(sb["xts"][:], ps["T"][:, 0:512]), waits=ttx + [ttb], inc="sV")
        ex2 = op("vector", lambda e: e.tensor_copy(sb["bts"][:], ps["T"][:, 512:640]), waits=[ex1], inc="sV")
        st["prev_ex2"] = ex2
        vb = []
        for q in range(4):
            p_ = 4 * j + q
            vb.append(op("vector", lambda e, q=q, p_=p_: e.tensor_scalar(sb["b4"][:, q * 128:q * 128 + 128], sb["bts"][:], sb["spc"][:, 32 + p_:32 + p_ + 1], None, ALU.mult, ALU.bypass), waits=([ex2, v7] if q == 0 else []), inc="sV"))
        vds = []
        for q in range(4):
            p_ = 4 * j + q
            vds.append(op("vector", lambda e, q=q, p_=p_, pd=pd: e.scalar_tensor_tensor(
                sb["d4"][:, q * 128:q * 128 + 128], pd[:, q * 128:q * 128 + 128], sb["spc"][:, p_:p_ + 1],
                sb["maskneg"][:], ALU.add, ALU.add), waits=([tdb, v4] if q == 0 else []), inc="sV"))
        vd = vds[-1]
        sl4 = op("scalar", lambda e: e.activation(sb["l4"][:], sb["d4"][:], AF.Exp), waits=[vd], inc="sS")
        ssd4 = op("scalar", lambda e, pd=pd: e.activation(sb["sd4"][:], pd[:], AF.Exp), waits=[vd], inc="sS")
        screp = ps["C"][:, j * 128:j * 128 + 128].unsqueeze(1).broadcast_to([128, 4, 128])
        vm4 = op("vector", lambda e, screp=screp: e.tensor_tensor(sb["sc4"][:].rearrange("p (r t) -> p r t", r=4), screp, sb["l4"][:].rearrange("p (r t) -> p r t", r=4), ALU.mult), waits=[sl4, tsc], inc="sV")
        crep = cfm(j).unsqueeze(1).broadcast_to([128, 4, 128])
        vc4 = op("vector", lambda e, j=j, crep=crep: e.tensor_tensor(sb["c4"][:].rearrange("p (r t) -> p r t", r=4), crep, sb["sd4"][:].rearrange("p (r t) -> p r t", r=4), ALU.mult), waits=[ssd4], inc="sV")
        yv = ps["Bk"][:, (j % 2) * 512:(j % 2) * 512 + 512]
        t_y = None
        for q in range(4):
            p_ = 4 * j + q
            w_ = ([vm4, vc4, st["last_sbf"], st["yprev"].get(j % 2)] if q == 0 else [])
            w_ = [x for x in w_ if x is not None]
            op("tensor", lambda e, q=q, yv=yv: e.matmul(yv[:, q * 128:q * 128 + 128], sb["xts"][:, q * 128:q * 128 + 128], sb["sc4"][:, q * 128:q * 128 + 128], start=True, stop=False), waits=w_, inc=None)
            t_y = op("tensor", lambda e, q=q, p_=p_, yv=yv: e.matmul(yv[:, q * 128:q * 128 + 128], sb["sbf"][:, p_ * 128:p_ * 128 + 128], sb["c4"][:, q * 128:q * 128 + 128], start=False, stop=True), waits=[], inc=("sT" if q == 3 else None))
        stp = (ps["D"], ps["Ek"])[j % 2]
        smm = None
        for q in range(4):
            smm = op("tensor", lambda e, q=q, stp=stp: e.matmul(stp[:, q * 128:q * 128 + 128], sb["b4"][:, q * 128:q * 128 + 128], sb["xts"][:, q * 128:q * 128 + 128], start=True, stop=True), waits=(([vd, ssd4, vm4] + vb) if q == 0 else []), inc=("sT" if q == 3 else None))
        vt1 = []
        for q in range(4):
            p_ = 4 * j + q
            lh = 4 * q + j
            vt1.append(op("vector", lambda e, q=q, p_=p_, lh=lh, yv=yv: e.scalar_tensor_tensor(
                sb["t1b"][:, q * 128:q * 128 + 128], xfm(lh), sb["dcols"][:, p_:p_ + 1],
                yv[:, q * 128:q * 128 + 128], ALU.mult, ALU.add), waits=([t_y] if q == 0 else []), inc="sV"))
        st["yprev"][j % 2] = vt1[-1]
        h32_ap = sb["h32c"][:].rearrange("p (lh t) -> p lh t", lh=16)[:, j::4, :]
        vh = op("vector", lambda e, h32_ap=h32_ap: e.tensor_tensor(h32_ap, sb["t1b"][:].rearrange("p (r t) -> p r t", r=4), h32_ap, ALU.mult),
                waits=vt1 + [t_sil], inc="sV")
        h32_toks.append(vh)
        vch = []
        srn, srn2 = st["srun"], st["srun2"]
        for q in range(4):
            p_ = 4 * j + q
            vch.append(op("vector", lambda e, q=q, p_=p_, stp=stp, srn=srn, srn2=srn2: e.scalar_tensor_tensor(
                sb[srn2][:, p_ * 128:p_ * 128 + 128], sb[srn][:, p_ * 128:p_ * 128 + 128],
                sb["spc"][:, 16 + p_:16 + p_ + 1], stp[:, q * 128:q * 128 + 128], ALU.mult, ALU.add),
                waits=([smm, v5, st["last_chain"]] if q == 0 else []), inc="sV"))
        vsb = op("vector", lambda e, j=j, srn2=srn2: e.tensor_copy(sb["sbf"][:, 4 * j * 128:(4 * j + 4) * 128], sb[srn2][:, 4 * j * 128:(4 * j + 4) * 128]), waits=[vch[-1], t_y], inc="sV")
        if j == 3:
            st["last_chain"] = vch[-1]
            st["last_sbf"] = vsb
            st["srun"], st["srun2"] = st["srun2"], st["srun"]
    vsq = op("vector", lambda e: e.tensor_tensor(sb["ctmp1"][:], sb["h32c"][:], sb["h32c"][:], ALU.mult), waits=h32_toks + [silu_toks[-1]], inc="sV")
    st["last_vsq"] = vsq
    st["last_vh"] = h32_toks[-1]
    gmm = None
    for lh in range(16):
        gg = lh // 4
        gmm = op("tensor", lambda e, lh=lh, gg=gg: e.matmul(ps["Ek"][0:1, gg * 128:gg * 128 + 128], sb["ones_cb"][:], sb["ctmp1"][:, lh * 128:lh * 128 + 128], start=(lh % 4 == 0), stop=(lh % 4 == 3)), waits=([vsq, st["last_chain"]] if lh == 0 else []), inc=("sT" if lh == 15 else None))
    sgn1 = op("scalar", lambda e: e.activation(sb["rows"][0:1, 0:512], ps["Ek"][0:1, 0:512], AF.Ln, bias=sb["kcol"][0:1, 2:3], scale=1.0 / 512), waits=[gmm], inc="sS")
    sgn2 = op("scalar", lambda e: e.activation(sb["rowbf"][0:1, 0:512], sb["rows"][0:1, 0:512], AF.Exp, scale=-0.5), waits=[sgn1], inc="sS")
    st["last_gn"] = sgn2
    bcg = op("tensor", lambda e: e.matmul(ps["C"][:, 0:512], sb["ones_rb"][:], sb["rowbf"][0:1, 0:512], start=True, stop=True), waits=[sgn2], inc="sT")
    slot = st["chunkpar"]
    gap = []
    for gg in range(4):
        rep = ps["C"][:, gg * 128:gg * 128 + 128].unsqueeze(1).broadcast_to([128, 4, 128])
        gap.append(op("vector", lambda e, gg=gg, slot=slot, rep=rep: e.tensor_tensor(
            sb["h32n2"][:].rearrange("p (lh u) -> p lh u", lh=16)[:, gg * 4:(gg + 1) * 4, slot * 128:slot * 128 + 128],
            sb["h32c"][:, gg * 512:(gg + 1) * 512].rearrange("p (r t) -> p r t", r=4), rep, ALU.mult),
            waits=([bcg, st["h32dma"]] if gg == 0 else []), inc="sV"))
    st["h32toks"].append(gap[-1])
    st["last_y"] = gap[-1]
    if gchunk % 2 == 1:
        half = gchunk // 16
        basech = (gchunk % 16) - 1
        wv = st["h32toks"][-2:]
        cw[half].append(op("sync", lambda e, half=half, basech=basech: e.dma_start(
            out=d[f"cc2i{half}"][:].rearrange("p (lh t) -> p lh t", lh=16)[:, :, basech * 128:(basech + 2) * 128],
            in_=sb["h32n2"][:].rearrange("p (lh u) -> p lh u", lh=16)),
            waits=wv, inc="dO", dma=True))
        st["h32dma"] = cw[half][-1]
    st["chunkpar"] ^= 1



def _jax_cache():
    try:
        import jax
        jax.config.update("jax_compilation_cache_dir", "/tmp/jaxcache_kernel")
        jax.config.update("jax_persistent_cache_min_compile_time_secs", 1.0)
        jax.config.update("jax_persistent_cache_min_entry_size_bytes", 0)
    except Exception:
        pass


def make_inputs(core, inp):
    b, lr = core // 2, core % 2
    f32 = np.float32
    x = np.asarray(inp["L0_attn_x"][b], f32)
    wip = np.asarray(inp["L0_attn_in_proj_w"], f32)
    wop = np.asarray(inp["L0_attn_out_proj_w"], f32)
    w8 = np.asarray(inp["L0_ffn_weight"]).view(ml_dtypes.float8_e4m3fn)
    GH = _heads(lr)
    GB = [4 * lr + j for j in range(4)]
    my_e = np.arange(lr * 1024, lr * 1024 + 1024)
    loc_e = np.concatenate([my_e, np.arange((1 - lr) * 1024, (1 - lr) * 1024 + 1024)])

    m = {}
    m["xh"] = np.ascontiguousarray(x[:, loc_e].T.reshape(16, 128, S).transpose(1, 0, 2).reshape(128, 16 * S))
    hs_rows = np.concatenate([np.arange(I + g * 128, I + g * 128 + 128) for g in GH])
    b_rows = np.concatenate([np.arange(2 * I + g * 128, 2 * I + g * 128 + 128) for g in GB])
    c_rows = np.concatenate([np.arange(2 * I + G * N + g * 128, 2 * I + G * N + g * 128 + 128) for g in GB])
    gate_rows = np.concatenate([np.arange(g * 128, g * 128 + 128) for g in GH])
    Wsel = np.zeros((NOC * 128, E), f32)
    Wsel[0:2048] = wip[hs_rows]
    Wsel[2048:2560] = wip[b_rows]
    Wsel[2560:3072] = wip[c_rows]
    for p in range(16):
        Wsel[3072 + p] = wip[I + CD + GH[_lh_of_part(p)]]
    Wsel[3200:5248] = wip[gate_rows]
    m["winh"] = np.ascontiguousarray(Wsel.reshape(NOC, 128, 16, 128).transpose(0, 2, 3, 1))
    perm_i = np.concatenate([np.arange(g * 128, g * 128 + 128) for g in (_heads(0) + _heads(1))])
    wo = wop[my_e][:, perm_i]
    m["wouth"] = np.ascontiguousarray(wo.T.reshape(32, 128, 8, 128))
    wf1 = w8[my_e][:, loc_e]
    wf2 = w8[my_e, :]

    def ffn_tiles(wf):
        t = np.ascontiguousarray(wf.T).reshape(16, 128, 8, 128)
        return np.ascontiguousarray(t.transpose(1, 0, 2, 3).reshape(128, 16 * 1024))
    m["wf1h"] = ffn_tiles(wf1)
    m["wf2h"] = ffn_tiles(wf2)
    cwt = np.asarray(inp["L0_attn_conv_w"], f32)
    cbt = np.asarray(inp["L0_attn_conv_b"], f32)
    ch_sel = np.concatenate([np.concatenate([np.arange(g * 128, g * 128 + 128) for g in GH]),
                             np.concatenate([np.arange(I + g * 128, I + g * 128 + 128) for g in GB]),
                             np.concatenate([np.arange(I + G * N + g * 128, I + G * N + g * 128 + 128) for g in GB])])
    m["convw"] = np.ascontiguousarray(cwt[ch_sel, 0, :].reshape(24, 128, 4).transpose(1, 0, 2).reshape(128, 96))
    m["convbh"] = np.ascontiguousarray(cbt[ch_sel].reshape(24, 128).T)
    perm16 = [GH[_lh_of_part(p)] for p in range(16)]
    m["dtb_h"] = np.asarray(inp["L0_attn_dt_bias"], f32)[perm16].reshape(16, 1)
    m["alog_h"] = np.asarray(inp["L0_attn_A_log"], f32)[perm16].reshape(16, 1)
    m["d_h"] = np.asarray(inp["L0_attn_D"], f32)[perm16].reshape(1, 16)
    m["nrmw_h"] = np.ascontiguousarray(np.asarray(inp["L0_attn_norm_w"], f32)[perm_i].reshape(32, 128).T)
    m["isc_h"] = np.asarray(inp["L0_ffn_input_scale"]).reshape(1, 1)
    m["wsc_h"] = np.asarray(inp["L0_ffn_weight_scale"], f32).reshape(1, 1)
    m["identh"] = np.eye(128, dtype=ml_dtypes.bfloat16)
    m["identf_h"] = np.eye(128, dtype=f32)
    tuv = np.triu(np.ones((128, 128), f32))
    m["tuh"] = tuv
    m["maskh"] = np.where(tuv > 0, 0.0, NEG).astype(f32)
    m["ones_r_f"] = np.ones((1, 128), f32)
    m["ones_r_b"] = np.ones((1, 128), ml_dtypes.bfloat16)
    m["ones_c_b"] = np.ones((128, 1), ml_dtypes.bfloat16)
    return m


def kernel(**inputs):
    _jax_cache()
    nc = build_program()
    in_maps = [make_inputs(core, inputs) for core in range(8)]
    res = run_bass_kernel_spmd(nc, in_maps, core_ids=list(range(8)))
    kernel._last = res
    out = np.zeros((BATCH, S, E), np.float32)
    for core in range(8):
        b, lr = core // 2, core % 2
        o = res.results[core]["outh"].reshape(128, 8, S)
        out[b, :, lr * 1024:(lr + 1) * 1024] = o.transpose(2, 1, 0).reshape(S, 1024)
    return out


# version 08
# speedup vs baseline: 1.0685x; 1.0191x over previous
from contextlib import ExitStack

import numpy as np
import ml_dtypes

import concourse.bass as bass
import concourse.mybir as mybir
from concourse.bass_utils import run_bass_kernel_spmd

F32 = mybir.dt.float32
BF16 = mybir.dt.bfloat16
FP8 = mybir.dt.float8e4
AF = mybir.ActivationFunctionType
ALU = mybir.AluOpType

H, P, N, G, KK, E = 32, 128, 128, 8, 4, 2048
I = H * P
CD = I + 2 * G * N
S = 4096
BATCH = 4
EPS_GN = 1e-6
EPS_LN = 1e-5
NEG = -1e30

T1 = 512
T2 = 512
T3 = 512
NT1 = S // T1
NT2 = S // T2
NT3 = S // T3
NCH = T2 // 128
NOC = 41

DBG = {"phases": 99, "dump": (), "no_cc": False, "sweep_tiles": NT2 if False else None}

ENGS = ("tensor", "vector", "scalar", "gpsimd", "sync")
SEMS = ("dK", "dG", "dW", "dA0", "dA1", "dX", "dO", "dP", "sT", "sV", "sS", "sG", "cc")


def _heads(lr):
    return [8 * a + j + 4 * lr for a in range(4) for j in range(4)]


def _lh_of_part(p):
    return (p % 4) * 4 + p // 4


class Tok:
    __slots__ = ("sem", "val")

    def __init__(self, sem, val):
        self.sem, self.val = sem, val


class Ctx:
    pass


def build_program():
    nc = bass.Bass("TRN2", target_bir_lowering=False)
    c = Ctx()
    c.nc = nc
    dram = {}

    def din(name, shape, dt):
        dram[name] = nc.dram_tensor(name, shape, dt, kind="ExternalInput").ap()
        return dram[name]

    def dint(name, shape, dt):
        dram[name] = nc.dram_tensor(name, shape, dt, kind="Internal").ap()
        return dram[name]

    din("xh", [128, 16 * S], F32)
    din("winh", [NOC, 16, 128, 128], F32)
    din("wouth", [32, 128, 8, 128], F32)
    din("wf1h", [128, 16 * 1024], FP8)
    din("wf2h", [128, 16 * 1024], FP8)
    din("convw", [128, 24 * 4], F32)
    din("convbh", [128, 24], F32)
    din("dtb_h", [16, 1], F32)
    din("alog_h", [16, 1], F32)
    din("d_h", [1, 16], F32)
    din("nrmw_h", [128, 32], F32)
    din("isc_h", [1, 1], BF16)
    din("wsc_h", [1, 1], F32)
    din("identh", [128, 128], BF16)
    din("identf_h", [128, 128], F32)
    din("tuh", [128, 128], F32)
    din("maskh", [128, 128], F32)
    din("ones_r_f", [1, 128], F32)
    din("ones_r_b", [1, 128], BF16)
    din("ones_c_b", [128, 1], BF16)
    dram["outh"] = nc.dram_tensor("outh", [128, 8 * S], F32, kind="ExternalOutput").ap()

    dint("win_bf", [NOC, 128, 16, 128], BF16)
    dint("wout_bf", [128, 8, 32, 128], BF16)
    dint("xnorm_d", [128, 16 * S], BF16)
    for k in range(4):
        dint(f"cc1i{k}", [128, 8 * (S // 4)], BF16)
        dint(f"cc1o{k}", [2, 128, 8 * (S // 4)], BF16)
        dint(f"cc2i{k}", [128, 16 * (S // 4)], BF16)
        dint(f"cc2o{k}", [2, 128, 16 * (S // 4)], BF16)
        dint(f"cc3i{k}", [128, 8 * (S // 4)], BF16)
        dint(f"cc3o{k}", [2, 128, 8 * (S // 4)], BF16)
    c.dram = dram

    dumps = {}
    for nm in DBG["dump"]:
        shp, dt = {"x1me": ([128, 8 * S], BF16), "xn2": ([128, 16 * S], BF16),
                   "convout": ([128, 24 * (4 + T2)], BF16), "dtsp": ([16, T2], F32),
                   "h32n": ([128, 16 * S], BF16), "sg": ([128, 16 * T2], BF16),
                   "x2me": ([128, 8 * S], BF16), "xq1": ([128, 16 * T1], FP8),
                   "srun": ([128, 16 * 128], F32), "convpre": ([128, 24 * (4 + T2)], BF16),
                   "spc": ([128, 48], F32), "sc4": ([128, 512], BF16), "sd4": ([128, 512], BF16),
                   "c4": ([128, 512], BF16), "b4": ([128, 512], BF16), "xts": ([128, 512], BF16),
                   "bts": ([128, 128], BF16), "l4": ([128, 512], BF16), "d4": ([128, 512], F32),
                   "h32c": ([128, 2048], BF16), "t1b": ([128, 512], F32), "sbf": ([128, 2048], BF16),
                   "tmb": ([128, 64], F32), "cumafm": ([16, T2], F32)}[nm]
        dumps[nm] = nc.dram_tensor("dump_" + nm, shp, dt, kind="ExternalOutput").ap()
    c.dumps = dumps

    counts = {s: 0 for s in SEMS}
    streams = {k: [] for k in ENGS}
    c.counts = counts

    def op(eng, emit, waits=(), inc=None, dma=False):
        tok = None
        if inc is not None:
            counts[inc] += (16 if dma else 1)
            tok = Tok(inc, counts[inc])
        streams[eng].append(([w for w in waits if w is not None], emit, inc, 16 if dma else 1))
        return tok

    c.op = op

    def barrier():
        toks = [Tok(s, counts[s]) for s in SEMS]
        for eng in ENGS:
            op(eng, None, waits=toks)

    c.barrier = barrier

    main_stack = ExitStack()
    sems = {}
    for snm in SEMS:
        sems[snm] = main_stack.enter_context(nc.semaphore(snm))
    c.sems = sems

    sb = {}
    c.sb = sb

    def alloc(stk, name, shape, dt):
        sb[name] = stk.enter_context(nc.sbuf_tensor(name, shape, dt))
        return sb[name]

    c.alloc = alloc

    for nm, shp, dt in [("ident", [128, 128], BF16), ("identf", [128, 128], F32),
                        ("tu", [128, 128], F32), ("maskneg", [128, 128], F32),
                        ("ones_rf", [1, 128], F32), ("ones_rb", [1, 128], BF16),
                        ("ones_cb", [128, 1], BF16), ("convw_b", [128, 96], F32),
                        ("convb_f", [128, 24], F32), ("dtb", [16, 1], F32),
                        ("aneg", [16, 1], F32), ("dcols", [128, 16], F32),
                        ("swscol", [128, 1], F32), ("sc_f", [1, 4], F32),
                        ("nrmw", [128, 32], F32), ("rows", [1, 2048], F32),
                        ("rowbf", [1, 1024], BF16), ("kcol", [16, 4], F32)]:
        alloc(main_stack, nm, shp, dt)

    psum = {}
    psum["A"] = main_stack.enter_context(nc.psum_tensor("psA", [128, 1024], F32))
    psum["Bk"] = main_stack.enter_context(nc.psum_tensor("psB", [128, 1024], F32))
    psum["C"] = main_stack.enter_context(nc.psum_tensor("psC", [128, 512], F32))
    psum["D"] = main_stack.enter_context(nc.psum_tensor("psD", [128, 512], F32))
    psum["Ek"] = main_stack.enter_context(nc.psum_tensor("psE", [128, 512], F32))
    psum["T"] = main_stack.enter_context(nc.psum_tensor("psT", [128, 1024], BF16))
    c.psum = psum

    emit_phases(c)

    OWN = {"tensor": ("sT",), "vector": ("sV",), "scalar": ("sS",), "gpsimd": ("sG",), "sync": ()}
    with nc.Block() as block:
        def mk(eng_name):
            own = OWN[eng_name]

            def run(eng):
                waited = {}
                done_own = {s: 0 for s in own}
                for waits, emit, inc, step in streams[eng_name]:
                    for tok in waits:
                        if waited.get(tok.sem, -1) >= tok.val:
                            continue
                        if DBG.get("prune_own") and tok.sem in done_own and done_own[tok.sem] >= tok.val:
                            continue
                        eng.wait_ge(sems[tok.sem], tok.val)
                        waited[tok.sem] = tok.val
                    if emit is None:
                        continue
                    inst = emit(eng)
                    if inc is not None:
                        inst.then_inc(sems[inc], step)
                        if inc in done_own:
                            done_own[inc] += step
            return run
        block.tensor(mk("tensor"))
        block.vector(mk("vector"))
        block.scalar(mk("scalar"))
        block.gpsimd(mk("gpsimd"))
        block.sync(mk("sync"))

    main_stack.close()
    return nc


def emit_phases(c):
    op, sb, ps, d = c.op, c.sb, c.psum, c.dram

    tcs = []
    for nm_s, nm_d in [("identh", "ident"), ("identf_h", "identf"), ("tuh", "tu"),
                       ("maskh", "maskneg"), ("ones_r_f", "ones_rf"),
                       ("ones_r_b", "ones_rb"), ("ones_c_b", "ones_cb"),
                       ("convbh", "convb_f"), ("dtb_h", "dtb"), ("nrmw_h", "nrmw")]:
        tcs.append(op("sync", lambda e, a=d[nm_s], b=sb[nm_d]: e.dma_start(out=b[:], in_=a[:]), inc="dK", dma=True))
    t_convw = op("sync", lambda e: e.dma_start(out=sb["convw_b"][:], in_=d["convw"][:]), inc="dK", dma=True)
    t_sc1 = op("sync", lambda e: e.dma_start(out=sb["rowbf"][0:1, 0:1], in_=d["isc_h"][:]), inc="dK", dma=True)
    t_sc2 = op("sync", lambda e: e.dma_start(out=sb["sc_f"][0:1, 1:2], in_=d["wsc_h"][:]), inc="dK", dma=True)
    t_dr = op("sync", lambda e: e.dma_start(out=sb["rows"][0:1, 0:16], in_=d["d_h"][:]), inc="dK", dma=True)
    t_al = op("sync", lambda e: e.dma_start(out=sb["aneg"][:], in_=d["alog_h"][:]), inc="dK", dma=True)
    kall0 = Tok("dK", c.counts["dK"])
    t_s1 = op("scalar", lambda e: e.copy(sb["sc_f"][0:1, 0:1], sb["rowbf"][0:1, 0:1]), waits=[kall0], inc="sS")
    t_s2 = op("vector", lambda e: e.tensor_tensor(sb["sc_f"][0:1, 2:3], sb["sc_f"][0:1, 0:1], sb["sc_f"][0:1, 1:2], ALU.mult), waits=[t_s1, t_sc2], inc="sV")
    t_s3 = op("vector", lambda e: e.tensor_scalar(sb["sc_f"][0:1, 2:3], sb["sc_f"][0:1, 2:3], 0.5, None, ALU.mult, ALU.bypass), waits=[t_s2], inc="sV")
    t_s4 = op("vector", lambda e: e.reciprocal(sb["sc_f"][0:1, 3:4], sb["sc_f"][0:1, 0:1]), waits=[t_s2], inc="sV")
    t_a1 = op("scalar", lambda e: e.activation(sb["aneg"][:], sb["aneg"][:], AF.Exp), waits=[kall0], inc="sS")
    t_a2 = op("vector", lambda e: e.tensor_scalar(sb["aneg"][:], sb["aneg"][:], -1.0, None, ALU.mult, ALU.bypass), waits=[t_a1], inc="sV")
    t_db = op("tensor", lambda e: e.matmul(ps["C"][:, 0:16], sb["ones_rf"][:], sb["rows"][0:1, 0:16], start=True, stop=True), waits=[kall0], inc="sT")
    t_db2 = op("vector", lambda e: e.tensor_copy(sb["dcols"][:], ps["C"][:, 0:16]), waits=[t_db], inc="sV")
    t_sw = op("tensor", lambda e: e.matmul(ps["C"][:, 16:17], sb["ones_rf"][:], sb["sc_f"][0:1, 2:3], start=True, stop=True), waits=[t_s3, t_db2], inc="sT")
    t_sw2 = op("vector", lambda e: e.tensor_copy(sb["swscol"][:], ps["C"][:, 16:17]), waits=[t_sw], inc="sV")
    kall = Tok("dK", c.counts["dK"])
    c.kall = kall
    t_k1 = op("vector", lambda e: e.memset(sb["kcol"][:, 0:1], EPS_LN), inc="sV")
    t_k2 = op("vector", lambda e: e.memset(sb["kcol"][:, 1:2], 1.0), inc="sV")
    t_k3 = op("vector", lambda e: e.memset(sb["kcol"][:, 2:3], EPS_GN), inc="sV")
    c.kcol_tok = t_k3
    setup_toks = [kall, t_a2, t_db2, t_sw2, t_k3]

    st1 = ExitStack()
    c.alloc(st1, "stg32", [128, 2 * 1024], F32)
    c.alloc(st1, "stgbf", [128, 2 * 1024], BF16)
    c.alloc(st1, "xt", [128, 16 * T1], F32)
    c.alloc(st1, "xtB", [128, 16 * T1], F32)
    c.alloc(st1, "xbf", [128, 16 * T1], BF16)
    c.alloc(st1, "xbfB", [128, 16 * T1], BF16)
    c.alloc(st1, "sqb", [128, 16 * T1], BF16)
    c.alloc(st1, "xq", [128, 16 * T1], FP8)
    c.alloc(st1, "xqB", [128, 16 * T1], FP8)
    c.alloc(st1, "x1me", [128, 8 * T1], BF16)
    c.alloc(st1, "x1meB", [128, 8 * T1], BF16)
    c.alloc(st1, "wffn", [128, 16 * 1024], FP8)
    c.alloc(st1, "bcsb", [128, 1024], BF16)

    t_ffw = op("sync", lambda e: e.dma_start(out=sb["wffn"][:], in_=d["wf1h"][:]), inc="dK", dma=True)
    t_ffw = Tok("dK", c.counts["dK"])

    ccw = {0: [], 1: [], 2: [], 3: []}
    p1_last_ln = None
    prev_ln1 = None
    ev_hist1 = {}
    tw_hist1 = {}
    psu1 = {}
    for tau in range(NT1):
        sx = ("", "B")[tau % 2]
        xtn, bfn, xqn, x1n = "xt" + sx, "xbf" + sx, "xq" + sx, "x1me" + sx
        tx = op("sync", lambda e, tau=tau, xtn=xtn: e.dma_start(
            out=sb[xtn][:].rearrange("p (c s) -> p c s", c=16),
            in_=d["xh"][:].rearrange("p (ch s) -> p ch s", ch=16)[:, :, tau * T1:(tau + 1) * T1]),
            waits=[ev_hist1.get(tau - 2)], inc="dX", dma=True)
        ln = emit_ln(c, src=(xtn, F32), Tt=T1, quant=True, waits=[tx, t_s4],
                     bufs=dict(bf=bfn, sq="sqb", t1="sqb", out=bfn, xq=xqn), prev_ln=prev_ln1,
                     g1_eng="vector")
        prev_ln1 = ln["last"]
        p1_last_ln = ln["last"]
        ev_toks = []
        for oc in range(8):
            pst = (ps["A"], ps["Bk"])[oc % 2]
            half = (oc // 2) % 2
            pv = pst[:, half * 512:half * 512 + 512]
            key = (oc % 2, half)
            w0 = ([ev_toks[oc - 2]] if oc >= 2 else [psu1.get(key)]) + [ln["xq"], t_ffw]
            w0 = [x for x in w0 if x is not None]
            mm = None
            for ic in range(16):
                mm = op("tensor", lambda e, oc=oc, ic=ic, pv=pv, xqn=xqn: e.matmul(
                    pv, sb["wffn"][:, ic * 1024 + oc * 128: ic * 1024 + oc * 128 + 128],
                    sb[xqn][:, ic * T1:(ic + 1) * T1], start=(ic == 0), stop=(ic == 15)),
                    waits=(w0 if ic == 0 else []), inc=("sT" if ic == 15 else None))
            evw = [mm, t_sw2] + ([tw_hist1.get(tau - 2)] if (oc == 0 and (tau - 2) in tw_hist1) else [])
            ev = op("vector", lambda e, oc=oc, pv=pv, x1n=x1n, xtn=xtn: e.scalar_tensor_tensor(
                sb[x1n][:, oc * T1:(oc + 1) * T1], pv, sb["swscol"][:, 0:1],
                sb[xtn][:, oc * T1:(oc + 1) * T1], ALU.mult, ALU.add),
                waits=evw, inc="sV")
            ev_toks.append(ev)
            psu1[key] = ev
        qq = tau // 2
        base = (tau % 2) * T1
        tw = op("sync", lambda e, qq=qq, base=base, x1n=x1n: e.dma_start(
            out=d[f"cc1i{qq}"][:].rearrange("p (ch s) -> p ch s", ch=8)[:, :, base:base + T1],
            in_=sb[x1n][:].rearrange("p (c s) -> p c s", c=8)), waits=ev_toks, inc="dO", dma=True)
        ccw[qq].append(tw)
        ev_hist1[tau] = ev_toks[-1]
        tw_hist1[tau] = tw

    gate_tok = ccw[0][0]
    stg_in = []
    for oc in range(NOC):
        src = d["winh"][oc].rearrange("ic p q -> p ic q")
        dst = d["win_bf"][oc]
        stg_in.append(op("gpsimd", lambda e, s_=src, t_=dst: e.dma_start(out=t_, in_=s_), waits=[gate_tok], inc="dG", dma=True))
    c.stgin_last = stg_in[-1]

    def mk_ag1(qq):
        if DBG["no_cc"]:
            return op("gpsimd", lambda e: e.engine_nop(), waits=[ccw[qq][-1]], inc="cc")
        return op("gpsimd", lambda e, qq=qq: e.collective_compute(
            "AllGather", ALU.bypass, ins=[d[f"cc1i{qq}"][:]], outs=[d[f"cc1o{qq}"][:]],
            replica_groups=[[0, 1], [2, 3], [4, 5], [6, 7]]), waits=[ccw[qq][-1]], inc="cc")
    ag1 = [mk_ag1(0), None, None, None]

    stg_store = {0: None, 1: None}
    for ic in range(32):
        slp = ic % 2
        s32 = sb["stg32"][:, slp * 1024:slp * 1024 + 1024]
        sbfv = sb["stgbf"][:, slp * 1024:slp * 1024 + 1024]
        tl = op("sync", lambda e, ic=ic, s32=s32: e.dma_start(out=s32, in_=d["wouth"][ic].rearrange("p ec q -> p (ec q)")), waits=[stg_store[slp], ag1[0]], inc="dW", dma=True)
        tv = op("vector", lambda e, ic=ic, s32=s32, sbfv=sbfv: e.tensor_scalar(sbfv, s32, sb["nrmw"][:, ic:ic + 1], None, ALU.mult, ALU.bypass), waits=[tl, kall], inc="sV")
        stg_store[slp] = op("sync", lambda e, ic=ic, sbfv=sbfv: e.dma_start(out=d["wout_bf"][:, :, ic, :], in_=sbfv.rearrange("p (ec q) -> p ec q", ec=8)), waits=[tv], inc="dW", dma=True)
    c.stgout_last = stg_store[1]

    if DBG["phases"] < 2:
        for qq in range(1, 4):
            ag1[qq] = mk_ag1(qq)
        fin(c, ag1[3])
        st1.close()
        return

    store_hist = {}
    prev_ln = p1_last_ln
    for tau in range(NT1):
        qq = tau // 2
        if tau >= 2 and tau % 2 == 0:
            ag1[qq] = mk_ag1(qq)
        base = (tau % 2) * T1
        slp = tau % 2
        bfn = ("xbf", "xbfB")[slp]
        outn = bfn
        w = [ag1[qq], store_hist.get(tau - 2) if tau >= 2 else p1_last_ln]
        tl = []
        for sslot in range(2):
            tl.append(op("sync", lambda e, qq=qq, base=base, sslot=sslot, bfn=bfn: e.dma_start(
                out=sb[bfn][:, sslot * 8 * T1:(sslot + 1) * 8 * T1].rearrange("p (c s) -> p c s", c=8),
                in_=d[f"cc1o{qq}"][sslot].rearrange("p (ch t) -> p ch t", ch=8)[:, :, base:base + T1]),
                waits=w, inc="dX", dma=True))
        ln = emit_ln(c, src=(bfn, BF16), Tt=T1, quant=False, waits=tl,
                     bufs=dict(bf=bfn, sq="sqb", t1="sqb", out=outn), prev_ln=prev_ln)
        prev_ln = ln["last"]
        store_hist[tau] = op("sync", lambda e, tau=tau, outn=outn: e.dma_start(
            out=d["xnorm_d"][:].rearrange("p (ch s) -> p ch s", ch=16)[:, :, tau * T1:(tau + 1) * T1],
            in_=sb[outn][:].rearrange("p (c s) -> p c s", c=16)), waits=[ln["xn"]], inc="dO", dma=True)
    t_p1b = store_hist[NT1 - 1]
    if "xn2" in c.dumps:
        op("sync", lambda e: e.dma_start(out=c.dumps["xn2"][:], in_=d["xnorm_d"][:]), waits=[t_p1b], inc="dX", dma=True)
    st1.close()
    if DBG["phases"] < 3:
        fin(c, t_p1b)
        return

    c.barrier()
    st2 = ExitStack()
    c.alloc(st2, "xn", [128, 16 * T2], BF16)
    c.alloc(st2, "wbuf", [128, 3 * 2048], BF16)
    c.alloc(st2, "convio", [128, 2 * 24 * (4 + T2)], BF16)
    c.alloc(st2, "carry", [128, 24 * 4], BF16)
    c.alloc(st2, "ctmp1", [128, 2048], BF16)
    c.alloc(st2, "ctmp2", [128, 2 * T2], BF16)
    c.alloc(st2, "sg", [128, 2 * 16 * T2], BF16)
    c.alloc(st2, "dtraw", [16, 2 * T2], F32)
    c.alloc(st2, "dtsp", [16, T2], F32)
    c.alloc(st2, "afm", [16, T2], F32)
    c.alloc(st2, "cumafm", [16, T2], F32)
    c.alloc(st2, "cuma_p0", [1, 2048], F32)
    c.alloc(st2, "tmb", [128, 64], F32)
    c.alloc(st2, "spc", [128, 48], F32)
    c.alloc(st2, "srun", [128, 16 * 128], F32)
    c.alloc(st2, "srun2", [128, 16 * 128], F32)
    c.alloc(st2, "sbf", [128, 16 * 128], BF16)
    c.alloc(st2, "xts", [128, 4 * 128], BF16)
    c.alloc(st2, "bts", [128, 128], BF16)
    c.alloc(st2, "b4", [128, 4 * 128], BF16)
    c.alloc(st2, "sc4", [128, 4 * 128], BF16)
    c.alloc(st2, "d4", [128, 4 * 128], F32)
    c.alloc(st2, "l4", [128, 4 * 128], BF16)
    c.alloc(st2, "sd4", [128, 4 * 128], BF16)
    c.alloc(st2, "c4", [128, 4 * 128], BF16)
    c.alloc(st2, "t1b", [128, 4 * 128], F32)
    c.alloc(st2, "h32c", [128, 16 * 128], BF16)
    c.alloc(st2, "h32n2", [128, 2 * 16 * 128], BF16)

    def mk_ag2(qq):
        return op("gpsimd", lambda e, qq=qq: e.collective_compute(
            "AllGather", ALU.bypass, ins=[d[f"cc2i{qq}"][:]], outs=[d[f"cc2o{qq}"][:]],
            replica_groups=[[0, 1], [2, 3], [4, 5], [6, 7]]), waits=[cw2[qq][-1]], inc="cc")
    t_done, cw2 = emit_sweep(c, t_p1b, setup_toks)
    ag2 = [mk_ag2(0), None, None, None]
    st2.close()
    if DBG["phases"] < 4:
        for qq in range(1, 4):
            ag2[qq] = mk_ag2(qq)
        fin(c, ag2[3])
        return

    c.barrier()
    st3 = ExitStack()
    c.alloc(st3, "rhs3", [128, 2 * 32 * T3], BF16)
    c.alloc(st3, "wo", [128, 2 * 32 * 128], BF16)
    c.alloc(st3, "x2me", [128, 2 * 8 * T3], BF16)
    c.alloc(st3, "x1r", [128, 2 * 8 * T3], BF16)

    last3 = {0: [], 1: [], 2: [], 3: []}
    tw_hist3 = {}
    ev_hist3 = {}
    wo_prev = {}
    for sl in range(NT3):
        qq = sl // 2
        if sl >= 2 and sl % 2 == 0:
            ag2[qq] = mk_ag2(qq)
        base = (sl % 2) * T3
        rbase = (sl % 2) * 32 * T3
        xbase = (sl % 2) * 8 * T3
        w = [ag2[qq], ev_hist3.get(sl - 2)]
        trh = []
        for sslot in range(2):
            trh.append(op("sync", lambda e, qq=qq, base=base, sslot=sslot, rbase=rbase: e.dma_start(
                out=sb["rhs3"][:, rbase + sslot * 16 * T3:rbase + (sslot + 1) * 16 * T3].rearrange("p (c s) -> p c s", c=16),
                in_=d[f"cc2o{qq}"][sslot].rearrange("p (ch t) -> p ch t", ch=16)[:, :, base:base + T3]),
                waits=w, inc="dX", dma=True))
        q13 = sl // 2
        b13 = (sl % 2) * T3
        tx1 = op("sync", lambda e, q13=q13, b13=b13, xbase=xbase: e.dma_start(
            out=sb["x1r"][:, xbase:xbase + 8 * T3].rearrange("p (c s) -> p c s", c=8),
            in_=d[f"cc1i{q13}"][:].rearrange("p (ch t) -> p ch t", ch=8)[:, :, b13:b13 + T3]),
            waits=w, inc="dX", dma=True)
        ev_toks = []
        for ec in range(8):
            wws = [ev_toks[ec - 2]] if ec >= 2 else [wo_prev.get(ec % 2)]
            wws = [x for x in wws if x is not None]
            two = op("scalar", lambda e, ec=ec: e.dma_start(
                out=sb["wo"][:, (ec % 2) * 4096:(ec % 2) * 4096 + 4096],
                in_=d["wout_bf"][:, ec].rearrange("p ic q -> p (ic q)")),
                waits=wws + [c.stgout_last], inc=("dA0", "dA1")[ec % 2], dma=True)
            pv = (ps["A"], ps["Bk"])[ec % 2][:, 0:512]
            mm = None
            for ic in range(32):
                mm = op("tensor", lambda e, ec=ec, ic=ic, pv=pv, rbase=rbase: e.matmul(
                    pv, sb["wo"][:, (ec % 2) * 4096 + ic * 128:(ec % 2) * 4096 + ic * 128 + 128],
                    sb["rhs3"][:, rbase + ic * T3:rbase + (ic + 1) * T3], start=(ic == 0), stop=(ic == 31)),
                    waits=(([two, tx1] + trh + wws) if ic == 0 else []),
                    inc=("sT" if ic == 31 else None))
            evw = [mm, tx1] + ([tw_hist3.get(sl - 2)] if (ec == 0 and (sl - 2) in tw_hist3) else [])
            ev = op("vector", lambda e, ec=ec, pv=pv, xbase=xbase: e.tensor_tensor(
                sb["x2me"][:, xbase + ec * T3:xbase + (ec + 1) * T3], pv,
                sb["x1r"][:, xbase + ec * T3:xbase + (ec + 1) * T3], ALU.add),
                waits=evw, inc="sV")
            ev_toks.append(ev)
            wo_prev[ec % 2] = ev
        tw = op("sync", lambda e, qq=qq, base=base, xbase=xbase: e.dma_start(
            out=d[f"cc3i{qq}"][:].rearrange("p (ch t) -> p ch t", ch=8)[:, :, base:base + T3],
            in_=sb["x2me"][:, xbase:xbase + 8 * T3].rearrange("p (c s) -> p c s", c=8)), waits=ev_toks, inc="dO", dma=True)
        last3[qq].append(tw)
        tw_hist3[sl] = tw
        ev_hist3[sl] = ev_toks[-1]
    def mk_ag3(qq):
        return op("gpsimd", lambda e, qq=qq: e.collective_compute(
            "AllGather", ALU.bypass, ins=[d[f"cc3i{qq}"][:]], outs=[d[f"cc3o{qq}"][:]],
            replica_groups=[[0, 1], [2, 3], [4, 5], [6, 7]]), waits=[last3[qq][-1]], inc="cc")
    ag3 = [mk_ag3(0), None, None, None]
    st3.close()
    if DBG["phases"] < 5:
        for qq in range(1, 4):
            ag3[qq] = mk_ag3(qq)
        fin(c, ag3[3])
        return

    c.barrier()
    st4 = ExitStack()
    for slp in ("", "B"):
        c.alloc(st4, "xbf4" + slp, [128, 16 * T1], BF16)
        c.alloc(st4, "xq4" + slp, [128, 16 * T1], FP8)
        c.alloc(st4, "x2r" + slp, [128, 8 * T1], BF16)
        c.alloc(st4, "outst" + slp, [128, 8 * T1], F32)
        c.alloc(st4, "xnb4" + slp, [128, 16 * T1], BF16)
    c.alloc(st4, "sqb4", [128, 16 * T1], BF16)
    c.alloc(st4, "wffn2", [128, 16 * 1024], FP8)
    c.alloc(st4, "bcsb4", [128, 1024], BF16)

    t_ffw2 = op("sync", lambda e: e.dma_start(out=sb["wffn2"][:], in_=d["wf2h"][:]), inc="dK", dma=True)
    lastout_hist = {}
    psu = {}
    prev_ln = None
    last_out = None
    for tau in range(NT1):
        qq = tau // 2
        if tau >= 2 and tau % 2 == 0:
            ag3[qq] = mk_ag3(qq)
        base = (tau % 2) * T1
        sx = ("", "B")[tau % 2]
        bfn, xqn, x2rn, outsn, xnn = "xbf4" + sx, "xq4" + sx, "x2r" + sx, "outst" + sx, "xnb4" + sx
        w = [ag3[qq], lastout_hist.get(tau - 2)]
        tl = []
        for sslot in range(2):
            tl.append(op("sync", lambda e, qq=qq, base=base, sslot=sslot, bfn=bfn: e.dma_start(
                out=sb[bfn][:, sslot * 8 * T1:(sslot + 1) * 8 * T1].rearrange("p (c s) -> p c s", c=8),
                in_=d[f"cc3o{qq}"][sslot].rearrange("p (ch t) -> p ch t", ch=8)[:, :, base:base + T1]),
                waits=w, inc="dX", dma=True))
        tx2 = op("sync", lambda e, qq=qq, base=base, x2rn=x2rn: e.dma_start(
            out=sb[x2rn][:].rearrange("p (c s) -> p c s", c=8),
            in_=d[f"cc3i{qq}"][:].rearrange("p (ch t) -> p ch t", ch=8)[:, :, base:base + T1]),
            waits=w, inc="dX", dma=True)
        ln = emit_ln(c, src=(bfn, BF16), Tt=T1, quant=True, waits=tl + [t_s4],
                     bufs=dict(bf=bfn, sq="sqb4", t1="sqb4", out=xnn, xq=xqn, bc="bcsb4"),
                     prev_ln=prev_ln)
        prev_ln = ln["last"]
        ev_toks = []
        for oc in range(8):
            pst = (ps["A"], ps["Bk"])[oc % 2]
            half2 = (oc // 2) % 2
            pv = pst[:, half2 * 512:half2 * 512 + 512]
            key = (oc % 2, half2)
            w0 = ([ev_toks[oc - 2]] if oc >= 2 else [psu.get(key)]) + [ln["xq"], t_ffw2]
            w0 = [x for x in w0 if x is not None]
            mm = None
            for ic in range(16):
                mm = op("tensor", lambda e, oc=oc, ic=ic, pv=pv, xqn=xqn: e.matmul(
                    pv, sb["wffn2"][:, ic * 1024 + oc * 128: ic * 1024 + oc * 128 + 128],
                    sb[xqn][:, ic * T1:(ic + 1) * T1], start=(ic == 0), stop=(ic == 15)),
                    waits=(w0 if ic == 0 else []), inc=("sT" if ic == 15 else None))
            evw = [mm, tx2] + ([lastout_hist.get(tau - 2)] if (oc == 0 and (tau - 2) in lastout_hist) else [])
            ev = op("vector", lambda e, oc=oc, pv=pv, outsn=outsn, x2rn=x2rn: e.scalar_tensor_tensor(
                sb[outsn][:, oc * T1:(oc + 1) * T1], pv, sb["swscol"][:, 0:1],
                sb[x2rn][:, oc * T1:(oc + 1) * T1], ALU.mult, ALU.add),
                waits=evw, inc="sV")
            ev_toks.append(ev)
            psu[key] = ev
        last_out = op("sync", lambda e, tau=tau, outsn=outsn: e.dma_start(
            out=d["outh"][:].rearrange("p (ch s) -> p ch s", ch=8)[:, :, tau * T1:(tau + 1) * T1],
            in_=sb[outsn][:].rearrange("p (c s) -> p c s", c=8)), waits=ev_toks, inc="dO", dma=True)
        lastout_hist[tau] = last_out
    st4.close()
    fin(c, last_out)


def fin(c, tok):
    c.op("gpsimd", lambda e: e.engine_nop(), waits=[tok], inc="sG")


def emit_ln(c, src, Tt, quant, waits, bufs, prev_ln=None, g1_eng="gpsimd"):
    op, sb, ps = c.op, c.sb, c.psum
    nm, dt = src
    bf, sq, t1n, outn = bufs["bf"], bufs["sq"], bufs["t1"], bufs["out"]
    bcn = bufs.get("bc", "bcsb")
    w0 = list(waits) + ([prev_ln] if prev_ln is not None else [])
    if dt == F32:
        tb = op("vector", lambda e: e.tensor_copy(sb[bf][:], sb[nm][:]), waits=w0, inc="sV")
        bfn = bf
    else:
        tb = op("vector", lambda e: e.engine_nop(), waits=w0, inc="sV")
        bfn = nm
    tq = op("vector", lambda e: e.tensor_tensor(sb[sq][:], sb[bfn][:], sb[bfn][:], ALU.mult), waits=[tb], inc="sV")
    m1 = m2 = None
    for ic in range(16):
        m1 = op("tensor", lambda e, ic=ic: e.matmul(ps["C"][0:1, 0:Tt], sb["ones_cb"][:], sb[bfn][:, ic * Tt:(ic + 1) * Tt], start=(ic == 0), stop=(ic == 15)), waits=([tb] if ic == 0 else []), inc=("sT" if ic == 15 else None))
    for ic in range(16):
        m2 = op("tensor", lambda e, ic=ic: e.matmul(ps["D"][0:1, 0:Tt], sb["ones_cb"][:], sb[sq][:, ic * Tt:(ic + 1) * Tt], start=(ic == 0), stop=(ic == 15)), waits=([tq] if ic == 0 else []), inc=("sT" if ic == 15 else None))
    r = sb["rows"]
    t1 = op("vector", lambda e: e.tensor_scalar(r[0:1, 0:Tt], ps["C"][0:1, 0:Tt], 1.0 / 2048, None, ALU.mult, ALU.bypass), waits=[m1], inc="sV")
    t2 = op("vector", lambda e: e.tensor_scalar(r[0:1, 512:512 + Tt], ps["D"][0:1, 0:Tt], 1.0 / 2048, None, ALU.mult, ALU.bypass), waits=[m2], inc="sV")
    t3 = op("vector", lambda e: e.tensor_tensor(r[0:1, 1536:1536 + Tt], r[0:1, 0:Tt], r[0:1, 0:Tt], ALU.mult), waits=[t1], inc="sV")
    t4 = op("vector", lambda e: e.tensor_tensor(r[0:1, 1024:1024 + Tt], r[0:1, 512:512 + Tt], r[0:1, 1536:1536 + Tt], ALU.subtract), waits=[t2, t3], inc="sV")
    t5 = op("scalar", lambda e: e.activation(r[0:1, 1536:1536 + Tt], r[0:1, 1024:1024 + Tt], AF.Ln, bias=sb["kcol"][0:1, 0:1]), waits=[t4, c.kcol_tok], inc="sS")
    t6 = op("scalar", lambda e: e.activation(r[0:1, 1024:1024 + Tt], r[0:1, 1536:1536 + Tt], AF.Exp, scale=-0.5), waits=[t5], inc="sS")
    if quant:
        t7 = op("vector", lambda e: e.tensor_scalar(sb["rowbf"][0:1, 0:Tt], r[0:1, 1024:1024 + Tt], sb["sc_f"][0:1, 3:4], None, ALU.mult, ALU.bypass), waits=[t6], inc="sV")
    else:
        t7 = op("vector", lambda e: e.tensor_copy(sb["rowbf"][0:1, 0:Tt], r[0:1, 1024:1024 + Tt]), waits=[t6], inc="sV")
    t8 = op("vector", lambda e: e.tensor_tensor(sb["rowbf"][0:1, 512:512 + Tt], r[0:1, 0:Tt], sb["rowbf"][0:1, 0:Tt], ALU.mult), waits=[t7], inc="sV")
    b1 = op("tensor", lambda e: e.matmul(ps["C"][:, 0:512], sb["ones_rb"][:], sb["rowbf"][0:1, 0:512], start=True, stop=True), waits=[t8], inc="sT")
    b2 = op("tensor", lambda e: e.matmul(ps["D"][:, 0:512], sb["ones_rb"][:], sb["rowbf"][0:1, 512:1024], start=True, stop=True), waits=[t8], inc="sT")
    b3 = op("vector", lambda e: e.tensor_copy(sb[bcn][:, 0:512], ps["C"][:, 0:512]), waits=[b1], inc="sV")
    b3b = op("vector", lambda e: e.tensor_copy(sb[bcn][:, 512:1024], ps["D"][:, 0:512]), waits=[b2], inc="sV")
    rep_rs = sb[bcn][:, 0:Tt].unsqueeze(1).broadcast_to([128, 16, Tt])
    rep_mu = sb[bcn][:, 512:512 + Tt].unsqueeze(1).broadcast_to([128, 16, Tt])
    g1 = op(g1_eng, lambda e: e.tensor_tensor(sb[t1n][:].rearrange("p (c t) -> p c t", c=16), sb[bfn][:].rearrange("p (c t) -> p c t", c=16), rep_rs, ALU.mult), waits=[b3, b3b, tq], inc=("sG" if g1_eng == "gpsimd" else "sV"))
    v1 = op("vector", lambda e: e.tensor_tensor(sb[outn][:].rearrange("p (c t) -> p c t", c=16), sb[t1n][:].rearrange("p (c t) -> p c t", c=16), rep_mu, ALU.subtract), waits=[g1], inc="sV")
    res = {"xn": v1, "last": v1}
    if quant:
        res["xq"] = op("vector", lambda e: e.tensor_scalar(sb[bufs["xq"]][:], sb[outn][:], 448.0, -448.0, ALU.min, ALU.max), waits=[v1], inc="sV")
        res["last"] = res["xq"]
    return res


def emit_sweep(c, t_xn_ready, setup_toks):
    op, sb, ps, d = c.op, c.sb, c.psum, c.dram
    cw = {0: [], 1: [], 2: [], 3: []}
    t_sruninit = op("vector", lambda e: e.memset(sb["srun"][:], 0.0), inc="sV")
    t_sbfinit = op("vector", lambda e: e.memset(sb["sbf"][:], 0.0), inc="sV")
    t_carinit = op("vector", lambda e: e.memset(sb["carry"][:], 0.0), inc="sV")
    st = {"srun": "srun", "srun2": "srun2", "last_chain": t_sruninit, "last_sbf": t_sbfinit,
          "last_y": None, "h32toks": [], "chunkpar": 0, "prev_ex2": None, "last_gn": None,
          "last_carry": t_carinit, "h32dma": None, "yprev": {}, "last_vh": None,
          "last_vsq": None, "convio_free": {}}
    psA_user = {}
    prev_w = {}
    info = {}
    GRPS = [list(range(0, 11)), list(range(11, 21)), list(range(21, 31)), list(range(31, 41))]
    NTT = DBG["sweep_tiles"] or NT2

    def emit_inproj_grp(tt, ocs):
        par = tt % 2
        cvb = par * 24 * (4 + T2)
        tb0 = par * T2
        sgb = par * 16 * T2
        inf = info[tt]
        free_tok = st["convio_free"].get(par)
        for oc in ocs:
            bslot = oc % 3
            twl = op("gpsimd", lambda e, oc=oc, bslot=bslot: e.dma_start(
                out=sb["wbuf"][:, bslot * 2048:bslot * 2048 + 2048],
                in_=d["win_bf"][oc].rearrange("p ic q -> p (ic q)")),
                waits=[prev_w.get(bslot), c.stgin_last], inc=("dA0", "dA1")[oc % 2], dma=True)
            sub = oc % 2
            pv = ps["A"][:, sub * 512:sub * 512 + 512]
            mm = None
            for ic in range(16):
                mm = op("tensor", lambda e, oc=oc, ic=ic, pv=pv, bslot=bslot: e.matmul(
                    pv, sb["wbuf"][:, bslot * 2048 + ic * 128:bslot * 2048 + ic * 128 + 128],
                    sb["xn"][:, ic * T2:ic * T2 + T2], start=(ic == 0), stop=(ic == 15)),
                    waits=([twl, inf["txn"], psA_user.get(sub)] if ic == 0 else []),
                    inc=("sT" if ic == 15 else None))
            prev_w[bslot] = mm
            if oc < 24:
                gv = op("vector", lambda e, oc=oc, pv=pv, cvb=cvb: e.tensor_copy(
                    sb["convio"][:, cvb + oc * (4 + T2) + 4: cvb + oc * (4 + T2) + 4 + T2], pv),
                    waits=[mm, free_tok], inc="sV")
            elif oc == 24:
                gv = op("vector", lambda e, pv=pv, tb0=tb0: e.tensor_copy(
                    sb["dtraw"][:, tb0:tb0 + T2], pv[0:16, :]), waits=[mm, free_tok], inc="sV")
            else:
                lh = oc - 25
                gv = op("vector", lambda e, lh=lh, pv=pv, sgb=sgb: e.tensor_copy(
                    sb["sg"][:, sgb + lh * T2:sgb + (lh + 1) * T2], pv), waits=[mm, free_tok], inc="sV")
            psA_user[sub] = gv
            inf["gv"].append(gv)

    def emit_post(tt):
        par = tt % 2
        cvb = par * 24 * (4 + T2)
        tb0 = par * T2
        inf = info[tt]
        gv24 = inf["gv"][24]
        s1 = op("scalar", lambda e, tb0=tb0: e.activation(sb["afm"][:, 0:T2], sb["dtraw"][:, tb0:tb0 + T2], AF.Relu, bias=sb["dtb"][:, 0:1]), waits=[gv24] + setup_toks, inc="sS")
        s2 = op("scalar", lambda e, tb0=tb0: e.activation(sb["dtsp"][:, 0:T2], sb["dtraw"][:, tb0:tb0 + T2], AF.Abs, bias=sb["dtb"][:, 0:1]), waits=[gv24], inc="sS")
        s3 = op("scalar", lambda e: e.activation(sb["dtsp"][:, 0:T2], sb["dtsp"][:, 0:T2], AF.Exp, scale=-1.0), waits=[s2], inc="sS")
        s4 = op("scalar", lambda e: e.activation(sb["dtsp"][:, 0:T2], sb["dtsp"][:, 0:T2], AF.Ln, bias=sb["kcol"][0:16, 1:2]), waits=[s3], inc="sS")
        v5 = op("vector", lambda e: e.tensor_tensor(sb["dtsp"][:, 0:T2], sb["afm"][:, 0:T2], sb["dtsp"][:, 0:T2], ALU.add), waits=[s4, s1], inc="sV")
        v5 = op("vector", lambda e: e.tensor_scalar(sb["dtsp"][:, 0:T2], sb["dtsp"][:, 0:T2], 1e-19, None, ALU.max, ALU.bypass), waits=[v5], inc="sV")
        v6 = op("vector", lambda e: e.tensor_scalar(sb["afm"][:, 0:T2], sb["dtsp"][:, 0:T2], sb["aneg"][:, 0:1], None, ALU.mult, ALU.bypass), waits=[v5], inc="sV")
        inf["tdt"] = v6
        cvv = sb["convio"][:, cvb:cvb + 24 * (4 + T2)].rearrange("p (m t) -> p m t", m=24)
        t_car = op("vector", lambda e, cvv=cvv: e.tensor_copy(cvv[:, :, 0:4], sb["carry"][:].rearrange("p (m t) -> p m t", m=24)),
                   waits=[st["last_carry"]] + [inf["gv"][m] for m in range(24)] + setup_toks, inc="sV")
        silu_toks = []
        vc_toks = []
        for m in range(24):
            cpar = m % 2
            acc1 = sb["ctmp1"][:, cpar * T2:(cpar + 1) * T2]
            acc2 = sb["ctmp2"][:, cpar * T2:(cpar + 1) * T2]

            def um(k, m=m, cvb=cvb):
                base = cvb + m * (4 + T2) + 1 + k
                return sb["convio"][:, base:base + T2]

            def wc(k, m=m):
                return sb["convw_b"][:, m * 4 + k:m * 4 + k + 1]

            wv = [t_car] + ([silu_toks[m - 2]] if m >= 2 else [st["last_vsq"]])
            a1 = op("vector", lambda e, m=m, acc1=acc1: e.tensor_scalar(acc1, um(0, m), wc(0, m), None, ALU.mult, ALU.bypass), waits=wv, inc="sV")
            a2 = op("vector", lambda e, m=m, acc1=acc1, acc2=acc2: e.scalar_tensor_tensor(acc2, um(1, m), wc(1, m), acc1, ALU.mult, ALU.add), waits=[a1], inc="sV")
            a3 = op("vector", lambda e, m=m, acc1=acc1, acc2=acc2: e.scalar_tensor_tensor(acc1, um(2, m), wc(2, m), acc2, ALU.mult, ALU.add), waits=[a2], inc="sV")
            a4 = op("vector", lambda e, m=m, acc1=acc1, acc2=acc2: e.scalar_tensor_tensor(acc2, um(3, m), wc(3, m), acc1, ALU.mult, ALU.add), waits=[a3], inc="sV")
            vc = op("vector", lambda e, m=m, cvb=cvb: e.tensor_copy(
                sb["carry"][:, m * 4:m * 4 + 4],
                sb["convio"][:, cvb + m * (4 + T2) + T2: cvb + m * (4 + T2) + T2 + 4]), waits=[a4], inc="sV")
            vc_toks.append(vc)
            silu_toks.append(op("scalar", lambda e, m=m, acc2=acc2, cvb=cvb: e.activation(
                sb["convio"][:, cvb + m * (4 + T2) + 4: cvb + m * (4 + T2) + 4 + T2],
                acc2, AF.Silu, bias=sb["convb_f"][:, m:m + 1]),
                waits=[vc], inc="sS"))
        st["last_carry"] = vc_toks[-1]
        inf["silu"] = silu_toks

    for tt in range(NTT):
        info[tt] = {"gv": []}
        txn = op("sync", lambda e, tt=tt: e.dma_start(
            out=sb["xn"][:].rearrange("p (c s) -> p c s", c=16),
            in_=d["xnorm_d"][:].rearrange("p (ch s) -> p ch s", ch=16)[:, :, tt * T2:(tt + 1) * T2]),
            waits=[t_xn_ready, (info[tt - 1]["gv"][-1] if tt >= 1 else None)], inc="dX", dma=True)
        info[tt]["txn"] = txn
        for grp in range(4):
            emit_inproj_grp(tt, GRPS[grp])
            if tt >= 1:
                ssd_chunk(c, tt - 1, grp, info[tt - 1]["silu"], info[tt - 1]["tdt"], info[tt - 1]["gv"][-1], st, cw)
        if tt >= 1:
            st["convio_free"][(tt - 1) % 2] = st["last_vh"]
        emit_post(tt)
    for grp in range(4):
        ssd_chunk(c, NTT - 1, grp, info[NTT - 1]["silu"], info[NTT - 1]["tdt"], info[NTT - 1]["gv"][-1], st, cw)
    return st["last_y"], cw


def ssd_chunk(c, u, cc_, silu_toks, tdt, gate_tok, st, cw):
    op, sb, ps, d = c.op, c.sb, c.psum, c.dram
    gchunk = u * NCH + cc_
    t0 = cc_ * 128
    par = u % 2
    cvb = par * 24 * (4 + T2)
    sgb = par * 16 * T2
    cv = sb["convio"]

    def xfm(lh):
        return cv[:, cvb + lh * (4 + T2) + 4 + t0: cvb + lh * (4 + T2) + 4 + t0 + 128]

    def bfm(j):
        return cv[:, cvb + (16 + j) * (4 + T2) + 4 + t0: cvb + (16 + j) * (4 + T2) + 4 + t0 + 128]

    def cfm(j):
        return cv[:, cvb + (20 + j) * (4 + T2) + 4 + t0: cvb + (20 + j) * (4 + T2) + 4 + t0 + 128]

    t_sil = op("scalar", lambda e: e.activation(
        sb["h32c"][:].rearrange("p (lh t) -> p lh t", lh=16),
        sb["sg"][:, sgb:sgb + 16 * T2].rearrange("p (lh t) -> p lh t", lh=16)[:, :, t0:t0 + 128],
        AF.Silu), waits=[gate_tok, st["last_vsq"], st["last_y"]], inc="sS")

    w0 = [tdt, st["last_chain"], st["last_gn"]]
    ta = op("tensor", lambda e, t0=t0: e.transpose(ps["D"][:, 0:16], sb["afm"][:, t0:t0 + 128], sb["identf"][0:16, 0:16]), waits=w0, inc="sT")
    tb = op("tensor", lambda e, t0=t0: e.transpose(ps["D"][:, 16:32], sb["dtsp"][:, t0:t0 + 128], sb["identf"][0:16, 0:16]), waits=[], inc="sT")
    tv = op("vector", lambda e: e.tensor_copy(sb["tmb"][:, 0:32], ps["D"][:, 0:32]), waits=[ta, tb], inc="sV")
    tc = op("tensor", lambda e: e.matmul(ps["Ek"][:, 0:16], sb["tu"][:], sb["tmb"][:, 0:16], start=True, stop=True), waits=[tv, st["last_gn"]], inc="sT")
    tv2 = op("vector", lambda e: e.tensor_copy(sb["tmb"][:, 32:48], ps["Ek"][:, 0:16]), waits=[tc], inc="sV")
    td = op("tensor", lambda e: e.transpose(ps["D"][0:16, 32:160], sb["tmb"][:, 32:48], sb["identf"][:]), waits=[tv2], inc="sT")
    tv3 = op("vector", lambda e, t0=t0: e.tensor_copy(sb["cumafm"][:, t0:t0 + 128], ps["D"][0:16, 32:160]), waits=[td], inc="sV")
    tp0 = op("sync", lambda e, t0=t0: e.dma_start(out=sb["cuma_p0"][0:1, :].rearrange("o (h t) -> o h t", h=16), in_=sb["cumafm"][:, t0:t0 + 128]), waits=[tv3], inc="dP", dma=True)
    s1 = op("scalar", lambda e: e.activation(sb["spc"][:, 0:16], sb["tmb"][:, 16:32], AF.Ln), waits=[tv, st["last_gn"]], inc="sS")
    v4 = op("vector", lambda e: e.tensor_tensor(sb["spc"][:, 0:16], sb["spc"][:, 0:16], sb["tmb"][:, 32:48], ALU.subtract), waits=[s1, tv2], inc="sV")
    lam_rhs = sb["cuma_p0"][0:1, :].rearrange("o (h t) -> o h t", h=16)[:, :, 127:128].rearrange("o h t -> o (h t)")
    tl = op("tensor", lambda e: e.matmul(ps["Ek"][:, 16:32], sb["ones_rf"][:], lam_rhs, start=True, stop=True), waits=[tp0, tv2], inc="sT")
    v5 = op("scalar", lambda e: e.activation(sb["spc"][:, 16:32], ps["Ek"][:, 16:32], AF.Exp), waits=[tl], inc="sS")
    v6 = op("vector", lambda e: e.tensor_tensor(sb["spc"][:, 32:48], ps["Ek"][:, 16:32], sb["tmb"][:, 32:48], ALU.subtract), waits=[tl], inc="sV")
    s3 = op("scalar", lambda e: e.activation(sb["spc"][:, 32:48], sb["spc"][:, 32:48], AF.Exp), waits=[v6], inc="sS")
    v7 = op("vector", lambda e: e.tensor_tensor(sb["spc"][:, 32:48], sb["spc"][:, 32:48], sb["tmb"][:, 16:32], ALU.mult), waits=[s3], inc="sV")

    h32_toks = []
    for j in range(4):
        wsc = (silu_toks if gchunk % NCH == 0 else []) if j == 0 else []
        tsc = op("tensor", lambda e, j=j: e.matmul(ps["C"][:, j * 128:j * 128 + 128], bfm(j), cfm(j), start=True, stop=True),
                 waits=wsc + ([st["last_y"]] if (j == 0 and st["last_y"]) else []), inc="sT")
        pd = (ps["D"], ps["Ek"])[j % 2]
        wdb = [tp0] + ([tv3] if j % 2 == 0 else [v6, v7])
        tdb = op("tensor", lambda e, j=j, pd=pd: e.matmul(pd[:], sb["ones_rf"][:], sb["cuma_p0"][0:1, 4 * j * 128:(4 * j + 4) * 128], start=True, stop=True), waits=wdb, inc="sT")
        ttx = []
        wtx = ([silu_toks[-1]] if (j == 0 and gchunk % NCH == 0) else []) + ([st["prev_ex2"]] if st["prev_ex2"] else [])
        for q in range(4):
            lh = 4 * q + j
            ttx.append(op("tensor", lambda e, q=q, lh=lh: e.transpose(ps["T"][:, q * 128:q * 128 + 128], xfm(lh), sb["ident"][:]), waits=(wtx if q == 0 else []), inc="sT"))
        ttb = op("tensor", lambda e, j=j: e.transpose(ps["T"][:, 512:640], bfm(j), sb["ident"][:]), waits=[], inc="sT")
        ex1 = op("vector", lambda e: e.tensor_copy(sb["xts"][:], ps["T"][:, 0:512]), waits=ttx + [ttb], inc="sV")
        ex2 = op("vector", lambda e: e.tensor_copy(sb["bts"][:], ps["T"][:, 512:640]), waits=[ex1], inc="sV")
        st["prev_ex2"] = ex2
        vb = []
        for q in range(4):
            p_ = 4 * j + q
            vb.append(op("vector", lambda e, q=q, p_=p_: e.tensor_scalar(sb["b4"][:, q * 128:q * 128 + 128], sb["bts"][:], sb["spc"][:, 32 + p_:32 + p_ + 1], None, ALU.mult, ALU.bypass), waits=([ex2, v7] if q == 0 else []), inc="sV"))
        vds = []
        for q in range(4):
            p_ = 4 * j + q
            vds.append(op("vector", lambda e, q=q, p_=p_, pd=pd: e.scalar_tensor_tensor(
                sb["d4"][:, q * 128:q * 128 + 128], pd[:, q * 128:q * 128 + 128], sb["spc"][:, p_:p_ + 1],
                sb["maskneg"][:], ALU.add, ALU.add), waits=([tdb, v4] if q == 0 else []), inc="sV"))
        vd = vds[-1]
        sl4 = op("scalar", lambda e: e.activation(sb["l4"][:], sb["d4"][:], AF.Exp), waits=[vd], inc="sS")
        ssd4 = op("scalar", lambda e, pd=pd: e.activation(sb["sd4"][:], pd[:], AF.Exp), waits=[vd], inc="sS")
        screp = ps["C"][:, j * 128:j * 128 + 128].unsqueeze(1).broadcast_to([128, 4, 128])
        vm4 = op("vector", lambda e, screp=screp: e.tensor_tensor(sb["sc4"][:].rearrange("p (r t) -> p r t", r=4), screp, sb["l4"][:].rearrange("p (r t) -> p r t", r=4), ALU.mult), waits=[sl4, tsc], inc="sV")
        crep = cfm(j).unsqueeze(1).broadcast_to([128, 4, 128])
        vc4 = op("vector", lambda e, j=j, crep=crep: e.tensor_tensor(sb["c4"][:].rearrange("p (r t) -> p r t", r=4), crep, sb["sd4"][:].rearrange("p (r t) -> p r t", r=4), ALU.mult), waits=[ssd4], inc="sV")
        yv = ps["Bk"][:, (j % 2) * 512:(j % 2) * 512 + 512]
        t_y = None
        for q in range(4):
            p_ = 4 * j + q
            w_ = ([vm4, vc4, st["last_sbf"], st["yprev"].get(j % 2)] if q == 0 else [])
            w_ = [x for x in w_ if x is not None]
            op("tensor", lambda e, q=q, yv=yv: e.matmul(yv[:, q * 128:q * 128 + 128], sb["xts"][:, q * 128:q * 128 + 128], sb["sc4"][:, q * 128:q * 128 + 128], start=True, stop=False), waits=w_, inc=None)
            t_y = op("tensor", lambda e, q=q, p_=p_, yv=yv: e.matmul(yv[:, q * 128:q * 128 + 128], sb["sbf"][:, p_ * 128:p_ * 128 + 128], sb["c4"][:, q * 128:q * 128 + 128], start=False, stop=True), waits=[], inc=("sT" if q == 3 else None))
        stp = (ps["D"], ps["Ek"])[j % 2]
        smm = None
        for q in range(4):
            smm = op("tensor", lambda e, q=q, stp=stp: e.matmul(stp[:, q * 128:q * 128 + 128], sb["b4"][:, q * 128:q * 128 + 128], sb["xts"][:, q * 128:q * 128 + 128], start=True, stop=True), waits=(([vd, ssd4, vm4] + vb) if q == 0 else []), inc=("sT" if q == 3 else None))
        vt1 = []
        for q in range(4):
            p_ = 4 * j + q
            lh = 4 * q + j
            vt1.append(op("vector", lambda e, q=q, p_=p_, lh=lh, yv=yv: e.scalar_tensor_tensor(
                sb["t1b"][:, q * 128:q * 128 + 128], xfm(lh), sb["dcols"][:, p_:p_ + 1],
                yv[:, q * 128:q * 128 + 128], ALU.mult, ALU.add), waits=([t_y] if q == 0 else []), inc="sV"))
        st["yprev"][j % 2] = vt1[-1]
        h32_ap = sb["h32c"][:].rearrange("p (lh t) -> p lh t", lh=16)[:, j::4, :]
        vh = op("vector", lambda e, h32_ap=h32_ap: e.tensor_tensor(h32_ap, sb["t1b"][:].rearrange("p (r t) -> p r t", r=4), h32_ap, ALU.mult),
                waits=vt1 + [t_sil], inc="sV")
        h32_toks.append(vh)
        vch = []
        srn, srn2 = st["srun"], st["srun2"]
        for q in range(4):
            p_ = 4 * j + q
            vch.append(op("vector", lambda e, q=q, p_=p_, stp=stp, srn=srn, srn2=srn2: e.scalar_tensor_tensor(
                sb[srn2][:, p_ * 128:p_ * 128 + 128], sb[srn][:, p_ * 128:p_ * 128 + 128],
                sb["spc"][:, 16 + p_:16 + p_ + 1], stp[:, q * 128:q * 128 + 128], ALU.mult, ALU.add),
                waits=([smm, v5, st["last_chain"]] if q == 0 else []), inc="sV"))
        vsb = op("vector", lambda e, j=j, srn2=srn2: e.tensor_copy(sb["sbf"][:, 4 * j * 128:(4 * j + 4) * 128], sb[srn2][:, 4 * j * 128:(4 * j + 4) * 128]), waits=[vch[-1], t_y], inc="sV")
        if j == 3:
            st["last_chain"] = vch[-1]
            st["last_sbf"] = vsb
            st["srun"], st["srun2"] = st["srun2"], st["srun"]
    vsq = op("vector", lambda e: e.tensor_tensor(sb["ctmp1"][:], sb["h32c"][:], sb["h32c"][:], ALU.mult), waits=h32_toks + [silu_toks[-1]], inc="sV")
    st["last_vsq"] = vsq
    st["last_vh"] = h32_toks[-1]
    gmm = None
    for lh in range(16):
        gg = lh // 4
        gmm = op("tensor", lambda e, lh=lh, gg=gg: e.matmul(ps["Ek"][0:1, gg * 128:gg * 128 + 128], sb["ones_cb"][:], sb["ctmp1"][:, lh * 128:lh * 128 + 128], start=(lh % 4 == 0), stop=(lh % 4 == 3)), waits=([vsq, st["last_chain"]] if lh == 0 else []), inc=("sT" if lh == 15 else None))
    sgn1 = op("scalar", lambda e: e.activation(sb["rows"][0:1, 0:512], ps["Ek"][0:1, 0:512], AF.Ln, bias=sb["kcol"][0:1, 2:3], scale=1.0 / 512), waits=[gmm], inc="sS")
    sgn2 = op("scalar", lambda e: e.activation(sb["rowbf"][0:1, 0:512], sb["rows"][0:1, 0:512], AF.Exp, scale=-0.5), waits=[sgn1], inc="sS")
    st["last_gn"] = sgn2
    bcg = op("tensor", lambda e: e.matmul(ps["C"][:, 0:512], sb["ones_rb"][:], sb["rowbf"][0:1, 0:512], start=True, stop=True), waits=[sgn2], inc="sT")
    slot = st["chunkpar"]
    gap = []
    for gg in range(4):
        rep = ps["C"][:, gg * 128:gg * 128 + 128].unsqueeze(1).broadcast_to([128, 4, 128])
        gap.append(op("vector", lambda e, gg=gg, slot=slot, rep=rep: e.tensor_tensor(
            sb["h32n2"][:].rearrange("p (lh u) -> p lh u", lh=16)[:, gg * 4:(gg + 1) * 4, slot * 128:slot * 128 + 128],
            sb["h32c"][:, gg * 512:(gg + 1) * 512].rearrange("p (r t) -> p r t", r=4), rep, ALU.mult),
            waits=([bcg, st["h32dma"]] if gg == 0 else []), inc="sV"))
    st["h32toks"].append(gap[-1])
    st["last_y"] = gap[-1]
    if gchunk % 2 == 1:
        qq = gchunk // 8
        basech = (gchunk % 8) - 1
        wv = st["h32toks"][-2:]
        cw[qq].append(op("sync", lambda e, qq=qq, basech=basech: e.dma_start(
            out=d[f"cc2i{qq}"][:].rearrange("p (lh t) -> p lh t", lh=16)[:, :, basech * 128:(basech + 2) * 128],
            in_=sb["h32n2"][:].rearrange("p (lh u) -> p lh u", lh=16)),
            waits=wv, inc="dO", dma=True))
        st["h32dma"] = cw[qq][-1]
    st["chunkpar"] ^= 1



def _jax_cache():
    try:
        import jax
        jax.config.update("jax_compilation_cache_dir", "/tmp/jaxcache_kernel")
        jax.config.update("jax_persistent_cache_min_compile_time_secs", 1.0)
        jax.config.update("jax_persistent_cache_min_entry_size_bytes", 0)
    except Exception:
        pass


def make_inputs(core, inp):
    b, lr = core // 2, core % 2
    f32 = np.float32
    x = np.asarray(inp["L0_attn_x"][b], f32)
    wip = np.asarray(inp["L0_attn_in_proj_w"], f32)
    wop = np.asarray(inp["L0_attn_out_proj_w"], f32)
    w8 = np.asarray(inp["L0_ffn_weight"]).view(ml_dtypes.float8_e4m3fn)
    GH = _heads(lr)
    GB = [4 * lr + j for j in range(4)]
    my_e = np.arange(lr * 1024, lr * 1024 + 1024)
    loc_e = np.concatenate([my_e, np.arange((1 - lr) * 1024, (1 - lr) * 1024 + 1024)])

    m = {}
    m["xh"] = np.ascontiguousarray(x[:, loc_e].T.reshape(16, 128, S).transpose(1, 0, 2).reshape(128, 16 * S))
    hs_rows = np.concatenate([np.arange(I + g * 128, I + g * 128 + 128) for g in GH])
    b_rows = np.concatenate([np.arange(2 * I + g * 128, 2 * I + g * 128 + 128) for g in GB])
    c_rows = np.concatenate([np.arange(2 * I + G * N + g * 128, 2 * I + G * N + g * 128 + 128) for g in GB])
    gate_rows = np.concatenate([np.arange(g * 128, g * 128 + 128) for g in GH])
    Wsel = np.zeros((NOC * 128, E), f32)
    Wsel[0:2048] = wip[hs_rows]
    Wsel[2048:2560] = wip[b_rows]
    Wsel[2560:3072] = wip[c_rows]
    for p in range(16):
        Wsel[3072 + p] = wip[I + CD + GH[_lh_of_part(p)]]
    Wsel[3200:5248] = wip[gate_rows]
    m["winh"] = np.ascontiguousarray(Wsel.reshape(NOC, 128, 16, 128).transpose(0, 2, 3, 1))
    perm_i = np.concatenate([np.arange(g * 128, g * 128 + 128) for g in (_heads(0) + _heads(1))])
    wo = wop[my_e][:, perm_i]
    m["wouth"] = np.ascontiguousarray(wo.T.reshape(32, 128, 8, 128))
    wf1 = w8[my_e][:, loc_e]
    wf2 = w8[my_e, :]

    def ffn_tiles(wf):
        t = np.ascontiguousarray(wf.T).reshape(16, 128, 8, 128)
        return np.ascontiguousarray(t.transpose(1, 0, 2, 3).reshape(128, 16 * 1024))
    m["wf1h"] = ffn_tiles(wf1)
    m["wf2h"] = ffn_tiles(wf2)
    cwt = np.asarray(inp["L0_attn_conv_w"], f32)
    cbt = np.asarray(inp["L0_attn_conv_b"], f32)
    ch_sel = np.concatenate([np.concatenate([np.arange(g * 128, g * 128 + 128) for g in GH]),
                             np.concatenate([np.arange(I + g * 128, I + g * 128 + 128) for g in GB]),
                             np.concatenate([np.arange(I + G * N + g * 128, I + G * N + g * 128 + 128) for g in GB])])
    m["convw"] = np.ascontiguousarray(cwt[ch_sel, 0, :].reshape(24, 128, 4).transpose(1, 0, 2).reshape(128, 96))
    m["convbh"] = np.ascontiguousarray(cbt[ch_sel].reshape(24, 128).T)
    perm16 = [GH[_lh_of_part(p)] for p in range(16)]
    m["dtb_h"] = np.asarray(inp["L0_attn_dt_bias"], f32)[perm16].reshape(16, 1)
    m["alog_h"] = np.asarray(inp["L0_attn_A_log"], f32)[perm16].reshape(16, 1)
    m["d_h"] = np.asarray(inp["L0_attn_D"], f32)[perm16].reshape(1, 16)
    m["nrmw_h"] = np.ascontiguousarray(np.asarray(inp["L0_attn_norm_w"], f32)[perm_i].reshape(32, 128).T)
    m["isc_h"] = np.asarray(inp["L0_ffn_input_scale"]).reshape(1, 1)
    m["wsc_h"] = np.asarray(inp["L0_ffn_weight_scale"], f32).reshape(1, 1)
    m["identh"] = np.eye(128, dtype=ml_dtypes.bfloat16)
    m["identf_h"] = np.eye(128, dtype=f32)
    tuv = np.triu(np.ones((128, 128), f32))
    m["tuh"] = tuv
    m["maskh"] = np.where(tuv > 0, 0.0, NEG).astype(f32)
    m["ones_r_f"] = np.ones((1, 128), f32)
    m["ones_r_b"] = np.ones((1, 128), ml_dtypes.bfloat16)
    m["ones_c_b"] = np.ones((128, 1), ml_dtypes.bfloat16)
    return m


def kernel(**inputs):
    _jax_cache()
    nc = build_program()
    in_maps = [make_inputs(core, inputs) for core in range(8)]
    res = run_bass_kernel_spmd(nc, in_maps, core_ids=list(range(8)))
    kernel._last = res
    out = np.zeros((BATCH, S, E), np.float32)
    for core in range(8):
        b, lr = core // 2, core % 2
        o = res.results[core]["outh"].reshape(128, 8, S)
        out[b, :, lr * 1024:(lr + 1) * 1024] = o.transpose(2, 1, 0).reshape(S, 1024)
    return out
